# version 01
# baseline (speedup 1.0000x reference)
import math
import numpy as np
import concourse.bass as bass
import concourse.mybir as mybir
from concourse.bass import IndirectOffsetOnAxis
from concourse.bass_utils import run_bass_kernel_spmd

F32 = mybir.dt.float32
BF16 = mybir.dt.bfloat16
I32 = mybir.dt.int32
I16 = mybir.dt.int16
AF = mybir.ActivationFunctionType
OP = mybir.AluOpType
AX = mybir.AxisListType


def full_cfg():
    return dict(
        D=2048, HD=128, NH=16, S=2048, B=16,
        E=16, TOPK=4, CAP=1024, I=1408, ISH=2816,
        NCORES=8, NB=2,
        TH=5120,
        QT=512,
        KG=1024,
        NT=512,
        HOT={0: dict(t0=0, cpad=1024), 1: dict(t0=4096, cpad=256)},
        LN_EPS=1e-5,
    )


def mini_cfg():
    return dict(
        D=256, HD=128, NH=2, S=256, B=4,
        E=16, TOPK=4, CAP=24, I=256, ISH=384,
        NCORES=2, NB=2,
        TH=384,
        QT=128, KG=256, NT=128,
        HOT={0: dict(t0=0, cpad=32), 1: dict(t0=512, cpad=16)},
        LN_EPS=1e-5,
    )



def build_constants(cfg):
    QT = cfg["QT"]
    c = {}
    c["identity"] = np.eye(128, dtype=np.float32)
    rs = list(range(-(cfg["S"] - QT) // 128, QT // 128))
    ki = np.arange(128)[:, None]
    qj = np.arange(QT)[None, :]
    bt = []
    for r in rs:
        v = (ki - qj + 128 * r).astype(np.float32)
        v = np.where(v <= 0, v, -1e30).astype(np.float32)
        bt.append(v)
    c["btiles"] = np.stack(bt, 0)
    c["brs"] = rs
    c["ltri"] = np.triu(np.ones((128, 128), np.float32), 1)
    c["ones128"] = np.ones((128, 1), np.float32)
    c["iota16"] = np.arange(cfg["E"], dtype=np.float32)[None, :]
    H = cfg["NH"]
    start = 2.0 ** (-8.0 / H)
    c["slopes"] = np.array([start * start ** i for i in range(H)], np.float32)
    return c



class Sems:

    def __init__(self, nc, ctx):
        self.nc = nc
        self.ctx = ctx
        self.val = {}
        self.h = {}

    def get(self, name):
        if name not in self.h:
            self.h[name] = self.ctx.enter_context(self.nc.semaphore(name))
            self.val[name] = 0
        return self.h[name]

    def inc(self, name, by):
        self.get(name)
        self.val[name] += by
        return self.h[name], self.val[name]


def ceil_div(a, b):
    return (a + b - 1) // b



class Builder:
    def __init__(self, nc, cfg, stack):
        self.nc = nc
        self.cfg = cfg
        self.stack = stack
        self.sems = Sems(nc, stack)
        c = cfg
        self.TOWN = c["NB"] * c["S"]
        self.TPRE = max((h["t0"] for h in c["HOT"].values()), default=0)
        self.DC = c["D"] // 128
        self.IC = c["I"] // 128
        self.ISC = c["ISH"] // 128
        self.CPADMAX = max((h["cpad"] for h in c["HOT"].values()), default=128)
        self.ESTMAX = ceil_div(self.CPADMAX + 1, 128) * 128
        self.ETROWS = c["E"] * self.ESTMAX
        self.dram = {}
        self.declare_io()
        self.cid_regs = {}
        self._alloc_psum()

    def sb(self, name, shape, dtype):
        return self.stack.enter_context(self.nc.sbuf_tensor(name, shape, dtype))

    def ps(self, name, shape, dtype=F32):
        return self.stack.enter_context(self.nc.psum_tensor(name, shape, dtype))

    def declare_io(self):
        nc, c = self.nc, self.cfg
        D, E, I, ISH, S, NB = c["D"], c["E"], c["I"], c["ISH"], c["S"], c["NB"]
        QT = c["QT"]
        T = self.TOWN
        dd = self.dram
        ein = lambda n, sh, dt=F32: nc.dram_tensor(n, sh, dt, kind="ExternalInput").ap()
        intl = lambda n, sh, dt: nc.dram_tensor(n, sh, dt, kind="Internal").ap()
        dd["cid"] = ein("cid", [1, 1], I32)
        dd["x_own"] = ein("x_own", [T, D])
        if self.TPRE:
            dd["x_pre"] = ein("x_pre", [self.TPRE, D])
        dd["w_qkvo"] = ein("w_qkvo", [4, D, D])
        dd["b_qkvo"] = ein("b_qkvo", [4, D])
        dd["gate_w"] = ein("gate_w", [E, D])
        dd["sw_g"] = ein("sw_g", [D, ISH])
        dd["sw_u"] = ein("sw_u", [D, ISH])
        dd["sw_d"] = ein("sw_d", [ISH, D])
        dd["ew_g"] = ein("ew_g", [E, D, I])
        dd["ew_u"] = ein("ew_u", [E, D, I])
        dd["ew_d"] = ein("ew_d", [E, I, D])
        nr = c["S"] // 128
        dd["c_bt"] = ein("c_bt", [nr, 128, QT])
        dd["c_id"] = ein("c_id", [128, 128])
        dd["c_ltri"] = ein("c_ltri", [128, 128])
        dd["c_iota16"] = ein("c_iota16", [128, E])
        dd["c_slopes"] = ein("c_slopes", [1, c["NH"]])
        dd["out"] = nc.dram_tensor("out", [T, D], F32, kind="ExternalOutput").ap()
        dd["xn"] = intl("xn_h", [T, D], BF16)
        dd["xnfm"] = intl("xnfm_h", [NB, D, S], BF16)
        dd["qf"] = intl("qf_h", [NB, D, S], BF16)
        dd["kf"] = intl("kf_h", [NB, D, S], BF16)
        dd["vt"] = intl("vt_h", [NB, S, D], BF16)
        dd["aof"] = intl("aof_h", [NB, D, S], BF16)
        dd["oattn"] = intl("oattn_h", [T, D], BF16)
        dd["gsh"] = intl("gsh_h", [NB, ISH, S], BF16)
        dd["msh"] = intl("msh_h", [NB, ISH, S], BF16)
        dd["lown"] = intl("lown_h", [T, E], F32)
        dd["moe"] = intl("moe_h", [T + self.CPADMAX, D], F32)
        dd["etab"] = intl("etab_h", [self.ETROWS, 2], F32)

    def barrier(self, tag):
        return

    def ph_setup(self):
        nc, c, dd = self.nc, self.cfg, self.dram
        D, E, NH = c["D"], c["E"], c["NH"]
        DC = self.DC
        NR = c["S"] // 128
        QT = c["QT"]
        self.cid_sb = self.sb("cid_sb", [1, 1], I32)
        self.ident = self.sb("ident", [128, 128], BF16)
        self.ltri = self.sb("ltri", [128, 128], BF16)
        self.iota16 = self.sb("iota16", [128, E], F32)
        self.slopes = self.sb("slopes", [1, NH], F32)
        self.bias_sb = self.sb("bias_sb", [128, 4, DC], F32)
        self.bqs_sb = self.sb("bqs_sb", [128, DC], F32)
        self.ones1 = self.sb("ones1", [1, 128], F32)
        self.zero128 = self.sb("zero128", [128, 2048 if c["D"] >= 2048 else c["D"]], F32)
        with nc.Block() as bl:
            @bl.sync
            def _(sy):
                h, v = self.sems.inc("su_d", 16); sy.dma_start(self.cid_sb[:], dd["cid"][:]).then_inc(h, 16)
                h, v = self.sems.inc("su_d", 16); sy.dma_start(self.iota16[:], dd["c_iota16"][:]).then_inc(h, 16)
                h, v = self.sems.inc("su_d", 16); sy.dma_start(self.slopes[:], dd["c_slopes"][:]).then_inc(h, 16)
                h, v = self.sems.inc("su_d", 16)
                sy.dma_start(self.bias_sb[:], bass.AP(dd["b_qkvo"].tensor, 0,
                             [[1, 128], [D, 4], [128, DC]]),
                             allow_slow_non_contiguous=True).then_inc(h, 16)

            @bl.gpsimd
            def _(g):
                h, v = self.sems.inc("su_g", 16); g.dma_start(self.ident[:], dd["c_id"][:]).then_inc(h, 16)
                h, v = self.sems.inc("su_g", 16); g.dma_start(self.ltri[:], dd["c_ltri"][:]).then_inc(h, 16)
            @bl.vector
            def _(ve):
                ve.wait_ge(self.sems.get("su_d"), self.sems.val["su_d"])
                ve.memset(self.ones1[:], 1.0)
                ve.memset(self.zero128[:], 0.0)
                ve.tensor_scalar(self.bqs_sb[:], self.bias_sb[:, 0, :],
                                 1.0 / math.sqrt(c["HD"]), None, op0=OP.mult)
                h, v = self.sems.inc("su_v", 1)
                ve.drain().then_inc(h, 1)
        with nc.Block() as bl:
            def mk(engname, dec):
                def body(eng):
                    eng.wait_ge(self.sems.get("su_d"), self.sems.val["su_d"])
                    reg = eng.alloc_register("cidreg_" + engname)
                    eng.load(reg, self.cid_sb[0:1, 0:1])
                    self.cid_regs[engname] = reg
                dec(body)
            mk("sy", bl.sync); mk("sc", bl.scalar); mk("ve", bl.vector)
            mk("te", bl.tensor); mk("gp", bl.gpsimd)
        with nc.Block() as bl:
            @bl.gpsimd
            def _(g):
                g.wait_ge(self.sems.get("su_g"), self.sems.val["su_g"])
        self.barrier("setup")

    def ph_ln(self):
        import contextlib
        nc, c, dd = self.nc, self.cfg, self.dram
        D = c["D"]
        NTILE = self.TOWN // 128
        st_ln = contextlib.ExitStack()
        lsb = lambda n, sh, dt: st_ln.enter_context(nc.sbuf_tensor(n, sh, dt))
        xa = lsb("ln_x", [128, 2, D], F32)
        xn = lsb("ln_xn", [128, 2, D], BF16)
        st6 = lsb("ln_st", [128, 2, ceil_div(D, 512) * 6], F32)
        mv = lsb("ln_mv", [128, 2, 2], F32)
        sd = lsb("ln_sd", [128, 2, 1], F32)
        inv = lsb("ln_inv", [128, 2, 1], F32)
        ngm = lsb("ln_ngm", [128, 2, 1], F32)
        nst = ceil_div(D, 512)
        S = self.sems
        with nc.Block() as bl:
            @bl.sync
            def _(sy):
                for i in range(NTILE):
                    b = i % 2
                    if i >= 2:
                        sy.wait_ge(S.get("ln_f"), i - 1)
                    h, v = S.inc("ln_l%d" % b, 16)
                    sy.dma_start(xa[:, b, :], dd["x_own"][i * 128:(i + 1) * 128, :]).then_inc(h, 16)
            @bl.vector
            def _(ve):
                for i in range(NTILE):
                    b = i % 2
                    ve.wait_ge(S.get("ln_l%d" % b), 16 * (i // 2 + 1))
                    for j in range(nst):
                        n = min(512, D - j * 512)
                        ve.bn_stats(st6[:, b, j * 6:(j + 1) * 6], xa[:, b, j * 512:j * 512 + n])
                    ve.drain()
                    ve.bn_aggr(mv[:, b, :], st6[:, b, :])
                    ve.drain()
                    ve.tensor_scalar(mv[:, b, 1:2], mv[:, b, 1:2], c["LN_EPS"], None, op0=OP.add)
                    h, v = S.inc("ln_s", 1)
                    ve.drain().then_inc(h, 1)
                    ve.wait_ge(S.get("ln_sd"), i + 1)
                    if i >= 2:
                        ve.wait_ge(S.get("ln_st%d" % b), 16 * ((i - 2) // 2 + 1))
                    ve.reciprocal(inv[:, b, :], sd[:, b, :])
                    ve.drain()
                    ve.tensor_scalar(ngm[:, b, :], mv[:, b, 0:1], inv[:, b, :].opt(), -1.0,
                                     op0=OP.mult, op1=OP.mult)
                    ve.drain()
                    ve.tensor_scalar(xn[:, b, :], xa[:, b, :], inv[:, b, :].opt(),
                                     ngm[:, b, :].opt(), op0=OP.mult, op1=OP.add)
                    h, v = S.inc("ln_c", 1)
                    ve.drain().then_inc(h, 1)
                    h, v = S.inc("ln_f", 1)
                    ve.nop().then_inc(h, 1)
            @bl.scalar
            def _(sc):
                for i in range(NTILE):
                    b = i % 2
                    sc.wait_ge(S.get("ln_s"), i + 1)
                    sc.activation(sd[:, b, :], mv[:, b, 1:2], AF.Sqrt)
                    h, v = S.inc("ln_sd", 1)
                    sc.drain().then_inc(h, 1)
                    sc.wait_ge(S.get("ln_c"), i + 1)
                    h, v = S.inc("ln_st%d" % b, 16)
                    sc.dma_start(dd["xn"][i * 128:(i + 1) * 128, :], xn[:, b, :]).then_inc(h, 16)
                sc.wait_ge(S.get("ln_st0"), S.val["ln_st0"])
                sc.wait_ge(S.get("ln_st1"), S.val["ln_st1"])
            @bl.tensor
            def _(te):
                pass
            @bl.gpsimd
            def _(g):
                pass
        st_ln.close()
        self.barrier("ln")

def build_program(cfg, phases=("setup", "ln", "att", "route", "moe", "shared")):
    import contextlib
    nc = bass.Bass("TRN2", target_bir_lowering=False)
    stack = contextlib.ExitStack()
    b = Builder(nc, cfg, stack)
    with stack:
        if "setup" in phases:
            b.ph_setup()
            b.ph_prezero()
        if "ln" in phases:
            b.ph_ln()
        if "att" in phases:
            b.ph_att()
        if "route" in phases:
            b.ph_route()
        if "moe" in phases:
            b.ph_moe()
        if "shared" in phases:
            b.ph_shared()
        b.finalize()
    return nc, b


def host_constants_inmap(cfg):
    c = build_constants(cfg)
    return {
        "c_bt": c["btiles"],
        "c_id": c["identity"],
        "c_ltri": c["ltri"],
        "c_iota16": np.tile(c["iota16"], (128, 1)).astype(np.float32),
        "c_slopes": c["slopes"][None, :],
    }


def _finalize_stub(self):
    nc = self.nc
    with nc.Block() as bl:
        @bl.sync
        def _(sy):
            for q_ in range(4):
                n = "out_d%d" % q_
                if self.sems.val.get(n):
                    sy.wait_ge(self.sems.get(n), self.sems.val[n])


Builder.finalize = _finalize_stub


def _alloc_psum(self):
    if hasattr(self, "psA"):
        return
    self.psA = self.ps("psA", [128, 512])
    self.psB = self.ps("psB", [128, 512])
    self.psC = self.ps("psC", [128, 512])
    self.psE = self.ps("psE", [128, 512])
    self.psD = self.ps("psD", [128, 512])
    self.psF = self.ps("psF", [128, 512])
    self.psT = [self.ps("psT0", [128, 128], BF16), self.ps("psT1", [128, 128], BF16)]


def _ph_att(self):
    import contextlib
    nc, c, dd = self.nc, self.cfg, self.dram
    D, S, NB, NH, HD = c["D"], c["S"], c["NB"], c["NH"], c["HD"]
    QT, KG, NT = c["QT"], c["KG"], c["NT"]
    DC = self.DC
    NQT = QT // 128
    S_ = self.sems
    self._alloc_psum()
    rs0 = -(S - QT) // 128
    start = 2.0 ** (-8.0 / NH)
    slopes = [start * start ** i for i in range(NH)]
    S_ = self.sems

    NR = S // 128
    st_att = contextlib.ExitStack()
    self.btiles = st_att.enter_context(nc.sbuf_tensor("btiles", [128, NR, QT], F32))
    self.bvrep = st_att.enter_context(nc.sbuf_tensor("bvrep", [128, D], F32))
    self.borep = st_att.enter_context(nc.sbuf_tensor("borep", [128, D], F32))
    st_brow = contextlib.ExitStack()
    self.brow = st_brow.enter_context(nc.sbuf_tensor("brow", [1, 2, D], F32))
    with nc.Block() as bl:
        @bl.sync
        def _(sy):
            h, v = S_.inc("bt_l", 16)
            sy.dma_start(self.btiles[:], dd["c_bt"].rearrange("n p q -> p n q")).then_inc(h, 16)
            h, v = S_.inc("bt_l", 16)
            sy.dma_start(self.brow[:], bass.AP(dd["b_qkvo"].tensor, 2 * D,
                         [[2 * D, 1], [D, 2], [1, D]])).then_inc(h, 16)
            sy.wait_ge(h, v)
    base_mm = S_.val.get("su_mm", 0)
    base_v2 = S_.val.get("su_ve2", 0)
    with nc.Block() as bl:
        @bl.tensor
        def _(te):
            te.wait_ge(S_.get("bt_l"), S_.val["bt_l"])
            k = 0
            for w, dst in ((0, self.bvrep), (1, self.borep)):
                for jc in range(ceil_div(D, 512)):
                    n = min(512, D - jc * 512)
                    k += 1
                    if k > 1:
                        te.wait_ge(S_.get("su_ve2"), base_v2 + k - 1)
                    h, v = S_.inc("su_mm", 1)
                    te.matmul(self.psF[:, 0:n], self.ones1[:],
                              self.brow[:, w, jc * 512:jc * 512 + n],
                              start=True, stop=True).then_inc(h, 1)
        @bl.vector
        def _(ve):
            k = 0
            for w, dst in ((0, self.bvrep), (1, self.borep)):
                for jc in range(ceil_div(D, 512)):
                    n = min(512, D - jc * 512)
                    k += 1
                    ve.wait_ge(S_.get("su_mm"), base_mm + k)
                    h, v = S_.inc("su_ve2", 1)
                    ve.tensor_copy(dst[:, jc * 512:jc * 512 + n], self.psF[:, 0:n]).then_inc(h, 1)
    st_brow.close()

    for nb in range(NB):
        tag = "a%d" % nb
        st_b = contextlib.ExitStack()
        xnfm = st_b.enter_context(nc.sbuf_tensor("xnfm%d" % nb, [128, DC, S], BF16))
        xt = st_b.enter_context(nc.sbuf_tensor("xt%d" % nb, [128, 2, D], BF16))
        BA = {n: S_.val.get(n, 0) for n in ("t_xl0", "t_xl1", "t_tr", "t_trf", "t_fmst")}
        with nc.Block() as bl:
            @bl.sync
            def _(sy):
                for i in range(S // 128):
                    b = i % 2
                    if i >= 2:
                        sy.wait_ge(S_.get("t_trf"), BA["t_trf"] + (i - 2) * DC + DC)
                    h, v = S_.inc("t_xl%d" % b, 16)
                    sy.dma_start(xt[:, b, :], dd["xn"][nb * S + i * 128: nb * S + (i + 1) * 128, :]).then_inc(h, 16)
            @bl.tensor
            def _(te):
                for i in range(S // 128):
                    b = i % 2
                    te.wait_ge(S_.get("t_xl%d" % b), BA["t_xl%d" % b] + 16 * (i // 2 + 1))
                    for dc in range(DC):
                        t = (i * DC + dc) % 2
                        if i * DC + dc >= 2:
                            te.wait_ge(S_.get("t_trf"), BA["t_trf"] + i * DC + dc - 2 + 1)
                        h, v = S_.inc("t_tr", 1)
                        te.transpose(self.psT[t][:, :], xt[:, b, dc * 128:(dc + 1) * 128], self.ident[:]).then_inc(h, 1)
            @bl.vector
            def _(ve):
                for i in range(S // 128):
                    for dc in range(DC):
                        t = (i * DC + dc) % 2
                        ve.wait_ge(S_.get("t_tr"), BA["t_tr"] + i * DC + dc + 1)
                        h, v = S_.inc("t_trf", 1)
                        ve.tensor_copy(xnfm[:, dc, i * 128:(i + 1) * 128], self.psT[t][:, :]).then_inc(h, 1)
            @bl.scalar
            def _(sc):
                sc.wait_ge(S_.get("t_trf"), BA["t_trf"] + (S // 128) * DC)
                for dc in range(DC):
                    h, v = S_.inc("t_fmst", 16)
                    sc.dma_start(dd["xnfm"][nb, dc * 128:(dc + 1) * 128, :], xnfm[:, dc, :]).then_inc(h, 16)
                sc.wait_ge(S_.get("t_fmst"), S_.val["t_fmst"])
            @bl.gpsimd
            def _(g):
                pass

        w_sb = st_b.enter_context(nc.sbuf_tensor("wsb%d" % nb, [128, DC, D], BF16))
        ev = st_b.enter_context(nc.sbuf_tensor("ev%d" % nb, [128, 4, NT], BF16))
        projs = [("q", 0, dd["qf"]), ("k", 1, dd["kf"]), ("v", 2, dd["vt"])]
        prev_mm = None
        for pname, wi, dst in projs:
            BP = {n: S_.val.get(n, 0) for n in
                  ("p_wl", "p_mm", "p_ef", "p_st0", "p_st1", "p_st2", "p_st3")}
            with nc.Block() as bl:
                pmm = prev_mm
                @bl.gpsimd
                def _(g, pmm=pmm):
                    if pmm is not None:
                        g.wait_ge(S_.get(pmm), S_.val[pmm])
                    for dc in range(DC):
                        h, v = S_.inc("p_wl", 16)
                        g.dma_start(w_sb[:, dc, :], dd["w_qkvo"][wi, dc * 128:(dc + 1) * 128, :]).then_inc(h, 16)
                @bl.tensor
                def _(te):
                    te.wait_ge(S_.get("p_wl"), BP["p_wl"] + 16 * DC)
                    nmm = 0
                    if pname in ("q", "k"):
                        for oc in range(DC):
                            for tg in range(S // NT):
                                pb = (oc * (S // NT) + tg) % 2
                                ps = [self.psA, self.psB][pb]
                                if oc * (S // NT) + tg >= 2:
                                    te.wait_ge(S_.get("p_ef"), BP["p_ef"] + oc * (S // NT) + tg - 2 + 1)
                                h, v = S_.inc("p_mm", 1)
                                for dc in range(DC):
                                    mmi = te.matmul(ps[:, 0:NT], w_sb[:, dc, oc * 128:(oc + 1) * 128],
                                                    xnfm[:, dc, tg * NT:(tg + 1) * NT],
                                                    start=(dc == 0), stop=(dc == DC - 1))
                                    if dc == DC - 1:
                                        mmi.then_inc(h, 1)
                    else:
                        for tt in range(S // 128):
                            for jc in range(D // NT):
                                pb = (tt * (D // NT) + jc) % 2
                                ps = [self.psA, self.psB][pb]
                                if tt * (D // NT) + jc >= 2:
                                    te.wait_ge(S_.get("p_ef"), BP["p_ef"] + tt * (D // NT) + jc - 2 + 1)
                                h, v = S_.inc("p_mm", 1)
                                for dc in range(DC):
                                    mmi = te.matmul(ps[:, 0:NT], xnfm[:, dc, tt * 128:(tt + 1) * 128],
                                                    w_sb[:, dc, jc * NT:(jc + 1) * NT],
                                                    start=(dc == 0), stop=(dc == DC - 1))
                                    if dc == DC - 1:
                                        mmi.then_inc(h, 1)
                if pname in ("q", "k"):
                    @bl.scalar
                    def _(sc):
                        k = 0
                        for oc in range(DC):
                            for tg in range(S // NT):
                                pb = k % 2
                                ps = [self.psA, self.psB][pb]
                                eb = k % 4
                                sc.wait_ge(S_.get("p_mm"), BP["p_mm"] + k + 1)
                                if k >= 4:
                                    sc.wait_ge(S_.get("p_st%d" % eb), BP["p_st%d" % eb] + 16 * ((k - 4) // 4 + 1))
                                scale = 1.0 / math.sqrt(HD) if pname == "q" else 1.0
                                bias = self.bqs_sb[:, oc:oc + 1] if pname == "q" else self.bias_sb[:, 1, oc:oc + 1]
                                sc.activation(ev[:, eb, :], ps[:, 0:NT], AF.Identity,
                                              bias=bias, scale=scale)
                                h, v = S_.inc("p_ef", 1)
                                sc.drain().then_inc(h, 1)
                                h, v = S_.inc("p_st%d" % eb, 16)
                                sc.dma_start(dst[nb, oc * 128:(oc + 1) * 128, tg * NT:(tg + 1) * NT],
                                             ev[:, eb, :]).then_inc(h, 16)
                                k += 1
                        for q_ in range(4):
                            sc.wait_ge(S_.get("p_st%d" % q_), S_.val.get("p_st%d" % q_, 0))
                else:
                    @bl.vector
                    def _(ve):
                        k = 0
                        for tt in range(S // 128):
                            for jc in range(D // NT):
                                pb = k % 2
                                ps = [self.psA, self.psB][pb]
                                eb = k % 4
                                ve.wait_ge(S_.get("p_mm"), BP["p_mm"] + k + 1)
                                if k >= 4:
                                    ve.wait_ge(S_.get("p_st%d" % eb), BP["p_st%d" % eb] + 16 * ((k - 4) // 4 + 1))
                                ve.tensor_tensor(ev[:, eb, :], ps[:, 0:NT],
                                                 self.bvrep[:, jc * NT:(jc + 1) * NT], op=OP.add)
                                h, v = S_.inc("p_ef", 1)
                                ve.drain().then_inc(h, 1)
                                k += 1
                    @bl.scalar
                    def _(sc):
                        k = 0
                        for tt in range(S // 128):
                            for jc in range(D // NT):
                                eb = k % 4
                                sc.wait_ge(S_.get("p_ef"), BP["p_ef"] + k + 1)
                                h, v = S_.inc("p_st%d" % eb, 16)
                                sc.dma_start(dst[nb, tt * 128:(tt + 1) * 128, jc * NT:(jc + 1) * NT],
                                             ev[:, eb, :]).then_inc(h, 16)
                                k += 1
                        for q_ in range(4):
                            sc.wait_ge(S_.get("p_st%d" % q_), S_.val.get("p_st%d" % q_, 0))
            prev_mm = "p_mm"
        st_b.close()
        self.barrier(tag + "proj")
        self._att_inner(nb)
        self._att_wo(nb)
        self.barrier(tag + "end")
    st_att.close()


Builder.ph_att = _ph_att
Builder._alloc_psum = _alloc_psum


def _att_inner(self, nb):
    import contextlib
    nc, c, dd = self.nc, self.cfg, self.dram
    D, S, NH, HD = c["D"], c["S"], c["NH"], c["HD"]
    QT, KG = c["QT"], c["KG"]
    DC = self.DC
    assert NH == DC and HD == 128
    NQT = QT // 128
    NKG = KG // 128
    S_ = self.sems
    rs0 = -(S - QT) // 128
    start = 2.0 ** (-8.0 / NH)
    slopes = [start * start ** i for i in range(NH)]

    st = contextlib.ExitStack()
    qsup = st.enter_context(nc.sbuf_tensor("qsup%d" % nb, [128, DC, QT], BF16))
    kgrp = st.enter_context(nc.sbuf_tensor("kgrp%d" % nb, [128, DC, KG], BF16))
    vgrp = st.enter_context(nc.sbuf_tensor("vgrp%d" % nb, [128, NKG, D], BF16))
    Psb = st.enter_context(nc.sbuf_tensor("Psb%d" % nb, [128, 2, QT], BF16))
    Sb = st.enter_context(nc.sbuf_tensor("Sbb%d" % nb, [128, 2, QT], F32))
    aoac = st.enter_context(nc.sbuf_tensor("aoac%d" % nb, [128, NQT, D], F32))
    aosc = st.enter_context(nc.sbuf_tensor("aosc%d" % nb, [128, 2, 128], BF16))
    aostg = st.enter_context(nc.sbuf_tensor("aostg%d" % nb, [128, 2, QT], BF16))
    rsb = st.enter_context(nc.sbuf_tensor("rsb%d" % nb, [128, NH * NQT], F32))
    onesb = st.enter_context(nc.sbuf_tensor("onesb%d" % nb, [128, 1], BF16))

    with nc.Block() as bl:
        @bl.vector
        def _(ve):
            ve.memset(onesb[:], 1.0)
            h, v = S_.inc("c_ob", 1)
            ve.drain().then_inc(h, 1)

    for qi in range(S // QT):
        kvn = QT * (qi + 1)
        nkg = ceil_div(kvn, KG)
        steps = []
        for kg in range(nkg):
            kgn = min(KG, kvn - kg * KG)
            for h in range(NH):
                for kt in range(kgn // 128):
                    ktg = kg * NKG + kt
                    steps.append(dict(kg=kg, h=h, kt=kt, ktg=ktg,
                                      bidx=ktg - qi * NQT - rs0,
                                      gfirst=(kt == 0), glast=(kt == kgn // 128 - 1),
                                      kgn=kgn))
        ns = len(steps)
        groups = []
        for kg in range(nkg):
            for h in range(NH):
                groups.append((kg, h))
        ng = len(groups)
        gidx = {g: i for i, g in enumerate(groups)}
        for i, s in enumerate(steps):
            s["g"] = gidx[(s["kg"], s["h"])]
        glast_step = {}
        for i, s in enumerate(steps):
            glast_step[s["g"]] = i
        kg_last_step = {}
        for i, s in enumerate(steps):
            kg_last_step[s["kg"]] = i

        B = {n: S_.val.get(n, 0) for n in
             ("cq_l", "ck_l", "sc_mm", "scf", "expd", "pv", "ao_f", "rc_d",
              "as_r", "tr2", "tg_r", "st_ao", "tl_d", "c_ob")}
        ntl = NH * NQT

        with nc.Block() as bl:
            @bl.sync
            def _(sy):
                if qi > 0:
                    sy.wait_ge(S_.get("sc_mm"), B["sc_mm"])
                h, v = S_.inc("cq_l", 16)
                sy.dma_start(qsup[:, :, :],
                             bass.AP(dd["qf"].tensor, nb * D * S + qi * QT,
                                     [[S, 128], [128 * S, DC], [1, QT]])).then_inc(h, 16)
                for kg in range(nkg):
                    kgn = min(KG, kvn - kg * KG)
                    if qi + kg > 0:
                        prev_end = B["pv"] + (kg_last_step[kg - 1] + 1 if kg > 0 else 0)
                        if kg > 0:
                            sy.wait_ge(S_.get("pv"), prev_end)
                        else:
                            sy.wait_ge(S_.get("pv"), B["pv"])
                    h, v = S_.inc("ck_l", 16)
                    sy.dma_start(kgrp[:, :, 0:kgn],
                                 bass.AP(dd["kf"].tensor, nb * D * S + kg * KG,
                                         [[S, 128], [128 * S, DC], [1, kgn]])).then_inc(h, 16)
                    h, v = S_.inc("ck_l", 16)
                    sy.dma_start(vgrp[:, 0:kgn // 128, :],
                                 bass.AP(dd["vt"].tensor, nb * S * D + kg * KG * D,
                                         [[D, 128], [128 * D, kgn // 128], [1, D]])).then_inc(h, 16)
            @bl.tensor
            def _(te):
                te.wait_ge(S_.get("c_ob"), 1)
                te.wait_ge(S_.get("cq_l"), B["cq_l"] + 16)
                for i, s in enumerate(steps):
                    pb = i % 2
                    psS = [self.psC, self.psE][pb]
                    te.wait_ge(S_.get("ck_l"), B["ck_l"] + 32 * (s["kg"] + 1))
                    if i >= 2:
                        te.wait_ge(S_.get("scf"), B["scf"] + i - 1)
                    h, v = S_.inc("sc_mm", 1)
                    te.matmul(psS[:, 0:QT], kgrp[:, s["h"], s["kt"] * 128:(s["kt"] + 1) * 128],
                              qsup[:, s["h"], :], start=True, stop=True).then_inc(h, 1)
                    te.wait_ge(S_.get("expd"), B["expd"] + i + 1)
                    g = s["g"]
                    pa = [self.psA, self.psB][g % 2]
                    if s["gfirst"] and g >= 2:
                        te.wait_ge(S_.get("ao_f"), B["ao_f"] + g - 1)
                    if i == 0 and (B["rc_d"] > 0):
                        te.wait_ge(S_.get("rc_d"), B["rc_d"])
                    h, v = S_.inc("pv", 1)
                    for qt in range(NQT):
                        lp = Psb[:, i % 2, qt * 128:(qt + 1) * 128]
                        te.matmul(pa[:, qt * 128:(qt + 1) * 128], lp,
                                  vgrp[:, s["ktg"] - s["kg"] * NKG, s["h"] * HD:(s["h"] + 1) * HD],
                                  start=(s["gfirst"] and qt == 0),
                                  stop=(s["glast"] and qt == NQT - 1))
                        mmi = te.matmul(self.psD[:, s["h"] * NQT + qt:s["h"] * NQT + qt + 1], lp,
                                        onesb[:, :], start=(i == 0 and qt == 0),
                                        stop=(i == ns - 1 and qt == NQT - 1))
                    mmi.then_inc(h, 1)
                for j in range(ntl):
                    te.wait_ge(S_.get("as_r"), B["as_r"] + j + 1)
                    if j >= 2:
                        te.wait_ge(S_.get("tg_r"), B["tg_r"] + j - 1)
                    h, v = S_.inc("tr2", 1)
                    te.transpose(self.psT[j % 2][:, :], aosc[:, j % 2, :], self.ident[:]).then_inc(h, 1)
            @bl.vector
            def _(ve):
                for i, s in enumerate(steps):
                    pb = i % 2
                    psS = [self.psC, self.psE][pb]
                    ve.wait_ge(S_.get("sc_mm"), B["sc_mm"] + i + 1)
                    if i >= 2:
                        ve.wait_ge(S_.get("expd"), B["expd"] + i - 1)
                    ve.scalar_tensor_tensor(Sb[:, i % 2, :], self.btiles[:, s["bidx"], 0:QT],
                                            float(slopes[s["h"]]), psS[:, 0:QT],
                                            op0=OP.mult, op1=OP.add)
                    h, v = S_.inc("scf", 1)
                    ve.drain().then_inc(h, 1)
                    if s["glast"]:
                        g = s["g"]
                        kg, hh = groups[g]
                        pa = [self.psA, self.psB][g % 2]
                        ve.wait_ge(S_.get("pv"), B["pv"] + i + 1)
                        if qi > 0 and kg == 0:
                            ve.wait_ge(S_.get("tl_d"), B["tl_d"])
                        for qt in range(NQT):
                            if kg == 0:
                                ve.tensor_copy(aoac[:, qt, hh * HD:(hh + 1) * HD],
                                               pa[:, qt * 128:(qt + 1) * 128])
                            else:
                                ve.tensor_tensor(aoac[:, qt, hh * HD:(hh + 1) * HD],
                                                 aoac[:, qt, hh * HD:(hh + 1) * HD],
                                                 pa[:, qt * 128:(qt + 1) * 128], op=OP.add)
                        h, v = S_.inc("ao_f", 1)
                        ve.drain().then_inc(h, 1)
                ve.wait_ge(S_.get("pv"), B["pv"] + ns)
                ve.reciprocal(rsb[:, 0:ntl], self.psD[:, 0:ntl])
                h, v = S_.inc("rc_d", 1)
                ve.drain().then_inc(h, 1)
                for j in range(ntl):
                    hh, qt = j // NQT, j % NQT
                    if j >= 2:
                        ve.wait_ge(S_.get("tr2"), B["tr2"] + j - 1)
                    ve.tensor_scalar(aosc[:, j % 2, :], aoac[:, qt, hh * HD:(hh + 1) * HD],
                                     rsb[:, j:j + 1].opt(), None, op0=OP.mult)
                    h, v = S_.inc("as_r", 1)
                    ve.drain().then_inc(h, 1)
                    ve.wait_ge(S_.get("tr2"), B["tr2"] + j + 1)
                    if j >= 2 * NQT:
                        ve.wait_ge(S_.get("st_ao"), B["st_ao"] + 16 * (j // NQT - 1))
                    ve.tensor_copy(aostg[:, (j // NQT) % 2, qt * 128:(qt + 1) * 128],
                                   self.psT[j % 2][:, :])
                    h, v = S_.inc("tg_r", 1)
                    ve.drain().then_inc(h, 1)
                h, v = S_.inc("tl_d", 1)
                ve.nop().then_inc(h, 1)
            @bl.scalar
            def _(sc):
                for i, s in enumerate(steps):
                    sc.wait_ge(S_.get("scf"), B["scf"] + i + 1)
                    if i >= 2:
                        sc.wait_ge(S_.get("pv"), B["pv"] + i - 1)
                    sc.activation(Psb[:, i % 2, :], Sb[:, i % 2, :], AF.Exp)
                    h, v = S_.inc("expd", 1)
                    sc.drain().then_inc(h, 1)
                for hh in range(NH):
                    sc.wait_ge(S_.get("tg_r"), B["tg_r"] + (hh + 1) * NQT)
                    h, v = S_.inc("st_ao", 16)
                    sc.dma_start(bass.AP(dd["aof"].tensor,
                                         nb * D * S + (hh * HD) * S + qi * QT,
                                         [[S, 128], [1, QT]]),
                                 aostg[:, hh % 2, :]).then_inc(h, 16)
                sc.wait_ge(S_.get("st_ao"), S_.val["st_ao"])
            @bl.gpsimd
            def _(g2):
                pass
    st.close()


Builder._att_inner = _att_inner


def _att_wo(self, nb):
    import contextlib
    nc, c, dd = self.nc, self.cfg, self.dram
    D, S, NT = c["D"], c["S"], c["NT"]
    DC = self.DC
    S_ = self.sems
    st = contextlib.ExitStack()
    wo_sb = st.enter_context(nc.sbuf_tensor("wosb%d" % nb, [128, DC, D], BF16))
    ao_sl = st.enter_context(nc.sbuf_tensor("aosl%d" % nb, [128, 2, DC, 128], BF16))
    osb = st.enter_context(nc.sbuf_tensor("osb%d" % nb, [128, 4, NT], BF16))
    NJC = D // NT
    B = {n: S_.val.get(n, 0) for n in ("wo_l", "aos_l0", "aos_l1", "wo_mm", "wo_ef",
                                       "wo_st0", "wo_st1", "wo_st2", "wo_st3")}
    with nc.Block() as bl:
        @bl.gpsimd
        def _(g):
            for dc in range(DC):
                h, v = S_.inc("wo_l", 16)
                g.dma_start(wo_sb[:, dc, :], dd["w_qkvo"][3, dc * 128:(dc + 1) * 128, :]).then_inc(h, 16)
        @bl.sync
        def _(sy):
            for qt in range(S // 128):
                b = qt % 2
                if qt >= 2:
                    sy.wait_ge(S_.get("wo_mm"), B["wo_mm"] + (qt - 1) * NJC)
                h, v = S_.inc("aos_l%d" % b, 16)
                sy.dma_start(ao_sl[:, b, :, :],
                             bass.AP(dd["aof"].tensor, nb * D * S + qt * 128,
                                     [[S, 128], [128 * S, DC], [1, 128]])).then_inc(h, 16)
        @bl.tensor
        def _(te):
            te.wait_ge(S_.get("wo_l"), B["wo_l"] + 16 * DC)
            k = 0
            for qt in range(S // 128):
                b = qt % 2
                te.wait_ge(S_.get("aos_l%d" % b), B["aos_l%d" % b] + 16 * (qt // 2 + 1))
                for jc in range(NJC):
                    pb = k % 2
                    ps = [self.psA, self.psB][pb]
                    if k >= 2:
                        te.wait_ge(S_.get("wo_ef"), B["wo_ef"] + k - 1)
                    h, v = S_.inc("wo_mm", 1)
                    for dc in range(DC):
                        mmi = te.matmul(ps[:, 0:NT], ao_sl[:, b, dc, :],
                                        wo_sb[:, dc, jc * NT:(jc + 1) * NT],
                                        start=(dc == 0), stop=(dc == DC - 1))
                    mmi.then_inc(h, 1)
                    k += 1
        @bl.vector
        def _(ve):
            k = 0
            for qt in range(S // 128):
                for jc in range(NJC):
                    pb = k % 2
                    ps = [self.psA, self.psB][pb]
                    eb = k % 4
                    ve.wait_ge(S_.get("wo_mm"), B["wo_mm"] + k + 1)
                    if k >= 4:
                        ve.wait_ge(S_.get("wo_st%d" % eb), B["wo_st%d" % eb] + 16 * ((k - 4) // 4 + 1))
                    ve.tensor_tensor(osb[:, eb, :], ps[:, 0:NT],
                                     self.borep[:, jc * NT:(jc + 1) * NT], op=OP.add)
                    h, v = S_.inc("wo_ef", 1)
                    ve.drain().then_inc(h, 1)
                    k += 1
        @bl.scalar
        def _(sc):
            k = 0
            for qt in range(S // 128):
                for jc in range(NJC):
                    eb = k % 4
                    sc.wait_ge(S_.get("wo_ef"), B["wo_ef"] + k + 1)
                    h, v = S_.inc("wo_st%d" % eb, 16)
                    sc.dma_start(dd["oattn"][nb * S + qt * 128: nb * S + (qt + 1) * 128,
                                             jc * NT:(jc + 1) * NT],
                                 osb[:, eb, :]).then_inc(h, 16)
                    k += 1
            for q_ in range(4):
                sc.wait_ge(S_.get("wo_st%d" % q_), S_.val.get("wo_st%d" % q_, 0))
    st.close()


Builder._att_wo = _att_wo


def _ph_prezero(self):
    nc, c, dd = self.nc, self.cfg, self.dram
    D = c["D"]
    S_ = self.sems
    T = self.TOWN
    with nc.Block() as bl:
        @bl.sync
        def _(sy):
            for i in range((T + self.CPADMAX) // 128):
                h, v = S_.inc("pz", 16)
                sy.dma_start(dd["moe"][i * 128:(i + 1) * 128, :],
                             self.zero128[:, 0:D]).then_inc(h, 16)
            R = self.ETROWS
            h, v = S_.inc("pz", 16)
            sy.dma_start(dd["etab"].rearrange("(p a) b -> p a b", p=128),
                         self.zero128[:, 0:(R // 128) * 2].rearrange("p (a b) -> p a b", b=2)).then_inc(h, 16)
            sy.wait_ge(S_.get("pz"), S_.val["pz"])


Builder.ph_prezero = _ph_prezero


def _ph_shared(self):
    import contextlib
    nc, c, dd = self.nc, self.cfg, self.dram
    D, S, NB, ISH, NT = c["D"], c["S"], c["NB"], c["ISH"], c["NT"]
    DC, ISC = self.DC, self.ISC
    S_ = self.sems
    NTG = S // NT

    for nb in range(NB):
        for which, wten, dsth in (("g", dd["sw_g"], dd["gsh"]), ("u", dd["sw_u"], dd["msh"])):
            tag = "sh_"
            bn = "s%d%s" % (nb, which)
            st = contextlib.ExitStack()
            w_sb = st.enter_context(nc.sbuf_tensor("swsb" + bn, [128, DC, ISH], BF16))
            xnsl = st.enter_context(nc.sbuf_tensor("xnsl" + bn, [128, 2, DC, NT], BF16))
            ev = st.enter_context(nc.sbuf_tensor("swev" + bn, [128, 4, NT], BF16))
            gsl = st.enter_context(nc.sbuf_tensor("gsl" + bn, [128, 2, NT], BF16)) if which == "u" else None
            B = {n: S_.val.get(n, 0) for n in
                 (tag + "wl", tag + "x0", tag + "x1", tag + "mm", tag + "ef",
                  tag + "st0", tag + "st1", tag + "st2", tag + "st3",
                  tag + "gl0", tag + "gl1")}
            with nc.Block() as bl:
                @bl.gpsimd
                def _(g):
                    for dc in range(DC):
                        h, v = S_.inc(tag + "wl", 16)
                        g.dma_start(w_sb[:, dc, :], wten[dc * 128:(dc + 1) * 128, :]).then_inc(h, 16)
                @bl.sync
                def _(sy):
                    for tg in range(NTG):
                        b = tg % 2
                        if tg >= 2:
                            sy.wait_ge(S_.get(tag + "mm"), B[tag + "mm"] + (tg - 1) * ISC)
                        h, v = S_.inc(tag + "x%d" % b, 16)
                        sy.dma_start(xnsl[:, b, :, :],
                                     bass.AP(dd["xnfm"].tensor, nb * D * S + tg * NT,
                                             [[S, 128], [128 * S, DC], [1, NT]])).then_inc(h, 16)
                        if which == "u":
                            for ic in range(ISC):
                                kk = tg * ISC + ic
                                if kk >= 2:
                                    sy.wait_ge(S_.get(tag + "ef"), B[tag + "ef"] + kk - 1)
                                h, v = S_.inc(tag + "gl%d" % (kk % 2), 16)
                                sy.dma_start(gsl[:, kk % 2, :],
                                             bass.AP(dd["gsh"].tensor, nb * ISH * S + (ic * 128) * S + tg * NT,
                                                     [[S, 128], [1, NT]])).then_inc(h, 16)
                @bl.tensor
                def _(te):
                    te.wait_ge(S_.get(tag + "wl"), B[tag + "wl"] + 16 * DC)
                    k = 0
                    for tg in range(NTG):
                        b = tg % 2
                        te.wait_ge(S_.get(tag + "x%d" % b), B[tag + "x%d" % b] + 16 * (tg // 2 + 1))
                        for ic in range(ISC):
                            pb = k % 2
                            ps = [self.psA, self.psB][pb]
                            if k >= 2:
                                te.wait_ge(S_.get(tag + "ef"), B[tag + "ef"] + k - 1)
                            h, v = S_.inc(tag + "mm", 1)
                            for dc in range(DC):
                                mmi = te.matmul(ps[:, 0:NT], w_sb[:, dc, ic * 128:(ic + 1) * 128],
                                                xnsl[:, b, dc, :], start=(dc == 0), stop=(dc == DC - 1))
                            mmi.then_inc(h, 1)
                            k += 1
                if which == "g":
                    @bl.scalar
                    def _(sc):
                        k = 0
                        for tg in range(NTG):
                            for ic in range(ISC):
                                pb = k % 2
                                ps = [self.psA, self.psB][pb]
                                eb = k % 4
                                sc.wait_ge(S_.get(tag + "mm"), B[tag + "mm"] + k + 1)
                                if k >= 4:
                                    sc.wait_ge(S_.get(tag + "st%d" % eb), B[tag + "st%d" % eb] + 16 * ((k - 4) // 4 + 1))
                                sc.activation(ev[:, eb, :], ps[:, 0:NT], AF.Silu)
                                h, v = S_.inc(tag + "ef", 1)
                                sc.drain().then_inc(h, 1)
                                h, v = S_.inc(tag + "st%d" % eb, 16)
                                sc.dma_start(bass.AP(dsth.tensor, nb * ISH * S + (ic * 128) * S + tg * NT,
                                                     [[S, 128], [1, NT]]),
                                             ev[:, eb, :]).then_inc(h, 16)
                                k += 1
                        for q_ in range(4):
                            sc.wait_ge(S_.get(tag + "st%d" % q_), S_.val.get(tag + "st%d" % q_, 0))
                    @bl.vector
                    def _(ve):
                        pass
                else:
                    @bl.vector
                    def _(ve):
                        k = 0
                        for tg in range(NTG):
                            for ic in range(ISC):
                                pb = k % 2
                                ps = [self.psA, self.psB][pb]
                                eb = k % 4
                                ve.wait_ge(S_.get(tag + "mm"), B[tag + "mm"] + k + 1)
                                ve.wait_ge(S_.get(tag + "gl%d" % (k % 2)), B[tag + "gl%d" % (k % 2)] + 16 * (k // 2 + 1))
                                if k >= 4:
                                    ve.wait_ge(S_.get(tag + "st%d" % eb), B[tag + "st%d" % eb] + 16 * ((k - 4) // 4 + 1))
                                ve.tensor_tensor(ev[:, eb, :], ps[:, 0:NT], gsl[:, k % 2, :], op=OP.mult)
                                h, v = S_.inc(tag + "ef", 1)
                                ve.drain().then_inc(h, 1)
                                k += 1
                    @bl.scalar
                    def _(sc):
                        k = 0
                        for tg in range(NTG):
                            for ic in range(ISC):
                                eb = k % 4
                                sc.wait_ge(S_.get(tag + "ef"), B[tag + "ef"] + k + 1)
                                h, v = S_.inc(tag + "st%d" % eb, 16)
                                sc.dma_start(bass.AP(dsth.tensor, nb * ISH * S + (ic * 128) * S + tg * NT,
                                                     [[S, 128], [1, NT]]),
                                             ev[:, eb, :]).then_inc(h, 16)
                                k += 1
                        for q_ in range(4):
                            sc.wait_ge(S_.get(tag + "st%d" % q_), S_.val.get(tag + "st%d" % q_, 0))
            st.close()

    st = contextlib.ExitStack()
    swd = st.enter_context(nc.sbuf_tensor("swd", [128, ISC, D], BF16))
    msl = st.enter_context(nc.sbuf_tensor("msl", [128, 2, ISC, 128], BF16))
    xsl = st.enter_context(nc.sbuf_tensor("xsl", [128, 2, D], F32))
    osl = st.enter_context(nc.sbuf_tensor("osl", [128, 2, D], BF16))
    mosl = st.enter_context(nc.sbuf_tensor("mosl", [128, 2, D], F32))
    acc = st.enter_context(nc.sbuf_tensor("acc", [128, 2, NT], F32))
    oev = st.enter_context(nc.sbuf_tensor("oev", [128, 4, NT], F32))
    NJC = D // NT
    B = {n: S_.val.get(n, 0) for n in
         ("y_wl", "y_m0", "y_m1", "y_x0", "y_x1", "y_mm", "y_ef", "y_ac",
          "out_d0", "out_d1", "out_d2", "out_d3")}
    with nc.Block() as bl:
        @bl.gpsimd
        def _(g):
            for ic in range(ISC):
                h, v = S_.inc("y_wl", 16)
                g.dma_start(swd[:, ic, :], dd["sw_d"][ic * 128:(ic + 1) * 128, :]).then_inc(h, 16)
        @bl.sync
        def _(sy):
            for qt in range(self.TOWN // 128):
                b = qt % 2
                nb = qt // (S // 128)
                qtl = qt % (S // 128)
                if qt >= 2:
                    sy.wait_ge(S_.get("y_mm"), B["y_mm"] + (qt - 1) * NJC)
                    sy.wait_ge(S_.get("y_ac"), B["y_ac"] + (qt - 1) * NJC)
                h, v = S_.inc("y_m%d" % b, 16)
                sy.dma_start(msl[:, b, :, :],
                             bass.AP(dd["msh"].tensor, nb * ISH * S + qtl * 128,
                                     [[S, 128], [128 * S, ISC], [1, 128]])).then_inc(h, 16)
                h, v = S_.inc("y_x%d" % b, 48)
                sy.dma_start(xsl[:, b, :], dd["x_own"][qt * 128:(qt + 1) * 128, :]).then_inc(h, 16)
                sy.dma_start(osl[:, b, :], dd["oattn"][qt * 128:(qt + 1) * 128, :]).then_inc(h, 16)
                sy.dma_start(mosl[:, b, :], dd["moe"][qt * 128:(qt + 1) * 128, :]).then_inc(h, 16)
        @bl.tensor
        def _(te):
            te.wait_ge(S_.get("y_wl"), B["y_wl"] + 16 * ISC)
            k = 0
            for qt in range(self.TOWN // 128):
                b = qt % 2
                te.wait_ge(S_.get("y_m%d" % b), B["y_m%d" % b] + 16 * (qt // 2 + 1))
                for jc in range(NJC):
                    pb = k % 2
                    ps = [self.psA, self.psB][pb]
                    if k >= 2:
                        te.wait_ge(S_.get("y_ef"), B["y_ef"] + k - 1)
                    h, v = S_.inc("y_mm", 1)
                    for ic in range(ISC):
                        mmi = te.matmul(ps[:, 0:NT], msl[:, b, ic, :],
                                        swd[:, ic, jc * NT:(jc + 1) * NT],
                                        start=(ic == 0), stop=(ic == ISC - 1))
                    mmi.then_inc(h, 1)
                    k += 1
        @bl.vector
        def _(ve):
            k = 0
            for qt in range(self.TOWN // 128):
                b = qt % 2
                ve.wait_ge(S_.get("y_x%d" % b), B["y_x%d" % b] + 48 * (qt // 2 + 1))
                for jc in range(NJC):
                    pb = k % 2
                    ps = [self.psA, self.psB][pb]
                    ve.wait_ge(S_.get("y_mm"), B["y_mm"] + k + 1)
                    ve.tensor_tensor(acc[:, pb, :], ps[:, 0:NT],
                                     xsl[:, b, jc * NT:(jc + 1) * NT], op=OP.add)
                    h, v = S_.inc("y_ef", 1)
                    ve.drain().then_inc(h, 1)
                    ve.tensor_tensor(acc[:, pb, :], acc[:, pb, :],
                                     osl[:, b, jc * NT:(jc + 1) * NT], op=OP.add)
                    ve.drain()
                    eb = k % 4
                    if k >= 4:
                        ve.wait_ge(S_.get("out_d%d" % eb), B["out_d%d" % eb] + 16 * ((k - 4) // 4 + 1))
                    ve.tensor_tensor(oev[:, eb, :], acc[:, pb, :],
                                     mosl[:, b, jc * NT:(jc + 1) * NT], op=OP.add)
                    h, v = S_.inc("y_ac", 1)
                    ve.drain().then_inc(h, 1)
                    k += 1
        @bl.scalar
        def _(sc):
            k = 0
            for qt in range(self.TOWN // 128):
                for jc in range(NJC):
                    eb = k % 4
                    sc.wait_ge(S_.get("y_ac"), B["y_ac"] + k + 1)
                    h, v = S_.inc("out_d%d" % eb, 16)
                    sc.dma_start(dd["out"][qt * 128:(qt + 1) * 128, jc * NT:(jc + 1) * NT],
                                 oev[:, eb, :]).then_inc(h, 16)
                    k += 1
            for q_ in range(4):
                sc.wait_ge(S_.get("out_d%d" % q_), S_.val.get("out_d%d" % q_, 0))
    st.close()


Builder.ph_shared = _ph_shared


class BranchCnt:

    def __init__(self, sems):
        self.S = sems
        self.val = {}

    def inc(self, name, by):
        h = self.S.get(name)
        self.val[name] = self.val.get(name, 0) + by
        return h, self.val[name]

    def get(self, name):
        return self.S.get(name)

    def v(self, name):
        return self.val.get(name, 0)


def _hot_block(self, hc, bodies):
    nc = self.nc
    with nc.Block() as bl:
        for engname, dec in (("sy", bl.sync), ("sc", bl.scalar), ("ve", bl.vector),
                             ("te", bl.tensor), ("gp", bl.gpsimd)):
            body = bodies.get(engname)
            def wrap(eng, body=body, engname=engname):
                if body is None:
                    return
                with eng.If_eq(self.cid_regs[engname], hc):
                    body(eng)
            dec(wrap)


Builder.hot_block = _hot_block


def _ph_route(self):
    import contextlib
    nc, c, dd = self.nc, self.cfg, self.dram
    D, E, K, CAP, S = c["D"], c["E"], c["TOPK"], c["CAP"], c["S"]
    DC = self.DC
    EST = self.ESTMAX
    S_ = self.sems
    JTM = max((min(h["t0"] + self.TOWN, c["TH"]) // 128 for h in c["HOT"].values()), default=1)
    JAM = JTM * K
    DELTA = 2.0 ** -17

    st = contextlib.ExitStack()
    sb = lambda n, sh, dt=F32: st.enter_context(nc.sbuf_tensor(n, sh, dt))
    xsp = sb("xsp", [128, 2, D], BF16)
    xfmt = sb("xfmt", [128, 2, DC, 128], BF16)
    gwsb = sb("gwsb", [16, D], F32)
    gtmp = sb("gtmp", [16, D], F32)
    ghat = sb("ghat", [16, 3, D], BF16)
    gfm = sb("gfm", [128, 3, DC, E], BF16)
    lgsp = sb("lgsp", [128, JTM, E], F32)
    lg2 = sb("lg2", [128, JTM, E], F32)
    eqk = sb("eqk", [128, K, JTM, E], F32)
    idxa = sb("idxa", [128, JTM, K], F32)
    mrow = sb("mrow", [128, JTM], F32)
    ohs = sb("ohs", [128, E, JAM], F32)
    scn = sb("scn", [128, E, JAM], F32)
    csb = sb("csb", [128, E], F32)
    cbf = sb("cbf", [128, E], BF16)
    basesb = sb("basesb", [128, E], F32)
    basem = sb("basem", [128, E], F32)
    bndrow = sb("bndrow", [1, E], F32)
    rg = sb("rg", [128, JAM], F32)
    rl = sb("rl", [128, JAM], F32)
    tmpa = sb("tmpa", [128, JAM], F32)
    tmpb = sb("tmpb", [128, JAM], F32)
    tokl = sb("tokl", [128, JTM], I32)
    toklf = sb("toklf", [128, JTM], F32)
    ownm = sb("ownm", [128, JTM], F32)
    wk4 = sb("wk4", [128, K, JTM], F32)
    wden = sb("wden", [128, JTM], F32)
    pairs = sb("pairs", [128, JAM, 2], F32)
    gtg = sb("gtg", [128, JAM], F32)
    gtgi = sb("gtgi", [128, JAM], I32)
    losl = sb("losl", [128, 2, DC, 128], BF16)
    lown_sb = sb("lown_sb", [128, JTM, E], F32)
    wtmp = sb("wtmp", [128, JTM * E], F32)
    bnd2 = sb("bnd2", [1, E], F32)

    for hc, hcfg in sorted(c["HOT"].items()):
        t0 = hcfg["t0"]
        cpad = hcfg["cpad"]
        TR = min(t0 + self.TOWN, c["TH"])
        JT = TR // 128
        JA = JT * K
        p_b, j_b = t0 // JT, t0 % JT
        NLO = (TR - t0) // 128
        BC = BranchCnt(S_)
        JE = JT * E

        b0 = {n: BC.v(n) for n in ("r_d", "r_v", "r_t", "r_tf")}

        def s1_sy(sy):
            h, v = BC.inc("r_d", 16)
            sy.dma_start(gwsb[:], dd["gate_w"][:]).then_inc(h, 16)
            sy.wait_ge(h, v)

        def s1_ve(ve):
            ve.wait_ge(BC.get("r_d"), b0["r_d"] + 16)
            ve.tensor_copy(ghat[:, 2, :], gwsb[:])
            ve.drain()
            ve.tensor_reduce(gtmp[:, 0:1], gwsb[:], axis=AX.X, op=OP.add)
            ve.drain()
            ve.tensor_scalar(gtmp[:, 0:1], gtmp[:, 0:1], 1.0 / D, None, op0=OP.mult)
            ve.drain()
            ve.tensor_scalar(gwsb[:], gwsb[:], gtmp[:, 0:1].opt(), None, op0=OP.subtract)
            ve.drain()
            ve.tensor_copy(ghat[:, 0, :], gwsb[:])
            ve.drain()
            ve.tensor_copy(gtmp[:], ghat[:, 0, :])
            ve.drain()
            ve.tensor_tensor(gwsb[:], gwsb[:], gtmp[:], op=OP.subtract)
            ve.drain()
            ve.tensor_copy(ghat[:, 1, :], gwsb[:])
            h, v = BC.inc("r_v", 1)
            ve.drain().then_inc(h, 1)

        def s1_te(te):
            te.wait_ge(BC.get("r_v"), b0["r_v"] + 1)
            k = 0
            for which in range(3):
                for dc in range(DC):
                    t = k % 2
                    if k >= 2:
                        te.wait_ge(BC.get("r_tf"), b0["r_tf"] + k - 1)
                    h, v = BC.inc("r_t", 1)
                    te.transpose(self.psT[t][:, 0:16], ghat[0:16, which, dc * 128:(dc + 1) * 128],
                                 self.ident[0:16, 0:16]).then_inc(h, 1)
                    k += 1

        def s1_ve2(ve):
            k = 0
            for which in range(3):
                for dc in range(DC):
                    t = k % 2
                    ve.wait_ge(BC.get("r_t"), b0["r_t"] + k + 1)
                    h, v = BC.inc("r_tf", 1)
                    ve.tensor_copy(gfm[:, which, dc, :], self.psT[t][:, 0:16]).then_inc(h, 1)
                    k += 1
        self.hot_block(hc, dict(sy=s1_sy, ve=s1_ve))
        self.hot_block(hc, dict(te=s1_te, ve=s1_ve2))

        b2 = {n: BC.v(n) for n in ("r_gd0", "r_gd1", "r_t", "r_tf", "r_mm", "r_lge")}

        def s2_gp(g):
            for jj in range(JT):
                bslot = jj % 2
                if jj >= 2:
                    g.wait_ge(BC.get("r_tf"), b2["r_tf"] + (jj - 1) * DC)
                if t0 == 0:
                    h, v = BC.inc("r_gd%d" % bslot, 16)
                    g.dma_start(xsp[:, bslot, :],
                                bass.AP(dd["x_own"].tensor, jj * D, [[JT * D, 128], [1, D]])).then_inc(h, 16)
                else:
                    p_cut = ceil_div(t0 - jj, JT)
                    h, v = BC.inc("r_gd%d" % bslot, 16)
                    g.dma_start(xsp[0:p_cut, bslot, :],
                                bass.AP(dd["x_pre"].tensor, jj * D, [[JT * D, p_cut], [1, D]])).then_inc(h, 16)
                    h, v = BC.inc("r_gd%d" % bslot, 16)
                    g.dma_start(xsp[p_cut:128, bslot, :],
                                bass.AP(dd["x_own"].tensor, (p_cut * JT + jj - t0) * D,
                                        [[JT * D, 128 - p_cut], [1, D]])).then_inc(h, 16)

        nldma = 1 if t0 == 0 else 2

        def s2_te(te):
            for jj in range(JT):
                bslot = jj % 2
                te.wait_ge(BC.get("r_gd%d" % bslot), b2["r_gd%d" % bslot] + 16 * nldma * (jj // 2 + 1))
                for dc in range(DC):
                    k = jj * DC + dc
                    t = k % 2
                    if k >= 2:
                        te.wait_ge(BC.get("r_tf"), b2["r_tf"] + k - 1)
                    h, v = BC.inc("r_t", 1)
                    te.transpose(self.psT[t][:, :], xsp[:, bslot, dc * 128:(dc + 1) * 128],
                                 self.ident[:]).then_inc(h, 1)
                te.wait_ge(BC.get("r_tf"), b2["r_tf"] + (jj + 1) * DC)
                if jj >= 1:
                    te.wait_ge(BC.get("r_lge"), b2["r_lge"] + jj)
                h, v = BC.inc("r_mm", 1)
                for dc in range(DC):
                    for hl in range(2):
                        mmi = te.matmul(self.psF[:, 0:E], xfmt[:, jj % 2, dc, :],
                                        gfm[:, hl, dc, :],
                                        start=(dc == 0 and hl == 0),
                                        stop=(dc == DC - 1 and hl == 1))
                mmi.then_inc(h, 1)

        def s2_ve(ve):
            for jj in range(JT):
                if jj >= 2:
                    ve.wait_ge(BC.get("r_mm"), b2["r_mm"] + jj - 1)
                for dc in range(DC):
                    k = jj * DC + dc
                    t = k % 2
                    ve.wait_ge(BC.get("r_t"), b2["r_t"] + k + 1)
                    h, v = BC.inc("r_tf", 1)
                    ve.tensor_copy(xfmt[:, jj % 2, dc, :], self.psT[t][:, :]).then_inc(h, 1)
                ve.wait_ge(BC.get("r_mm"), b2["r_mm"] + jj + 1)
                h, v = BC.inc("r_lge", 1)
                ve.scalar_tensor_tensor(lgsp[:, jj, :], self.iota16[:, :], -DELTA,
                                        self.psF[:, 0:E], op0=OP.mult, op1=OP.add).then_inc(h, 1)
        self.hot_block(hc, dict(gp=s2_gp, te=s2_te, ve=s2_ve))

        def s3_ve(ve):
            ve.wait_ge(BC.get("r_lge"), b2["r_lge"] + JT)
            for k in range(K):
                ve.tensor_reduce(mrow[:, 0:JT], lgsp[:, 0:JT, :], axis=AX.X, op=OP.max)
                ve.drain()
                ve.tensor_tensor(eqk[:, k, 0:JT, :], lgsp[:, 0:JT, :],
                                 mrow[:, 0:JT].unsqueeze(2).broadcast_to([128, JT, E]),
                                 op=OP.is_equal)
                ve.drain()
                ve.tensor_tensor(lg2[:, 0:JT, :], eqk[:, k, 0:JT, :],
                                 self.iota16[:, :].unsqueeze(1).broadcast_to([128, JT, E]),
                                 op=OP.mult)
                ve.drain()
                ve.tensor_reduce(idxa[:, 0:JT, k], lg2[:, 0:JT, :], axis=AX.X, op=OP.add)
                ve.drain()
                if k < K - 1:
                    ve.tensor_scalar(lg2[:, 0:JT, :], eqk[:, k, 0:JT, :], 1e9, None, op0=OP.mult)
                    ve.drain()
                    ve.tensor_tensor(lgsp[:, 0:JT, :], lgsp[:, 0:JT, :], lg2[:, 0:JT, :],
                                     op=OP.subtract)
                    ve.drain()
            h, v = BC.inc("r_v", 1)
            ve.nop().then_inc(h, 1)
        self.hot_block(hc, dict(ve=s3_ve))

        b4 = {n: BC.v(n) for n in ("r_v", "r_mm", "r_lge", "r_g", "r_bd")}
        bb4 = b4["r_bd"]

        def s4_gp(g):
            g.wait_ge(BC.get("r_v"), b4["r_v"])
            g.iota(tokl[:, 0:JT], [[1, JT]], base=-t0, channel_multiplier=JT)
            g.drain()
            h, v = BC.inc("r_g", 1)
            g.tensor_copy(toklf[:, 0:JT], tokl[:, 0:JT]).then_inc(h, 1)

        def s4_ve(ve):
            ve.wait_ge(BC.get("r_v"), b4["r_v"])
            av = idxa[:, 0:JT, :].rearrange("p j k -> p (j k)")
            for e in range(E):
                ve.tensor_scalar(ohs[:, e, 0:JA], av, float(e), None, op0=OP.is_equal)
            ve.drain()
            ve.memset(tmpa[:, 0:JA], 0.0)
            ve.drain()
            for e in range(E):
                ve.tensor_tensor_scan(scn[:, e, 0:JA], ohs[:, e, 0:JA], tmpa[:, 0:JA],
                                      0.0, op0=OP.add, op1=OP.add)
                ve.drain()
                ve.tensor_copy(csb[:, e:e + 1], scn[:, e, JA - 1:JA])
            ve.drain()
            ve.tensor_copy(cbf[:, :], csb[:, :])
            h, v = BC.inc("r_v", 1)
            ve.drain().then_inc(h, 1)
            ve.wait_ge(BC.get("r_mm"), b4["r_mm"] + 1)
            ve.tensor_copy(basesb[:, :], self.psF[:, 0:E])
            h, v = BC.inc("r_v", 1)
            ve.drain().then_inc(h, 1)
            if t0 > 0:
                ve.wait_ge(BC.get("r_bd"), bb4 + 32)
                ve.tensor_tensor(bndrow[0:1, :], bndrow[0:1, :], bnd2[0:1, :], op=OP.add)
                ve.drain()
            h, v = BC.inc("r_v", 1)
            ve.nop().then_inc(h, 1)
            if t0 > 0:
                ve.wait_ge(BC.get("r_mm"), b4["r_mm"] + 2)
                ve.tensor_tensor(basem[:, :], basesb[:, :], self.psF[:, 0:E], op=OP.subtract)
                ve.drain()
            else:
                ve.tensor_copy(basem[:, :], basesb[:, :])
                ve.drain()
            ve.memset(rg[:, 0:JA], 0.0)
            ve.memset(rl[:, 0:JA], 0.0)
            ve.drain()
            for e in range(E):
                ve.scalar_tensor_tensor(tmpb[:, 0:JA], scn[:, e, 0:JA], basesb[:, e:e + 1].opt(),
                                        ohs[:, e, 0:JA], op0=OP.add, op1=OP.mult)
                ve.drain()
                ve.tensor_tensor(rg[:, 0:JA], rg[:, 0:JA], tmpb[:, 0:JA], op=OP.add)
                ve.drain()
                ve.scalar_tensor_tensor(tmpb[:, 0:JA], scn[:, e, 0:JA], basem[:, e:e + 1].opt(),
                                        ohs[:, e, 0:JA], op0=OP.add, op1=OP.mult)
                ve.drain()
                ve.tensor_tensor(rl[:, 0:JA], rl[:, 0:JA], tmpb[:, 0:JA], op=OP.add)
                ve.drain()
            ve.tensor_scalar(rg[:, 0:JA], rg[:, 0:JA], -1.0, None, op0=OP.add)
            ve.tensor_scalar(rl[:, 0:JA], rl[:, 0:JA], -1.0, None, op0=OP.add)
            h, v = BC.inc("r_v", 1)
            ve.drain().then_inc(h, 1)

        def s4_sy(sy):
            if t0 > 0:
                sy.wait_ge(BC.get("r_v"), b4["r_v"] + 2)
                h, v = BC.inc("r_bd", 16)
                sy.dma_start(bndrow[0:1, :], basesb[p_b:p_b + 1, :]).then_inc(h, 16)
                h, v = BC.inc("r_bd", 16)
                if j_b > 0:
                    sy.dma_start(bnd2[0:1, :],
                                 scn[p_b:p_b + 1, :, j_b * K - 1:j_b * K].rearrange("p e a -> p (e a)"),
                                 allow_slow_non_contiguous=True).then_inc(h, 16)
                else:
                    sy.dma_start(bnd2[0:1, :], self.zero128[0:1, 0:E]).then_inc(h, 16)

        def s4_te(te):
            te.wait_ge(BC.get("r_v"), b4["r_v"] + 1)
            h, v = BC.inc("r_mm", 1)
            te.matmul(self.psF[:, 0:E], self.ltri[:, :], cbf[:, :],
                      start=True, stop=True).then_inc(h, 1)
            if t0 > 0:
                te.wait_ge(BC.get("r_v"), b4["r_v"] + 3)
                h, v = BC.inc("r_mm", 1)
                te.matmul(self.psF[:, 0:E], self.ones1[:, :], bndrow[0:1, :],
                          start=True, stop=True).then_inc(h, 1)
        self.hot_block(hc, dict(gp=s4_gp, ve=s4_ve, te=s4_te, sy=s4_sy))

        b5 = {n: BC.v(n) for n in ("r_l0", "r_l1", "r_mm", "r_lge", "r_v", "r_st")}

        def s5_sy(sy):
            for ti in range(NLO):
                bslot = ti % 2
                u = ti * 128
                nb_, col = u // S, u % S
                if ti >= 2:
                    sy.wait_ge(BC.get("r_mm"), b5["r_mm"] + ti - 1)
                h, v = BC.inc("r_l%d" % bslot, 16)
                sy.dma_start(losl[:, bslot, :, :],
                             bass.AP(dd["xnfm"].tensor, nb_ * D * S + col,
                                     [[S, 128], [128 * S, DC], [1, 128]])).then_inc(h, 16)

        def s5_te(te):
            for ti in range(NLO):
                bslot = ti % 2
                te.wait_ge(BC.get("r_l%d" % bslot), b5["r_l%d" % bslot] + 16 * (ti // 2 + 1))
                if ti >= 1:
                    te.wait_ge(BC.get("r_lge"), b5["r_lge"] + ti)
                h, v = BC.inc("r_mm", 1)
                for dc in range(DC):
                    mmi = te.matmul(self.psF[:, 0:E], losl[:, bslot, dc, :], gfm[:, 2, dc, :],
                                    start=(dc == 0), stop=(dc == DC - 1))
                mmi.then_inc(h, 1)

        def s5_ve(ve):
            for ti in range(NLO):
                ve.wait_ge(BC.get("r_mm"), b5["r_mm"] + ti + 1)
                h, v = BC.inc("r_lge", 1)
                ve.tensor_copy(lown_sb[:, ti, :], self.psF[:, 0:E]).then_inc(h, 1)

        def s5_sc(sc):
            sc.wait_ge(BC.get("r_lge"), b5["r_lge"] + NLO)
            h, v = BC.inc("r_st", 16)
            sc.dma_start(bass.AP(dd["lown"].tensor, 0, [[E, 128], [128 * E, NLO], [1, E]]),
                         lown_sb[:, 0:NLO, :]).then_inc(h, 16)
            sc.wait_ge(h, v)
        self.hot_block(hc, dict(sy=s5_sy, te=s5_te, ve=s5_ve, sc=s5_sc))

        b5r = {n: BC.v(n) for n in ("r_d", "r_v", "r_st")}

        def s5r_ve(ve):
            ve.memset(lg2[:, 0:JT, :], 0.0)
            h, v = BC.inc("r_v", 1)
            ve.drain().then_inc(h, 1)

        def s5r_sy(sy):
            sy.wait_ge(BC.get("r_v"), b5r["r_v"] + 1)
            sy.wait_ge(BC.get("r_st"), b5r["r_st"])
            if t0 == 0:
                h, v = BC.inc("r_d", 16)
                sy.dma_start(lg2[:, 0:JT, :],
                             bass.AP(dd["lown"].tensor, 0,
                                     [[JT * E, 128], [E, JT], [1, E]])).then_inc(h, 16)
                sy.wait_ge(h, v)
            else:
                h, v = BC.inc("r_d", 16)
                sy.dma_start(lg2[p_b:p_b + 1, j_b:JT, :],
                             bass.AP(dd["lown"].tensor, 0, [[0, 1], [E, JT - j_b], [1, E]])).then_inc(h, 16)
                h, v = BC.inc("r_d", 16)
                sy.dma_start(lg2[p_b + 1:128, 0:JT, :],
                             bass.AP(dd["lown"].tensor, (JT - j_b) * E,
                                     [[JT * E, 127 - p_b], [E, JT], [1, E]])).then_inc(h, 16)
                sy.wait_ge(h, v)
        self.hot_block(hc, dict(ve=s5r_ve, sy=s5r_sy))

        b6 = {n: BC.v(n) for n in ("r_d", "r_v", "r_a")}

        def s6_ve(ve):
            ve.wait_ge(BC.get("r_d"), b6["r_d"])
            ve.tensor_reduce(mrow[:, 0:JT], lg2[:, 0:JT, :], axis=AX.X, op=OP.max)
            ve.drain()
            ve.tensor_tensor(lg2[:, 0:JT, :], lg2[:, 0:JT, :],
                             mrow[:, 0:JT].unsqueeze(2).broadcast_to([128, JT, E]),
                             op=OP.subtract)
            h, v = BC.inc("r_v", 1)
            ve.drain().then_inc(h, 1)
            ve.wait_ge(BC.get("r_a"), b6["r_a"] + 1)
            for k in range(K):
                ve.tensor_tensor(wtmp[:, 0:JE].rearrange("p (j e) -> p j e", e=E),
                                 eqk[:, k, 0:JT, :],
                                 lg2[:, 0:JT, :], op=OP.mult)
                ve.drain()
                ve.tensor_reduce(wk4[:, k, 0:JT],
                                 wtmp[:, 0:JE].rearrange("p (j e) -> p j e", e=E),
                                 axis=AX.X, op=OP.add)
                ve.drain()
            ve.tensor_tensor(wden[:, 0:JT], wk4[:, 0, 0:JT], wk4[:, 1, 0:JT], op=OP.add)
            ve.drain()
            for k in range(2, K):
                ve.tensor_tensor(wden[:, 0:JT], wden[:, 0:JT], wk4[:, k, 0:JT], op=OP.add)
                ve.drain()
            ve.reciprocal(wden[:, 0:JT], wden[:, 0:JT])
            ve.drain()
            for k in range(K):
                ve.tensor_tensor(wk4[:, k, 0:JT], wk4[:, k, 0:JT], wden[:, 0:JT], op=OP.mult)
            h, v = BC.inc("r_v", 1)
            ve.drain().then_inc(h, 1)

        def s6_sc(sc):
            sc.wait_ge(BC.get("r_v"), b6["r_v"] + 1)
            h, v = BC.inc("r_a", 1)
            sc.activation(lg2[:, 0:JT, :], lg2[:, 0:JT, :], AF.Exp).then_inc(h, 1)
        self.hot_block(hc, dict(ve=s6_ve, sc=s6_sc))

        b7 = {n: BC.v(n) for n in ("r_v", "r_g", "r_sc")}

        def s7_ve(ve):
            ve.wait_ge(BC.get("r_v"), b7["r_v"])
            ve.wait_ge(BC.get("r_g"), b7["r_g"])
            ve.tensor_scalar(ownm[:, 0:JT], tokl[:, 0:JT], 0.0, None, op0=OP.is_ge)
            ve.drain()
            ve.tensor_scalar(tmpa[:, 0:JA], rg[:, 0:JA], float(CAP), None, op0=OP.is_lt)
            ve.drain()
            ve.tensor_tensor(tmpa[:, 0:JA].rearrange("p (j k) -> p j k", k=K),
                             tmpa[:, 0:JA].rearrange("p (j k) -> p j k", k=K),
                             ownm[:, 0:JT].unsqueeze(2).broadcast_to([128, JT, K]),
                             op=OP.mult)
            ve.drain()
            ve.tensor_scalar(tmpb[:, 0:JA], rl[:, 0:JA], 0.0, float(cpad), op0=OP.max, op1=OP.min)
            ve.drain()
            ve.scalar_tensor_tensor(tmpb[:, 0:JA], tmpb[:, 0:JA], -float(cpad),
                                    tmpa[:, 0:JA], op0=OP.add, op1=OP.mult)
            ve.drain()
            ve.tensor_scalar(tmpb[:, 0:JA], tmpb[:, 0:JA], float(cpad), None, op0=OP.add)
            ve.drain()
            ve.scalar_tensor_tensor(gtg[:, 0:JA],
                                    idxa[:, 0:JT, :].rearrange("p j k -> p (j k)"),
                                    float(EST), tmpb[:, 0:JA], op0=OP.mult, op1=OP.add)
            ve.drain()
            ve.tensor_copy(gtgi[:, 0:JA], gtg[:, 0:JA])
            ve.drain()
            for k in range(K):
                ve.tensor_copy(pairs[:, 0:JA, 0].rearrange("p (j k) -> p j k", k=K)[:, :, k],
                               wk4[:, k, 0:JT])
            ve.tensor_copy(pairs[:, 0:JA, 1].rearrange("p (j k) -> p j k", k=K),
                           toklf[:, 0:JT].unsqueeze(2).broadcast_to([128, JT, K]))
            h, v = BC.inc("r_v", 1)
            ve.drain().then_inc(h, 1)

        def s7_gp(g):
            g.wait_ge(BC.get("r_v"), b7["r_v"] + 1)
            for col in range(JA):
                h, v = BC.inc("r_sc", 16)
                g.indirect_dma_start(dd["etab"][:, :],
                                     IndirectOffsetOnAxis(gtgi[:, col:col + 1], 0),
                                     pairs[:, col, :], None).then_inc(h, 16)
                g.wait_ge(h, v)
        self.hot_block(hc, dict(ve=s7_ve, gp=s7_gp))
    st.close()


Builder.ph_route = _ph_route


def _ph_moe(self):
    import contextlib
    nc, c, dd = self.nc, self.cfg, self.dram
    D, E, I, NT, S = c["D"], c["E"], c["I"], c["NT"], c["S"]
    DC, IC = self.DC, self.IC
    EST = self.ESTMAX
    ESTC = EST // 128
    R128 = self.ETROWS // 128
    S_ = self.sems
    T = self.TOWN
    CPM = self.CPADMAX

    st = contextlib.ExitStack()
    sb = lambda n, sh, dt=F32: st.enter_context(nc.sbuf_tensor(n, sh, dt))
    etw = sb("etw", [128, R128, 2], F32)
    toki = sb("toki", [128, R128], I32)
    tsci = sb("tsci", [128, R128], I32)
    sef = sb("sef", [128, R128], F32)
    mmask = sb("mmask", [128, R128], F32)
    ttmp = sb("ttmp", [128, R128], F32)
    wun = sb("wun", [128, 2, DC * I], BF16)
    xgr = sb("xgr", [128, 2, D], BF16)
    xgfm = sb("xgfm", [128, DC, CPM], BF16)
    gsil = sb("gsil", [128, IC, CPM], BF16)
    msb = sb("msb", [128, IC, CPM], BF16)
    eout = sb("eout", [128, 2, D], F32)

    for hc, hcfg in sorted(c["HOT"].items()):
        cpad = hcfg["cpad"]
        NST = cpad // 128
        NTC = ceil_div(cpad, NT)
        BC = BranchCnt(S_)
        b0 = {n: BC.v(n) for n in ("m_et", "m_v", "m_g")}

        def et_sy(sy):
            h, v = BC.inc("m_et", 16)
            sy.dma_start(etw[:, :, :],
                         dd["etab"].rearrange("(a p) b -> p a b", p=128)).then_inc(h, 16)
            sy.wait_ge(h, v)

        def et_gp(g):
            h, v = BC.inc("m_g", 1)
            g.iota(ttmp[:, 0:R128].bitcast(I32), [[0, E], [128, ESTC]], base=0,
                   channel_multiplier=1)
            g.drain().then_inc(h, 1)

        def et_ve(ve):
            ve.wait_ge(BC.get("m_et"), b0["m_et"] + 16)
            ve.wait_ge(BC.get("m_g"), b0["m_g"] + 1)
            ve.tensor_copy(sef[:, 0:R128], ttmp[:, 0:R128].bitcast(I32))
            ve.drain()
            ve.tensor_copy(toki[:, :], etw[:, :, 1])
            ve.drain()
            ve.tensor_scalar(mmask[:, :], etw[:, :, 0], 0.0, None, op0=OP.is_gt)
            ve.drain()
            ve.tensor_scalar(sef[:, :], sef[:, :], float(T), None, op0=OP.add)
            ve.drain()
            ve.tensor_tensor(ttmp[:, :], etw[:, :, 1], sef[:, :], op=OP.subtract)
            ve.drain()
            ve.tensor_tensor(ttmp[:, :], ttmp[:, :], mmask[:, :], op=OP.mult)
            ve.drain()
            ve.tensor_tensor(ttmp[:, :], ttmp[:, :], sef[:, :], op=OP.add)
            ve.drain()
            ve.tensor_copy(tsci[:, :], ttmp[:, :])
            h, v = BC.inc("m_v", 1)
            ve.drain().then_inc(h, 1)
        self.hot_block(hc, dict(sy=et_sy, gp=et_gp, ve=et_ve))

        for e in range(E):
            be = {n: BC.v(n) for n in
                  ("m_wl0", "m_wl1", "m_gd0", "m_gd1", "m_tr", "m_tf", "m_mm", "m_ef", "m_sl", "m_sd")}
            ecol0 = e * ESTC

            def ex_gp(g, e=e, be=be, ecol0=ecol0):
                for wt, slot in ((dd["ew_g"], 0), (dd["ew_u"], 1)):
                    if e > 0:
                        g.wait_ge(BC.get("m_mm"), be["m_mm"])
                    for rr in range(DC):
                        h, v = BC.inc("m_wl%d" % slot, 16)
                        g.dma_start(wun[:, slot, :].rearrange("p (a b) -> p a b", a=DC)[:, rr, :],
                                    wt[e, rr * 128:(rr + 1) * 128, :]).then_inc(h, 16)
                g.wait_ge(BC.get("m_v"), b0["m_v"] + 1)
                for stile in range(NST):
                    slot = stile % 2
                    if stile >= 2:
                        g.wait_ge(BC.get("m_tr"), be["m_tr"] + (stile - 1) * DC)
                    h, v = BC.inc("m_gd%d" % slot, 16)
                    g.indirect_dma_start(xgr[:, slot, :], None, dd["xn"][:, :],
                                         IndirectOffsetOnAxis(toki[:, ecol0 + stile:ecol0 + stile + 1], 0)
                                         ).then_inc(h, 16)
                g.wait_ge(BC.get("m_mm"), be["m_mm"] + IC * NTC)
                for rr in range(IC):
                    h, v = BC.inc("m_wl0", 16)
                    g.dma_start(wun[:, 0, :].rearrange("p (a b) -> p a b", a=IC)[:, rr, :],
                                dd["ew_d"][e, rr * 128:(rr + 1) * 128, :]).then_inc(h, 16)
                for stile in range(NST):
                    slot = stile % 2
                    g.wait_ge(BC.get("m_ef"),
                              be["m_ef"] + IC * NTC + (stile + 1) * (D // NT))
                    h, v = BC.inc("m_sd", 16)
                    g.indirect_dma_start(dd["moe"][:, :],
                                         IndirectOffsetOnAxis(tsci[:, ecol0 + stile:ecol0 + stile + 1], 0),
                                         eout[:, slot, :], None,
                                         compute_op=OP.add).then_inc(h, 16)
                g.wait_ge(BC.get("m_sd"), BC.v("m_sd"))

            def ex_te(te, e=e, be=be):
                for stile in range(NST):
                    slot = stile % 2
                    te.wait_ge(BC.get("m_gd%d" % slot), be["m_gd%d" % slot] + 16 * (stile // 2 + 1))
                    for dc in range(DC):
                        k = stile * DC + dc
                        t = k % 2
                        if k >= 2:
                            te.wait_ge(BC.get("m_tf"), be["m_tf"] + k - 1)
                        h, v = BC.inc("m_tr", 1)
                        te.transpose(self.psT[t][:, :], xgr[:, slot, dc * 128:(dc + 1) * 128],
                                     self.ident[:]).then_inc(h, 1)
                te.wait_ge(BC.get("m_wl0"), be["m_wl0"] + 16 * DC)
                te.wait_ge(BC.get("m_tf"), be["m_tf"] + NST * DC)
                k = 0
                for ic in range(IC):
                    for tc in range(NTC):
                        n = min(NT, cpad - tc * NT)
                        ps = [self.psA, self.psB][k % 2]
                        if k >= 2:
                            te.wait_ge(BC.get("m_sl"), be["m_sl"] + k - 1)
                        h, v = BC.inc("m_mm", 1)
                        for dc in range(DC):
                            mmi = te.matmul(ps[:, 0:n],
                                            wun[:, 0, :].rearrange("p (a b) -> p a b", a=DC)[:, dc, ic * 128:(ic + 1) * 128],
                                            xgfm[:, dc, tc * NT:tc * NT + n],
                                            start=(dc == 0), stop=(dc == DC - 1))
                        mmi.then_inc(h, 1)
                        k += 1
                te.wait_ge(BC.get("m_wl1"), be["m_wl1"] + 16 * DC)
                te.wait_ge(BC.get("m_sl"), be["m_sl"] + IC * NTC)
                k2 = 0
                for ic in range(IC):
                    for tc in range(NTC):
                        n = min(NT, cpad - tc * NT)
                        ps = [self.psA, self.psB][(k + k2) % 2]
                        if k2 >= 2:
                            te.wait_ge(BC.get("m_ef"), be["m_ef"] + k2 - 1)
                        h, v = BC.inc("m_mm", 1)
                        for dc in range(DC):
                            mmi = te.matmul(ps[:, 0:n],
                                            wun[:, 1, :].rearrange("p (a b) -> p a b", a=DC)[:, dc, ic * 128:(ic + 1) * 128],
                                            xgfm[:, dc, tc * NT:tc * NT + n],
                                            start=(dc == 0), stop=(dc == DC - 1))
                        mmi.then_inc(h, 1)
                        k2 += 1
                te.wait_ge(BC.get("m_wl0"), be["m_wl0"] + 16 * DC + 16 * IC)
                te.wait_ge(BC.get("m_ef"), be["m_ef"] + IC * NTC)
                k3 = 0
                for stile in range(NST):
                    for jc in range(D // NT):
                        ps = [self.psA, self.psB][(k + k2 + k3) % 2]
                        if k3 >= 2:
                            te.wait_ge(BC.get("m_ef"), be["m_ef"] + IC * NTC + k3 - 1)
                        h, v = BC.inc("m_mm", 1)
                        for ic in range(IC):
                            mmi = te.matmul(ps[:, 0:NT],
                                            msb[:, ic, stile * 128:(stile + 1) * 128],
                                            wun[:, 0, :].rearrange("p (a b) -> p a b", a=IC)[:, ic, jc * NT:(jc + 1) * NT],
                                            start=(ic == 0), stop=(ic == IC - 1))
                        mmi.then_inc(h, 1)
                        k3 += 1

            def ex_sc(sc, e=e, be=be):
                k = 0
                for ic in range(IC):
                    for tc in range(NTC):
                        n = min(NT, cpad - tc * NT)
                        ps = [self.psA, self.psB][k % 2]
                        sc.wait_ge(BC.get("m_mm"), be["m_mm"] + k + 1)
                        h, v = BC.inc("m_sl", 1)
                        sc.activation(gsil[:, ic, tc * NT:tc * NT + n], ps[:, 0:n],
                                      AF.Silu).then_inc(h, 1)
                        k += 1

            def ex_ve(ve, e=e, be=be):
                for stile in range(NST):
                    for dc in range(DC):
                        kk = stile * DC + dc
                        t = kk % 2
                        ve.wait_ge(BC.get("m_tr"), be["m_tr"] + kk + 1)
                        h, v = BC.inc("m_tf", 1)
                        ve.tensor_copy(xgfm[:, dc, stile * 128:(stile + 1) * 128],
                                       self.psT[t][:, :]).then_inc(h, 1)
                j = 0
                for ic in range(IC):
                    for tc in range(NTC):
                        n = min(NT, cpad - tc * NT)
                        ps = [self.psA, self.psB][(IC * NTC + j) % 2]
                        ve.wait_ge(BC.get("m_mm"), be["m_mm"] + IC * NTC + j + 1)
                        ve.wait_ge(BC.get("m_sl"), be["m_sl"] + j + 1)
                        h, v = BC.inc("m_ef", 1)
                        ve.tensor_tensor(msb[:, ic, tc * NT:tc * NT + n], ps[:, 0:n],
                                         gsil[:, ic, tc * NT:tc * NT + n], op=OP.mult).then_inc(h, 1)
                        j += 1
                j3 = 0
                for stile in range(NST):
                    slot = stile % 2
                    if stile >= 2:
                        ve.wait_ge(BC.get("m_sd"), be["m_sd"] + 16 * (stile - 1))
                    for jc in range(D // NT):
                        ps = [self.psA, self.psB][(2 * IC * NTC + j3) % 2]
                        ve.wait_ge(BC.get("m_mm"), be["m_mm"] + 2 * IC * NTC + j3 + 1)
                        h, v = BC.inc("m_ef", 1)
                        ve.tensor_scalar(eout[:, slot, jc * NT:(jc + 1) * NT], ps[:, 0:NT],
                                         etw[:, e * ESTC + stile, 0:1].opt(), None,
                                         op0=OP.mult).then_inc(h, 1)
                        j3 += 1
            self.hot_block(hc, dict(gp=ex_gp, te=ex_te, sc=ex_sc, ve=ex_ve))
    st.close()


Builder.ph_moe = _ph_moe



_PROGRAM_CACHE = {}


def kernel(L0_attn_x, L0_attn_wq, L0_attn_bq, L0_attn_wk, L0_attn_bk,
           L0_attn_wv, L0_attn_bv, L0_attn_wo, L0_attn_bo,
           L0_ffn_gate_w, L0_ffn_w_gate, L0_ffn_w_up, L0_ffn_w_down,
           L0_ffn_sw_gate, L0_ffn_sw_up, L0_ffn_sw_down):
    cfg = full_cfg()
    NC = cfg["NCORES"]
    NB = cfg["NB"]
    B, S, D = cfg["B"], cfg["S"], cfg["D"]
    key = "full"
    if key not in _PROGRAM_CACHE:
        _PROGRAM_CACHE[key] = build_program(cfg)
    nc, b = _PROGRAM_CACHE[key]
    T = b.TOWN

    asf = lambda a: np.ascontiguousarray(np.asarray(a, dtype=np.float32))
    x = asf(L0_attn_x).reshape(B * S, D)
    w_qkvo = np.stack([asf(L0_attn_wq), asf(L0_attn_wk), asf(L0_attn_wv), asf(L0_attn_wo)])
    b_qkvo = np.stack([asf(L0_attn_bq), asf(L0_attn_bk), asf(L0_attn_bv), asf(L0_attn_bo)])
    consts = host_constants_inmap(cfg)
    xpre = np.ascontiguousarray(x[0:b.TPRE])
    common = dict(
        w_qkvo=w_qkvo, b_qkvo=b_qkvo, gate_w=asf(L0_ffn_gate_w),
        sw_g=asf(L0_ffn_sw_gate), sw_u=asf(L0_ffn_sw_up), sw_d=asf(L0_ffn_sw_down),
        ew_g=asf(L0_ffn_w_gate), ew_u=asf(L0_ffn_w_up), ew_d=asf(L0_ffn_w_down),
        x_pre=xpre, **consts,
    )
    in_maps = []
    for c_ in range(NC):
        im = dict(common)
        im["cid"] = np.array([[c_]], np.int32)
        im["x_own"] = np.ascontiguousarray(x[c_ * T:(c_ + 1) * T])
        in_maps.append(im)
    res = run_bass_kernel_spmd(nc, in_maps, core_ids=list(range(NC)))
    out = np.concatenate([res.results[c_]["out"] for c_ in range(NC)], 0)
    return out.reshape(B, S, D)


# version 02
# speedup vs baseline: 1.2851x; 1.2851x over previous
import math
import numpy as np
import concourse.bass as bass
import concourse.mybir as mybir
from concourse.bass import IndirectOffsetOnAxis
from concourse.bass_utils import run_bass_kernel_spmd

F32 = mybir.dt.float32
BF16 = mybir.dt.bfloat16
I32 = mybir.dt.int32
I16 = mybir.dt.int16
AF = mybir.ActivationFunctionType
OP = mybir.AluOpType
AX = mybir.AxisListType


def full_cfg():
    return dict(
        D=2048, HD=128, NH=16, S=2048, B=16,
        E=16, TOPK=4, CAP=1024, I=1408, ISH=2816,
        NCORES=8, NB=2,
        TH=5120,
        QT=512,
        KG=512,
        NT=512,
        HOT={0: dict(t0=0, cpad=640, hlen=2048),
             1: dict(t0=2048, cpad=640, hlen=2048),
             2: dict(t0=4096, cpad=256, hlen=1024)},
        LN_EPS=1e-5,
    )


def mini_cfg():
    return dict(
        D=256, HD=128, NH=2, S=256, B=4,
        E=16, TOPK=4, CAP=24, I=256, ISH=384,
        NCORES=2, NB=2,
        TH=384,
        QT=128, KG=256, NT=128,
        HOT={0: dict(t0=0, cpad=32, hlen=256), 1: dict(t0=256, cpad=16, hlen=384)},
        LN_EPS=1e-5,
    )



def build_constants(cfg):
    QT = cfg["QT"]
    c = {}
    c["identity"] = np.eye(128, dtype=np.float32)
    rs = list(range(-(cfg["S"] - QT) // 128, QT // 128))
    ki = np.arange(128)[:, None]
    qj = np.arange(QT)[None, :]
    bt = []
    for r in rs:
        v = (ki - qj + 128 * r).astype(np.float32)
        v = np.where(v <= 0, v, -1e30).astype(np.float32)
        bt.append(v)
    c["btiles"] = np.stack(bt, 0)
    c["brs"] = rs
    c["ltri"] = np.triu(np.ones((128, 128), np.float32), 1)
    c["ones128"] = np.ones((128, 1), np.float32)
    c["iota16"] = np.arange(cfg["E"], dtype=np.float32)[None, :]
    H = cfg["NH"]
    start = 2.0 ** (-8.0 / H)
    c["slopes"] = np.array([start * start ** i for i in range(H)], np.float32)
    return c



class Sems:

    def __init__(self, nc, ctx):
        self.nc = nc
        self.ctx = ctx
        self.val = {}
        self.h = {}

    def get(self, name):
        if name not in self.h:
            self.h[name] = self.ctx.enter_context(self.nc.semaphore(name))
            self.val[name] = 0
        return self.h[name]

    def inc(self, name, by):
        self.get(name)
        self.val[name] += by
        return self.h[name], self.val[name]


def ceil_div(a, b):
    return (a + b - 1) // b



class Builder:
    def __init__(self, nc, cfg, stack):
        self.nc = nc
        self.cfg = cfg
        self.stack = stack
        self.sems = Sems(nc, stack)
        c = cfg
        self.TOWN = c["NB"] * c["S"]
        self.TPRE = max((h["t0"] for h in c["HOT"].values()), default=0)
        self.DC = c["D"] // 128
        self.IC = c["I"] // 128
        self.ISC = c["ISH"] // 128
        self.CPADMAX = max((h["cpad"] for h in c["HOT"].values()), default=128)
        self.ESTMAX = ceil_div(self.CPADMAX + 1, 128) * 128
        self.ETROWS = c["E"] * self.ESTMAX
        self.dram = {}
        self.declare_io()
        self.cid_regs = {}
        self._alloc_psum()

    def sb(self, name, shape, dtype):
        return self.stack.enter_context(self.nc.sbuf_tensor(name, shape, dtype))

    def ps(self, name, shape, dtype=F32):
        return self.stack.enter_context(self.nc.psum_tensor(name, shape, dtype))

    def declare_io(self):
        nc, c = self.nc, self.cfg
        D, E, I, ISH, S, NB = c["D"], c["E"], c["I"], c["ISH"], c["S"], c["NB"]
        QT = c["QT"]
        T = self.TOWN
        dd = self.dram
        ein = lambda n, sh, dt=F32: nc.dram_tensor(n, sh, dt, kind="ExternalInput").ap()
        intl = lambda n, sh, dt: nc.dram_tensor(n, sh, dt, kind="Internal").ap()
        dd["cid"] = ein("cid", [1, 1], I32)
        dd["x_own"] = ein("x_own", [T, D])
        if self.TPRE:
            dd["x_pre"] = ein("x_pre", [self.TPRE, D])
        dd["w_qkvo"] = ein("w_qkvo", [4, D, D])
        dd["b_qkvo"] = ein("b_qkvo", [4, D])
        dd["gate_w"] = ein("gate_w", [E, D])
        dd["sw_g"] = ein("sw_g", [D, ISH])
        dd["sw_u"] = ein("sw_u", [D, ISH])
        dd["sw_d"] = ein("sw_d", [ISH, D])
        dd["ew_g"] = ein("ew_g", [E, D, I])
        dd["ew_u"] = ein("ew_u", [E, D, I])
        dd["ew_d"] = ein("ew_d", [E, I, D])
        nr = c["S"] // 128
        dd["c_bt"] = ein("c_bt", [nr, 128, QT])
        dd["c_id"] = ein("c_id", [128, 128])
        dd["c_ltri"] = ein("c_ltri", [128, 128])
        dd["c_iota16"] = ein("c_iota16", [128, E])
        dd["c_slopes"] = ein("c_slopes", [1, c["NH"]])
        dd["out"] = nc.dram_tensor("out", [T, D], F32, kind="ExternalOutput").ap()
        dd["xn"] = intl("xn_h", [T, D], BF16)
        dd["xnfm"] = intl("xnfm_h", [NB, D, S], BF16)
        dd["qf"] = intl("qf_h", [NB, D, S], BF16)
        dd["kf"] = intl("kf_h", [NB, D, S], BF16)
        dd["vt"] = intl("vt_h", [NB, S, D], BF16)
        dd["aof"] = intl("aof_h", [NB, D, S], BF16)
        dd["oattn"] = intl("oattn_h", [T, D], BF16)
        dd["gsh"] = intl("gsh_h", [NB, ISH, S], BF16)
        dd["msh"] = intl("msh_h", [NB, ISH, S], BF16)
        dd["lown"] = intl("lown_h", [T, E], F32)
        dd["moe"] = intl("moe_h", [T + self.CPADMAX, D], F32)
        dd["etab"] = intl("etab_h", [self.ETROWS, 2], F32)

    def barrier(self, tag):
        return

    def ph_setup(self):
        nc, c, dd = self.nc, self.cfg, self.dram
        D, E, NH = c["D"], c["E"], c["NH"]
        DC = self.DC
        NR = c["S"] // 128
        QT = c["QT"]
        self.cid_sb = self.sb("cid_sb", [1, 1], I32)
        self.ident = self.sb("ident", [128, 128], BF16)
        self.ltri = self.sb("ltri", [128, 128], BF16)
        self.iota16 = self.sb("iota16", [128, E], F32)
        self.slopes = self.sb("slopes", [1, NH], F32)
        self.bias_sb = self.sb("bias_sb", [128, 4, DC], F32)
        self.bqs_sb = self.sb("bqs_sb", [128, DC], F32)
        self.ones1 = self.sb("ones1", [1, 128], F32)
        self.zero128 = self.sb("zero128", [128, 2048 if c["D"] >= 2048 else c["D"]], F32)
        with nc.Block() as bl:
            @bl.sync
            def _(sy):
                h, v = self.sems.inc("su_d", 16); sy.dma_start(self.cid_sb[:], dd["cid"][:]).then_inc(h, 16)
                h, v = self.sems.inc("su_d", 16); sy.dma_start(self.iota16[:], dd["c_iota16"][:]).then_inc(h, 16)
                h, v = self.sems.inc("su_d", 16); sy.dma_start(self.slopes[:], dd["c_slopes"][:]).then_inc(h, 16)
                h, v = self.sems.inc("su_d", 16)
                sy.dma_start(self.bias_sb[:], bass.AP(dd["b_qkvo"].tensor, 0,
                             [[1, 128], [D, 4], [128, DC]]),
                             allow_slow_non_contiguous=True).then_inc(h, 16)

            @bl.gpsimd
            def _(g):
                h, v = self.sems.inc("su_g", 16); g.dma_start(self.ident[:], dd["c_id"][:]).then_inc(h, 16)
                h, v = self.sems.inc("su_g", 16); g.dma_start(self.ltri[:], dd["c_ltri"][:]).then_inc(h, 16)
            @bl.vector
            def _(ve):
                ve.wait_ge(self.sems.get("su_d"), self.sems.val["su_d"])
                ve.memset(self.ones1[:], 1.0)
                ve.memset(self.zero128[:], 0.0)
                ve.tensor_scalar(self.bqs_sb[:], self.bias_sb[:, 0, :],
                                 1.0 / math.sqrt(c["HD"]), None, op0=OP.mult)
                h, v = self.sems.inc("su_v", 1)
                ve.drain().then_inc(h, 1)
        with nc.Block() as bl:
            def mk(engname, dec):
                def body(eng):
                    eng.wait_ge(self.sems.get("su_d"), self.sems.val["su_d"])
                    reg = eng.alloc_register("cidreg_" + engname)
                    eng.load(reg, self.cid_sb[0:1, 0:1])
                    self.cid_regs[engname] = reg
                dec(body)
            mk("sy", bl.sync); mk("sc", bl.scalar); mk("ve", bl.vector)
            mk("te", bl.tensor); mk("gp", bl.gpsimd)
        with nc.Block() as bl:
            @bl.gpsimd
            def _(g):
                g.wait_ge(self.sems.get("su_g"), self.sems.val["su_g"])
        self.barrier("setup")

    def ph_ln(self):
        import contextlib
        nc, c, dd = self.nc, self.cfg, self.dram
        D = c["D"]
        NTILE = self.TOWN // 128
        st_ln = contextlib.ExitStack()
        lsb = lambda n, sh, dt: st_ln.enter_context(nc.sbuf_tensor(n, sh, dt))
        xa = lsb("ln_x", [128, 2, D], F32)
        xn = lsb("ln_xn", [128, 2, D], BF16)
        st6 = lsb("ln_st", [128, 2, ceil_div(D, 512) * 6], F32)
        mv = lsb("ln_mv", [128, 2, 2], F32)
        sd = lsb("ln_sd", [128, 2, 1], F32)
        inv = lsb("ln_inv", [128, 2, 1], F32)
        ngm = lsb("ln_ngm", [128, 2, 1], F32)
        nst = ceil_div(D, 512)
        S = self.sems
        with nc.Block() as bl:
            @bl.sync
            def _(sy):
                for i in range(NTILE):
                    b = i % 2
                    if i >= 2:
                        sy.wait_ge(S.get("ln_f"), i - 1)
                    h, v = S.inc("ln_l%d" % b, 16)
                    sy.dma_start(xa[:, b, :], dd["x_own"][i * 128:(i + 1) * 128, :]).then_inc(h, 16)
            @bl.vector
            def _(ve):
                for i in range(NTILE):
                    b = i % 2
                    ve.wait_ge(S.get("ln_l%d" % b), 16 * (i // 2 + 1))
                    for j in range(nst):
                        n = min(512, D - j * 512)
                        ve.bn_stats(st6[:, b, j * 6:(j + 1) * 6], xa[:, b, j * 512:j * 512 + n])
                    ve.drain()
                    ve.bn_aggr(mv[:, b, :], st6[:, b, :])
                    ve.drain()
                    ve.tensor_scalar(mv[:, b, 1:2], mv[:, b, 1:2], c["LN_EPS"], None, op0=OP.add)
                    h, v = S.inc("ln_s", 1)
                    ve.drain().then_inc(h, 1)
                    ve.wait_ge(S.get("ln_sd"), i + 1)
                    if i >= 2:
                        ve.wait_ge(S.get("ln_st%d" % b), 16 * ((i - 2) // 2 + 1))
                    ve.reciprocal(inv[:, b, :], sd[:, b, :])
                    ve.drain()
                    ve.tensor_scalar(ngm[:, b, :], mv[:, b, 0:1], inv[:, b, :].opt(), -1.0,
                                     op0=OP.mult, op1=OP.mult)
                    ve.drain()
                    ve.tensor_scalar(xn[:, b, :], xa[:, b, :], inv[:, b, :].opt(),
                                     ngm[:, b, :].opt(), op0=OP.mult, op1=OP.add)
                    h, v = S.inc("ln_c", 1)
                    ve.drain().then_inc(h, 1)
                    h, v = S.inc("ln_f", 1)
                    ve.nop().then_inc(h, 1)
            @bl.scalar
            def _(sc):
                for i in range(NTILE):
                    b = i % 2
                    sc.wait_ge(S.get("ln_s"), i + 1)
                    sc.activation(sd[:, b, :], mv[:, b, 1:2], AF.Sqrt)
                    h, v = S.inc("ln_sd", 1)
                    sc.drain().then_inc(h, 1)
                    sc.wait_ge(S.get("ln_c"), i + 1)
                    h, v = S.inc("ln_st%d" % b, 16)
                    sc.dma_start(dd["xn"][i * 128:(i + 1) * 128, :], xn[:, b, :]).then_inc(h, 16)
                sc.wait_ge(S.get("ln_st0"), S.val["ln_st0"])
                sc.wait_ge(S.get("ln_st1"), S.val["ln_st1"])
            @bl.tensor
            def _(te):
                pass
            @bl.gpsimd
            def _(g):
                pass
        st_ln.close()
        self.barrier("ln")

def build_program(cfg, phases=("setup", "ln", "att", "route", "moe", "shared")):
    import contextlib
    nc = bass.Bass("TRN2", target_bir_lowering=False)
    stack = contextlib.ExitStack()
    b = Builder(nc, cfg, stack)
    with stack:
        if "setup" in phases:
            b.ph_setup()
            b.ph_prezero()
        if "ln" in phases:
            b.ph_ln()
        if "att" in phases:
            b.ph_att()
        if "route" in phases:
            b.ph_route()
        if "moe" in phases:
            b.ph_moe()
        if "shared" in phases:
            b.ph_shared()
        b.finalize()
    return nc, b


def host_constants_inmap(cfg):
    c = build_constants(cfg)
    return {
        "c_bt": c["btiles"],
        "c_id": c["identity"],
        "c_ltri": c["ltri"],
        "c_iota16": np.tile(c["iota16"], (128, 1)).astype(np.float32),
        "c_slopes": c["slopes"][None, :],
    }


def _finalize_stub(self):
    nc = self.nc
    with nc.Block() as bl:
        @bl.sync
        def _(sy):
            for q_ in range(4):
                n = "out_d%d" % q_
                if self.sems.val.get(n):
                    sy.wait_ge(self.sems.get(n), self.sems.val[n])


Builder.finalize = _finalize_stub


def _alloc_psum(self):
    if hasattr(self, "psA"):
        return
    self.psA = self.ps("psA", [128, 512])
    self.psB = self.ps("psB", [128, 512])
    self.psC = self.ps("psC", [128, 512])
    self.psE = self.ps("psE", [128, 512])
    self.psD = self.ps("psD", [128, 512])
    self.psF = self.ps("psF", [128, 512])
    self.psT = [self.ps("psT0", [128, 128], BF16), self.ps("psT1", [128, 128], BF16)]


def _ph_att(self):
    import contextlib
    nc, c, dd = self.nc, self.cfg, self.dram
    D, S, NB, NH, HD = c["D"], c["S"], c["NB"], c["NH"], c["HD"]
    QT, KG, NT = c["QT"], c["KG"], c["NT"]
    DC = self.DC
    NQT = QT // 128
    S_ = self.sems
    self._alloc_psum()
    rs0 = -(S - QT) // 128
    start = 2.0 ** (-8.0 / NH)
    slopes = [start * start ** i for i in range(NH)]
    S_ = self.sems

    NR = S // 128
    st_att = contextlib.ExitStack()
    self.btiles = st_att.enter_context(nc.sbuf_tensor("btiles", [128, NR, QT], F32))
    self.bvrep = st_att.enter_context(nc.sbuf_tensor("bvrep", [128, D], F32))
    self.borep = st_att.enter_context(nc.sbuf_tensor("borep", [128, D], F32))
    st_brow = contextlib.ExitStack()
    self.brow = st_brow.enter_context(nc.sbuf_tensor("brow", [1, 2, D], F32))
    with nc.Block() as bl:
        @bl.sync
        def _(sy):
            h, v = S_.inc("su_d", 16)
            sy.dma_start(self.btiles[:], dd["c_bt"].rearrange("n p q -> p n q")).then_inc(h, 16)
            h, v = S_.inc("su_d", 16)
            sy.dma_start(self.brow[:], bass.AP(dd["b_qkvo"].tensor, 2 * D,
                         [[2 * D, 1], [D, 2], [1, D]])).then_inc(h, 16)
            sy.wait_ge(h, v)
    base_mm = S_.val.get("su_mm", 0)
    base_v2 = S_.val.get("su_ve2", 0)
    with nc.Block() as bl:
        @bl.tensor
        def _(te):
            te.wait_ge(S_.get("su_d"), S_.val["su_d"])
            k = 0
            for w, dst in ((0, self.bvrep), (1, self.borep)):
                for jc in range(ceil_div(D, 512)):
                    n = min(512, D - jc * 512)
                    k += 1
                    if k > 1:
                        te.wait_ge(S_.get("su_ve2"), base_v2 + k - 1)
                    h, v = S_.inc("su_mm", 1)
                    te.matmul(self.psF[:, 0:n], self.ones1[:],
                              self.brow[:, w, jc * 512:jc * 512 + n],
                              start=True, stop=True).then_inc(h, 1)
        @bl.vector
        def _(ve):
            k = 0
            for w, dst in ((0, self.bvrep), (1, self.borep)):
                for jc in range(ceil_div(D, 512)):
                    n = min(512, D - jc * 512)
                    k += 1
                    ve.wait_ge(S_.get("su_mm"), base_mm + k)
                    h, v = S_.inc("su_ve2", 1)
                    ve.tensor_copy(dst[:, jc * 512:jc * 512 + n], self.psF[:, 0:n]).then_inc(h, 1)
    st_brow.close()

    for nb in range(NB):
        tag = "a%d" % nb
        st_b = contextlib.ExitStack()
        xnfm = st_b.enter_context(nc.sbuf_tensor("xnfm%d" % nb, [128, DC, S], BF16))
        xt = st_b.enter_context(nc.sbuf_tensor("xt%d" % nb, [128, 2, D], BF16))
        BA = {n: S_.val.get(n, 0) for n in ("t_xl0", "t_xl1", "t_tr", "t_trf", "t_fmst")}
        with nc.Block() as bl:
            @bl.sync
            def _(sy):
                for i in range(S // 128):
                    b = i % 2
                    if i >= 2:
                        sy.wait_ge(S_.get("t_trf"), BA["t_trf"] + (i - 2) * DC + DC)
                    h, v = S_.inc("t_xl%d" % b, 16)
                    sy.dma_start(xt[:, b, :], dd["xn"][nb * S + i * 128: nb * S + (i + 1) * 128, :]).then_inc(h, 16)
            @bl.tensor
            def _(te):
                for i in range(S // 128):
                    b = i % 2
                    te.wait_ge(S_.get("t_xl%d" % b), BA["t_xl%d" % b] + 16 * (i // 2 + 1))
                    for dc in range(DC):
                        t = (i * DC + dc) % 2
                        if i * DC + dc >= 2:
                            te.wait_ge(S_.get("t_trf"), BA["t_trf"] + i * DC + dc - 2 + 1)
                        h, v = S_.inc("t_tr", 1)
                        te.transpose(self.psT[t][:, :], xt[:, b, dc * 128:(dc + 1) * 128], self.ident[:]).then_inc(h, 1)
            @bl.vector
            def _(ve):
                for i in range(S // 128):
                    for dc in range(DC):
                        t = (i * DC + dc) % 2
                        ve.wait_ge(S_.get("t_tr"), BA["t_tr"] + i * DC + dc + 1)
                        h, v = S_.inc("t_trf", 1)
                        ve.tensor_copy(xnfm[:, dc, i * 128:(i + 1) * 128], self.psT[t][:, :]).then_inc(h, 1)
            @bl.scalar
            def _(sc):
                sc.wait_ge(S_.get("t_trf"), BA["t_trf"] + (S // 128) * DC)
                for dc in range(DC):
                    h, v = S_.inc("t_fmst", 16)
                    sc.dma_start(dd["xnfm"][nb, dc * 128:(dc + 1) * 128, :], xnfm[:, dc, :]).then_inc(h, 16)
                sc.wait_ge(S_.get("t_fmst"), S_.val["t_fmst"])
            @bl.gpsimd
            def _(g):
                pass

        w_sb = st_b.enter_context(nc.sbuf_tensor("wsb%d" % nb, [128, DC, D], BF16))
        ev = st_b.enter_context(nc.sbuf_tensor("ev%d" % nb, [128, 4, NT], BF16))
        projs = [("q", 0, dd["qf"]), ("k", 1, dd["kf"]), ("v", 2, dd["vt"])]
        prev_mm = None
        for pname, wi, dst in projs:
            BP = {n: S_.val.get(n, 0) for n in
                  ("p_wl", "p_mm", "p_ef", "p_st0", "p_st1", "p_st2", "p_st3")}
            with nc.Block() as bl:
                pmm = prev_mm
                @bl.gpsimd
                def _(g, pmm=pmm):
                    if pmm is not None:
                        g.wait_ge(S_.get(pmm), S_.val[pmm])
                    for dc in range(DC):
                        h, v = S_.inc("p_wl", 16)
                        g.dma_start(w_sb[:, dc, :], dd["w_qkvo"][wi, dc * 128:(dc + 1) * 128, :]).then_inc(h, 16)
                @bl.tensor
                def _(te):
                    te.wait_ge(S_.get("p_wl"), BP["p_wl"] + 16 * DC)
                    nmm = 0
                    if pname in ("q", "k"):
                        for oc in range(DC):
                            for tg in range(S // NT):
                                pb = (oc * (S // NT) + tg) % 2
                                ps = [self.psA, self.psB][pb]
                                if oc * (S // NT) + tg >= 2:
                                    te.wait_ge(S_.get("p_ef"), BP["p_ef"] + oc * (S // NT) + tg - 2 + 1)
                                h, v = S_.inc("p_mm", 1)
                                for dc in range(DC):
                                    mmi = te.matmul(ps[:, 0:NT], w_sb[:, dc, oc * 128:(oc + 1) * 128],
                                                    xnfm[:, dc, tg * NT:(tg + 1) * NT],
                                                    start=(dc == 0), stop=(dc == DC - 1))
                                    if dc == DC - 1:
                                        mmi.then_inc(h, 1)
                    else:
                        for tt in range(S // 128):
                            for jc in range(D // NT):
                                pb = (tt * (D // NT) + jc) % 2
                                ps = [self.psA, self.psB][pb]
                                if tt * (D // NT) + jc >= 2:
                                    te.wait_ge(S_.get("p_ef"), BP["p_ef"] + tt * (D // NT) + jc - 2 + 1)
                                h, v = S_.inc("p_mm", 1)
                                for dc in range(DC):
                                    mmi = te.matmul(ps[:, 0:NT], xnfm[:, dc, tt * 128:(tt + 1) * 128],
                                                    w_sb[:, dc, jc * NT:(jc + 1) * NT],
                                                    start=(dc == 0), stop=(dc == DC - 1))
                                    if dc == DC - 1:
                                        mmi.then_inc(h, 1)
                if pname in ("q", "k"):
                    @bl.scalar
                    def _(sc):
                        k = 0
                        for oc in range(DC):
                            for tg in range(S // NT):
                                pb = k % 2
                                ps = [self.psA, self.psB][pb]
                                eb = k % 4
                                sc.wait_ge(S_.get("p_mm"), BP["p_mm"] + k + 1)
                                if k >= 4:
                                    sc.wait_ge(S_.get("p_st%d" % eb), BP["p_st%d" % eb] + 16 * ((k - 4) // 4 + 1))
                                scale = 1.0 / math.sqrt(HD) if pname == "q" else 1.0
                                bias = self.bqs_sb[:, oc:oc + 1] if pname == "q" else self.bias_sb[:, 1, oc:oc + 1]
                                sc.activation(ev[:, eb, :], ps[:, 0:NT], AF.Identity,
                                              bias=bias, scale=scale)
                                h, v = S_.inc("p_ef", 1)
                                sc.drain().then_inc(h, 1)
                                h, v = S_.inc("p_st%d" % eb, 16)
                                sc.dma_start(dst[nb, oc * 128:(oc + 1) * 128, tg * NT:(tg + 1) * NT],
                                             ev[:, eb, :]).then_inc(h, 16)
                                k += 1
                        for q_ in range(4):
                            sc.wait_ge(S_.get("p_st%d" % q_), S_.val.get("p_st%d" % q_, 0))
                else:
                    @bl.vector
                    def _(ve):
                        k = 0
                        for tt in range(S // 128):
                            for jc in range(D // NT):
                                pb = k % 2
                                ps = [self.psA, self.psB][pb]
                                eb = k % 4
                                ve.wait_ge(S_.get("p_mm"), BP["p_mm"] + k + 1)
                                if k >= 4:
                                    ve.wait_ge(S_.get("p_st%d" % eb), BP["p_st%d" % eb] + 16 * ((k - 4) // 4 + 1))
                                ve.tensor_tensor(ev[:, eb, :], ps[:, 0:NT],
                                                 self.bvrep[:, jc * NT:(jc + 1) * NT], op=OP.add)
                                h, v = S_.inc("p_ef", 1)
                                ve.drain().then_inc(h, 1)
                                k += 1
                    @bl.scalar
                    def _(sc):
                        k = 0
                        for tt in range(S // 128):
                            for jc in range(D // NT):
                                eb = k % 4
                                sc.wait_ge(S_.get("p_ef"), BP["p_ef"] + k + 1)
                                h, v = S_.inc("p_st%d" % eb, 16)
                                sc.dma_start(dst[nb, tt * 128:(tt + 1) * 128, jc * NT:(jc + 1) * NT],
                                             ev[:, eb, :]).then_inc(h, 16)
                                k += 1
                        for q_ in range(4):
                            sc.wait_ge(S_.get("p_st%d" % q_), S_.val.get("p_st%d" % q_, 0))
            prev_mm = "p_mm"
        st_b.close()
        self.barrier(tag + "proj")
        self._att_inner(nb)
        self._att_wo(nb)
        self.barrier(tag + "end")
    st_att.close()


Builder.ph_att = _ph_att
Builder._alloc_psum = _alloc_psum


def _att_inner(self, nb):
    import contextlib
    nc, c, dd = self.nc, self.cfg, self.dram
    D, S, NH, HD = c["D"], c["S"], c["NH"], c["HD"]
    QT, KG = c["QT"], c["KG"]
    DC = self.DC
    assert NH == DC and HD == 128
    NQT = QT // 128
    NKG = KG // 128
    S_ = self.sems
    rs0 = -(S - QT) // 128
    start = 2.0 ** (-8.0 / NH)
    slopes = [start * start ** i for i in range(NH)]

    st = contextlib.ExitStack()
    qsup = st.enter_context(nc.sbuf_tensor("qsup%d" % nb, [128, DC, QT], BF16))
    kgrp = st.enter_context(nc.sbuf_tensor("kgrp%d" % nb, [128, 2, DC, KG], BF16))
    vgrp = st.enter_context(nc.sbuf_tensor("vgrp%d" % nb, [128, 2, NKG, D], BF16))
    Psb = st.enter_context(nc.sbuf_tensor("Psb%d" % nb, [128, 4, QT], BF16))
    Sb = st.enter_context(nc.sbuf_tensor("Sbb%d" % nb, [128, 4, QT], F32))
    aoac = st.enter_context(nc.sbuf_tensor("aoac%d" % nb, [128, NQT, D], F32))
    aosc = st.enter_context(nc.sbuf_tensor("aosc%d" % nb, [128, 2, 128], BF16))
    aostg = st.enter_context(nc.sbuf_tensor("aostg%d" % nb, [128, 2, QT], BF16))
    rsb = st.enter_context(nc.sbuf_tensor("rsb%d" % nb, [128, NH * NQT], F32))
    onesb = st.enter_context(nc.sbuf_tensor("onesb%d" % nb, [128, 1], BF16))

    with nc.Block() as bl:
        @bl.vector
        def _(ve):
            ve.memset(onesb[:], 1.0)
            h, v = S_.inc("su_v", 1)
            ve.drain().then_inc(h, 1)

    for qi in range(S // QT):
        kvn = QT * (qi + 1)
        nkg = ceil_div(kvn, KG)
        steps = []
        for kg in range(nkg):
            kgn = min(KG, kvn - kg * KG)
            for h in range(NH):
                for kt in range(kgn // 128):
                    ktg = kg * NKG + kt
                    steps.append(dict(kg=kg, h=h, kt=kt, ktg=ktg,
                                      bidx=ktg - qi * NQT - rs0,
                                      gfirst=(kt == 0), glast=(kt == kgn // 128 - 1),
                                      kgn=kgn))
        ns = len(steps)
        groups = []
        for kg in range(nkg):
            for h in range(NH):
                groups.append((kg, h))
        ng = len(groups)
        gidx = {g: i for i, g in enumerate(groups)}
        for i, s in enumerate(steps):
            s["g"] = gidx[(s["kg"], s["h"])]
        glast_step = {}
        for i, s in enumerate(steps):
            glast_step[s["g"]] = i
        kg_last_step = {}
        for i, s in enumerate(steps):
            kg_last_step[s["kg"]] = i

        B = {n: S_.val.get(n, 0) for n in
             ("cq_l", "ck_l0", "ck_l1", "sc_mm", "scf", "expd", "pv", "ao_f", "rc_d",
              "as_r", "tr2", "tg_r", "st_ao", "tl_d", "su_v")}
        ntl = NH * NQT

        with nc.Block() as bl:
            @bl.sync
            def _(sy):
                if qi > 0:
                    sy.wait_ge(S_.get("sc_mm"), B["sc_mm"])
                h, v = S_.inc("cq_l", 16)
                sy.dma_start(qsup[:, :, :],
                             bass.AP(dd["qf"].tensor, nb * D * S + qi * QT,
                                     [[S, 128], [128 * S, DC], [1, QT]])).then_inc(h, 16)
                for kg in range(nkg):
                    kgn = min(KG, kvn - kg * KG)
                    ks = kg % 2
                    if qi > 0 and kg < 2:
                        sy.wait_ge(S_.get("pv"), B["pv"])
                    elif kg >= 2:
                        sy.wait_ge(S_.get("pv"), B["pv"] + kg_last_step[kg - 2] + 1)
                    h, v = S_.inc("ck_l%d" % ks, 32)
                    sy.dma_start(kgrp[:, ks, :, 0:kgn],
                                 bass.AP(dd["kf"].tensor, nb * D * S + kg * KG,
                                         [[S, 128], [128 * S, DC], [1, kgn]])).then_inc(h, 16)
                    sy.dma_start(vgrp[:, ks, 0:kgn // 128, :],
                                 bass.AP(dd["vt"].tensor, nb * S * D + kg * KG * D,
                                         [[D, 128], [128 * D, kgn // 128], [1, D]])).then_inc(h, 16)
            @bl.tensor
            def _(te):
                te.wait_ge(S_.get("su_v"), 1)
                te.wait_ge(S_.get("cq_l"), B["cq_l"] + 16)

                def emit_scores(i):
                    s = steps[i]
                    psS = [self.psC, self.psE][i % 2]
                    ks = s["kg"] % 2
                    te.wait_ge(S_.get("ck_l%d" % ks), B["ck_l%d" % ks] + 32 * (s["kg"] // 2 + 1))
                    if i >= 2:
                        te.wait_ge(S_.get("scf"), B["scf"] + i - 1)
                    h, v = S_.inc("sc_mm", 1)
                    te.matmul(psS[:, 0:QT], kgrp[:, s["kg"] % 2, s["h"], s["kt"] * 128:(s["kt"] + 1) * 128],
                              qsup[:, s["h"], :], start=True, stop=True).then_inc(h, 1)

                if B["rc_d"] > 0:
                    te.wait_ge(S_.get("rc_d"), B["rc_d"])
                for i in range(min(2, ns)):
                    emit_scores(i)
                for i, s in enumerate(steps):
                    te.wait_ge(S_.get("expd"), B["expd"] + i + 1)
                    g = s["g"]
                    pa = [self.psA, self.psB][g % 2]
                    if s["gfirst"] and g >= 2:
                        te.wait_ge(S_.get("ao_f"), B["ao_f"] + g - 1)
                    h, v = S_.inc("pv", 1)
                    for qt in range(NQT):
                        lp = Psb[:, i % 4, qt * 128:(qt + 1) * 128]
                        te.matmul(pa[:, qt * 128:(qt + 1) * 128], lp,
                                  vgrp[:, s["kg"] % 2, s["ktg"] - s["kg"] * NKG, s["h"] * HD:(s["h"] + 1) * HD],
                                  start=(s["gfirst"] and qt == 0),
                                  stop=(s["glast"] and qt == NQT - 1))
                        mmi = te.matmul(self.psD[:, s["h"] * NQT + qt:s["h"] * NQT + qt + 1], lp,
                                        onesb[:, :], start=(i == 0 and qt == 0),
                                        stop=(i == ns - 1 and qt == NQT - 1))
                    mmi.then_inc(h, 1)
                    if i + 2 < ns:
                        emit_scores(i + 2)
                for j in range(ntl):
                    te.wait_ge(S_.get("as_r"), B["as_r"] + j + 1)
                    if j >= 2:
                        te.wait_ge(S_.get("tg_r"), B["tg_r"] + j - 1)
                    h, v = S_.inc("tr2", 1)
                    te.transpose(self.psT[j % 2][:, :], aosc[:, j % 2, :], self.ident[:]).then_inc(h, 1)
            @bl.vector
            def _(ve):
                for i, s in enumerate(steps):
                    pb = i % 2
                    psS = [self.psC, self.psE][pb]
                    ve.wait_ge(S_.get("sc_mm"), B["sc_mm"] + i + 1)
                    if i >= 4:
                        ve.wait_ge(S_.get("expd"), B["expd"] + i - 3)
                    ve.scalar_tensor_tensor(Sb[:, i % 4, :], self.btiles[:, s["bidx"], 0:QT],
                                            float(slopes[s["h"]]), psS[:, 0:QT],
                                            op0=OP.mult, op1=OP.add)
                    h, v = S_.inc("scf", 1)
                    ve.drain().then_inc(h, 1)
                    if s["glast"]:
                        g = s["g"]
                        kg, hh = groups[g]
                        pa = [self.psA, self.psB][g % 2]
                        ve.wait_ge(S_.get("pv"), B["pv"] + i + 1)
                        if qi > 0 and kg == 0:
                            ve.wait_ge(S_.get("tl_d"), B["tl_d"])
                        for qt in range(NQT):
                            if kg == 0:
                                ve.tensor_copy(aoac[:, qt, hh * HD:(hh + 1) * HD],
                                               pa[:, qt * 128:(qt + 1) * 128])
                            else:
                                ve.tensor_tensor(aoac[:, qt, hh * HD:(hh + 1) * HD],
                                                 aoac[:, qt, hh * HD:(hh + 1) * HD],
                                                 pa[:, qt * 128:(qt + 1) * 128], op=OP.add)
                        h, v = S_.inc("ao_f", 1)
                        ve.drain().then_inc(h, 1)
                ve.wait_ge(S_.get("pv"), B["pv"] + ns)
                ve.reciprocal(rsb[:, 0:ntl], self.psD[:, 0:ntl])
                h, v = S_.inc("rc_d", 1)
                ve.drain().then_inc(h, 1)
                for j in range(ntl):
                    hh, qt = j // NQT, j % NQT
                    if j >= 2:
                        ve.wait_ge(S_.get("tr2"), B["tr2"] + j - 1)
                    ve.tensor_scalar(aosc[:, j % 2, :], aoac[:, qt, hh * HD:(hh + 1) * HD],
                                     rsb[:, j:j + 1].opt(), None, op0=OP.mult)
                    h, v = S_.inc("as_r", 1)
                    ve.drain().then_inc(h, 1)
                    ve.wait_ge(S_.get("tr2"), B["tr2"] + j + 1)
                    if j >= 2 * NQT:
                        ve.wait_ge(S_.get("st_ao"), B["st_ao"] + 16 * (j // NQT - 1))
                    ve.tensor_copy(aostg[:, (j // NQT) % 2, qt * 128:(qt + 1) * 128],
                                   self.psT[j % 2][:, :])
                    h, v = S_.inc("tg_r", 1)
                    ve.drain().then_inc(h, 1)
                h, v = S_.inc("tl_d", 1)
                ve.nop().then_inc(h, 1)
            @bl.scalar
            def _(sc):
                for i, s in enumerate(steps):
                    sc.wait_ge(S_.get("scf"), B["scf"] + i + 1)
                    if i >= 4:
                        sc.wait_ge(S_.get("pv"), B["pv"] + i - 3)
                    sc.activation(Psb[:, i % 4, :], Sb[:, i % 4, :], AF.Exp)
                    h, v = S_.inc("expd", 1)
                    sc.drain().then_inc(h, 1)
                for hh in range(NH):
                    sc.wait_ge(S_.get("tg_r"), B["tg_r"] + (hh + 1) * NQT)
                    h, v = S_.inc("st_ao", 16)
                    sc.dma_start(bass.AP(dd["aof"].tensor,
                                         nb * D * S + (hh * HD) * S + qi * QT,
                                         [[S, 128], [1, QT]]),
                                 aostg[:, hh % 2, :]).then_inc(h, 16)
                sc.wait_ge(S_.get("st_ao"), S_.val["st_ao"])
            @bl.gpsimd
            def _(g2):
                pass
    st.close()


Builder._att_inner = _att_inner


def _att_wo(self, nb):
    import contextlib
    nc, c, dd = self.nc, self.cfg, self.dram
    D, S, NT = c["D"], c["S"], c["NT"]
    DC = self.DC
    S_ = self.sems
    st = contextlib.ExitStack()
    wo_sb = st.enter_context(nc.sbuf_tensor("wosb%d" % nb, [128, DC, D], BF16))
    ao_sl = st.enter_context(nc.sbuf_tensor("aosl%d" % nb, [128, 2, DC, 128], BF16))
    osb = st.enter_context(nc.sbuf_tensor("osb%d" % nb, [128, 4, NT], BF16))
    NJC = D // NT
    B = {n: S_.val.get(n, 0) for n in ("wo_l", "aos_l0", "aos_l1", "wo_mm", "wo_ef",
                                       "wo_st0", "wo_st1", "wo_st2", "wo_st3")}
    with nc.Block() as bl:
        @bl.gpsimd
        def _(g):
            for dc in range(DC):
                h, v = S_.inc("wo_l", 16)
                g.dma_start(wo_sb[:, dc, :], dd["w_qkvo"][3, dc * 128:(dc + 1) * 128, :]).then_inc(h, 16)
        @bl.sync
        def _(sy):
            for qt in range(S // 128):
                b = qt % 2
                if qt >= 2:
                    sy.wait_ge(S_.get("wo_mm"), B["wo_mm"] + (qt - 1) * NJC)
                h, v = S_.inc("aos_l%d" % b, 16)
                sy.dma_start(ao_sl[:, b, :, :],
                             bass.AP(dd["aof"].tensor, nb * D * S + qt * 128,
                                     [[S, 128], [128 * S, DC], [1, 128]])).then_inc(h, 16)
        @bl.tensor
        def _(te):
            te.wait_ge(S_.get("wo_l"), B["wo_l"] + 16 * DC)
            k = 0
            for qt in range(S // 128):
                b = qt % 2
                te.wait_ge(S_.get("aos_l%d" % b), B["aos_l%d" % b] + 16 * (qt // 2 + 1))
                for jc in range(NJC):
                    pb = k % 2
                    ps = [self.psA, self.psB][pb]
                    if k >= 2:
                        te.wait_ge(S_.get("wo_ef"), B["wo_ef"] + k - 1)
                    h, v = S_.inc("wo_mm", 1)
                    for dc in range(DC):
                        mmi = te.matmul(ps[:, 0:NT], ao_sl[:, b, dc, :],
                                        wo_sb[:, dc, jc * NT:(jc + 1) * NT],
                                        start=(dc == 0), stop=(dc == DC - 1))
                    mmi.then_inc(h, 1)
                    k += 1
        @bl.vector
        def _(ve):
            k = 0
            for qt in range(S // 128):
                for jc in range(NJC):
                    pb = k % 2
                    ps = [self.psA, self.psB][pb]
                    eb = k % 4
                    ve.wait_ge(S_.get("wo_mm"), B["wo_mm"] + k + 1)
                    if k >= 4:
                        ve.wait_ge(S_.get("wo_st%d" % eb), B["wo_st%d" % eb] + 16 * ((k - 4) // 4 + 1))
                    ve.tensor_tensor(osb[:, eb, :], ps[:, 0:NT],
                                     self.borep[:, jc * NT:(jc + 1) * NT], op=OP.add)
                    h, v = S_.inc("wo_ef", 1)
                    ve.drain().then_inc(h, 1)
                    k += 1
        @bl.scalar
        def _(sc):
            k = 0
            for qt in range(S // 128):
                for jc in range(NJC):
                    eb = k % 4
                    sc.wait_ge(S_.get("wo_ef"), B["wo_ef"] + k + 1)
                    h, v = S_.inc("wo_st%d" % eb, 16)
                    sc.dma_start(dd["oattn"][nb * S + qt * 128: nb * S + (qt + 1) * 128,
                                             jc * NT:(jc + 1) * NT],
                                 osb[:, eb, :]).then_inc(h, 16)
                    k += 1
            for q_ in range(4):
                sc.wait_ge(S_.get("wo_st%d" % q_), S_.val.get("wo_st%d" % q_, 0))
    st.close()


Builder._att_wo = _att_wo


def _ph_prezero(self):
    nc, c, dd = self.nc, self.cfg, self.dram
    D = c["D"]
    S_ = self.sems
    T = self.TOWN
    with nc.Block() as bl:
        @bl.sync
        def _(sy):
            for i in range((T + self.CPADMAX) // 128):
                h, v = S_.inc("su_d", 16)
                sy.dma_start(dd["moe"][i * 128:(i + 1) * 128, :],
                             self.zero128[:, 0:D]).then_inc(h, 16)
            R = self.ETROWS
            h, v = S_.inc("su_d", 16)
            sy.dma_start(dd["etab"].rearrange("(p a) b -> p a b", p=128),
                         self.zero128[:, 0:(R // 128) * 2].rearrange("p (a b) -> p a b", b=2)).then_inc(h, 16)
            sy.wait_ge(S_.get("su_d"), S_.val["su_d"])


Builder.ph_prezero = _ph_prezero


def _ph_shared(self):
    import contextlib
    nc, c, dd = self.nc, self.cfg, self.dram
    D, S, NB, ISH, NT = c["D"], c["S"], c["NB"], c["ISH"], c["NT"]
    DC, ISC = self.DC, self.ISC
    S_ = self.sems
    NTG = S // NT

    for nb in range(NB):
        for which, wten, dsth in (("g", dd["sw_g"], dd["gsh"]), ("u", dd["sw_u"], dd["msh"])):
            tag = "sh_"
            bn = "s%d%s" % (nb, which)
            st = contextlib.ExitStack()
            w_sb = st.enter_context(nc.sbuf_tensor("swsb" + bn, [128, DC, ISH], BF16))
            xnsl = st.enter_context(nc.sbuf_tensor("xnsl" + bn, [128, 2, DC, NT], BF16))
            ev = st.enter_context(nc.sbuf_tensor("swev" + bn, [128, 4, NT], BF16))
            gsl = st.enter_context(nc.sbuf_tensor("gsl" + bn, [128, 2, NT], BF16)) if which == "u" else None
            B = {n: S_.val.get(n, 0) for n in
                 (tag + "wl", tag + "x0", tag + "x1", tag + "mm", tag + "ef",
                  tag + "st0", tag + "st1", tag + "st2", tag + "st3",
                  tag + "gl0", tag + "gl1")}
            with nc.Block() as bl:
                @bl.gpsimd
                def _(g):
                    for dc in range(DC):
                        h, v = S_.inc(tag + "wl", 16)
                        g.dma_start(w_sb[:, dc, :], wten[dc * 128:(dc + 1) * 128, :]).then_inc(h, 16)
                @bl.sync
                def _(sy):
                    for tg in range(NTG):
                        b = tg % 2
                        if tg >= 2:
                            sy.wait_ge(S_.get(tag + "mm"), B[tag + "mm"] + (tg - 1) * ISC)
                        h, v = S_.inc(tag + "x%d" % b, 16)
                        sy.dma_start(xnsl[:, b, :, :],
                                     bass.AP(dd["xnfm"].tensor, nb * D * S + tg * NT,
                                             [[S, 128], [128 * S, DC], [1, NT]])).then_inc(h, 16)
                        if which == "u":
                            for ic in range(ISC):
                                kk = tg * ISC + ic
                                if kk >= 2:
                                    sy.wait_ge(S_.get(tag + "ef"), B[tag + "ef"] + kk - 1)
                                h, v = S_.inc(tag + "gl%d" % (kk % 2), 16)
                                sy.dma_start(gsl[:, kk % 2, :],
                                             bass.AP(dd["gsh"].tensor, nb * ISH * S + (ic * 128) * S + tg * NT,
                                                     [[S, 128], [1, NT]])).then_inc(h, 16)
                @bl.tensor
                def _(te):
                    te.wait_ge(S_.get(tag + "wl"), B[tag + "wl"] + 16 * DC)
                    k = 0
                    for tg in range(NTG):
                        b = tg % 2
                        te.wait_ge(S_.get(tag + "x%d" % b), B[tag + "x%d" % b] + 16 * (tg // 2 + 1))
                        for ic in range(ISC):
                            pb = k % 2
                            ps = [self.psA, self.psB][pb]
                            if k >= 2:
                                te.wait_ge(S_.get(tag + "ef"), B[tag + "ef"] + k - 1)
                            h, v = S_.inc(tag + "mm", 1)
                            for dc in range(DC):
                                mmi = te.matmul(ps[:, 0:NT], w_sb[:, dc, ic * 128:(ic + 1) * 128],
                                                xnsl[:, b, dc, :], start=(dc == 0), stop=(dc == DC - 1))
                            mmi.then_inc(h, 1)
                            k += 1
                if which == "g":
                    @bl.scalar
                    def _(sc):
                        k = 0
                        for tg in range(NTG):
                            for ic in range(ISC):
                                pb = k % 2
                                ps = [self.psA, self.psB][pb]
                                eb = k % 4
                                sc.wait_ge(S_.get(tag + "mm"), B[tag + "mm"] + k + 1)
                                if k >= 4:
                                    sc.wait_ge(S_.get(tag + "st%d" % eb), B[tag + "st%d" % eb] + 16 * ((k - 4) // 4 + 1))
                                sc.activation(ev[:, eb, :], ps[:, 0:NT], AF.Silu)
                                h, v = S_.inc(tag + "ef", 1)
                                sc.drain().then_inc(h, 1)
                                h, v = S_.inc(tag + "st%d" % eb, 16)
                                sc.dma_start(bass.AP(dsth.tensor, nb * ISH * S + (ic * 128) * S + tg * NT,
                                                     [[S, 128], [1, NT]]),
                                             ev[:, eb, :]).then_inc(h, 16)
                                k += 1
                        for q_ in range(4):
                            sc.wait_ge(S_.get(tag + "st%d" % q_), S_.val.get(tag + "st%d" % q_, 0))
                    @bl.vector
                    def _(ve):
                        pass
                else:
                    @bl.vector
                    def _(ve):
                        k = 0
                        for tg in range(NTG):
                            for ic in range(ISC):
                                pb = k % 2
                                ps = [self.psA, self.psB][pb]
                                eb = k % 4
                                ve.wait_ge(S_.get(tag + "mm"), B[tag + "mm"] + k + 1)
                                ve.wait_ge(S_.get(tag + "gl%d" % (k % 2)), B[tag + "gl%d" % (k % 2)] + 16 * (k // 2 + 1))
                                if k >= 4:
                                    ve.wait_ge(S_.get(tag + "st%d" % eb), B[tag + "st%d" % eb] + 16 * ((k - 4) // 4 + 1))
                                ve.tensor_tensor(ev[:, eb, :], ps[:, 0:NT], gsl[:, k % 2, :], op=OP.mult)
                                h, v = S_.inc(tag + "ef", 1)
                                ve.drain().then_inc(h, 1)
                                k += 1
                    @bl.scalar
                    def _(sc):
                        k = 0
                        for tg in range(NTG):
                            for ic in range(ISC):
                                eb = k % 4
                                sc.wait_ge(S_.get(tag + "ef"), B[tag + "ef"] + k + 1)
                                h, v = S_.inc(tag + "st%d" % eb, 16)
                                sc.dma_start(bass.AP(dsth.tensor, nb * ISH * S + (ic * 128) * S + tg * NT,
                                                     [[S, 128], [1, NT]]),
                                             ev[:, eb, :]).then_inc(h, 16)
                                k += 1
                        for q_ in range(4):
                            sc.wait_ge(S_.get(tag + "st%d" % q_), S_.val.get(tag + "st%d" % q_, 0))
            st.close()

    st = contextlib.ExitStack()
    swd = st.enter_context(nc.sbuf_tensor("swd", [128, ISC, D], BF16))
    msl = st.enter_context(nc.sbuf_tensor("msl", [128, 2, ISC, 128], BF16))
    xsl = st.enter_context(nc.sbuf_tensor("xsl", [128, 2, D], F32))
    osl = st.enter_context(nc.sbuf_tensor("osl", [128, 2, D], BF16))
    mosl = st.enter_context(nc.sbuf_tensor("mosl", [128, 2, D], F32))
    acc = st.enter_context(nc.sbuf_tensor("acc", [128, 2, NT], F32))
    oev = st.enter_context(nc.sbuf_tensor("oev", [128, 4, NT], F32))
    NJC = D // NT
    B = {n: S_.val.get(n, 0) for n in
         ("y_wl", "y_m0", "y_m1", "y_x0", "y_x1", "y_mm", "y_ef", "y_ac",
          "out_d0", "out_d1", "out_d2", "out_d3")}
    with nc.Block() as bl:
        @bl.gpsimd
        def _(g):
            for ic in range(ISC):
                h, v = S_.inc("y_wl", 16)
                g.dma_start(swd[:, ic, :], dd["sw_d"][ic * 128:(ic + 1) * 128, :]).then_inc(h, 16)
        @bl.sync
        def _(sy):
            for qt in range(self.TOWN // 128):
                b = qt % 2
                nb = qt // (S // 128)
                qtl = qt % (S // 128)
                if qt >= 2:
                    sy.wait_ge(S_.get("y_mm"), B["y_mm"] + (qt - 1) * NJC)
                    sy.wait_ge(S_.get("y_ac"), B["y_ac"] + (qt - 1) * NJC)
                h, v = S_.inc("y_m%d" % b, 16)
                sy.dma_start(msl[:, b, :, :],
                             bass.AP(dd["msh"].tensor, nb * ISH * S + qtl * 128,
                                     [[S, 128], [128 * S, ISC], [1, 128]])).then_inc(h, 16)
                h, v = S_.inc("y_x%d" % b, 48)
                sy.dma_start(xsl[:, b, :], dd["x_own"][qt * 128:(qt + 1) * 128, :]).then_inc(h, 16)
                sy.dma_start(osl[:, b, :], dd["oattn"][qt * 128:(qt + 1) * 128, :]).then_inc(h, 16)
                sy.dma_start(mosl[:, b, :], dd["moe"][qt * 128:(qt + 1) * 128, :]).then_inc(h, 16)
        @bl.tensor
        def _(te):
            te.wait_ge(S_.get("y_wl"), B["y_wl"] + 16 * ISC)
            k = 0
            for qt in range(self.TOWN // 128):
                b = qt % 2
                te.wait_ge(S_.get("y_m%d" % b), B["y_m%d" % b] + 16 * (qt // 2 + 1))
                for jc in range(NJC):
                    pb = k % 2
                    ps = [self.psA, self.psB][pb]
                    if k >= 2:
                        te.wait_ge(S_.get("y_ef"), B["y_ef"] + k - 1)
                    h, v = S_.inc("y_mm", 1)
                    for ic in range(ISC):
                        mmi = te.matmul(ps[:, 0:NT], msl[:, b, ic, :],
                                        swd[:, ic, jc * NT:(jc + 1) * NT],
                                        start=(ic == 0), stop=(ic == ISC - 1))
                    mmi.then_inc(h, 1)
                    k += 1
        @bl.vector
        def _(ve):
            k = 0
            for qt in range(self.TOWN // 128):
                b = qt % 2
                ve.wait_ge(S_.get("y_x%d" % b), B["y_x%d" % b] + 48 * (qt // 2 + 1))
                for jc in range(NJC):
                    pb = k % 2
                    ps = [self.psA, self.psB][pb]
                    ve.wait_ge(S_.get("y_mm"), B["y_mm"] + k + 1)
                    ve.tensor_tensor(acc[:, pb, :], ps[:, 0:NT],
                                     xsl[:, b, jc * NT:(jc + 1) * NT], op=OP.add)
                    h, v = S_.inc("y_ef", 1)
                    ve.drain().then_inc(h, 1)
                    ve.tensor_tensor(acc[:, pb, :], acc[:, pb, :],
                                     osl[:, b, jc * NT:(jc + 1) * NT], op=OP.add)
                    ve.drain()
                    eb = k % 4
                    if k >= 4:
                        ve.wait_ge(S_.get("out_d%d" % eb), B["out_d%d" % eb] + 16 * ((k - 4) // 4 + 1))
                    ve.tensor_tensor(oev[:, eb, :], acc[:, pb, :],
                                     mosl[:, b, jc * NT:(jc + 1) * NT], op=OP.add)
                    h, v = S_.inc("y_ac", 1)
                    ve.drain().then_inc(h, 1)
                    k += 1
        @bl.scalar
        def _(sc):
            k = 0
            for qt in range(self.TOWN // 128):
                for jc in range(NJC):
                    eb = k % 4
                    sc.wait_ge(S_.get("y_ac"), B["y_ac"] + k + 1)
                    h, v = S_.inc("out_d%d" % eb, 16)
                    sc.dma_start(dd["out"][qt * 128:(qt + 1) * 128, jc * NT:(jc + 1) * NT],
                                 oev[:, eb, :]).then_inc(h, 16)
                    k += 1
            for q_ in range(4):
                sc.wait_ge(S_.get("out_d%d" % q_), S_.val.get("out_d%d" % q_, 0))
    st.close()


Builder.ph_shared = _ph_shared


class BranchCnt:

    def __init__(self, sems):
        self.S = sems
        self.val = {}

    def inc(self, name, by):
        h = self.S.get(name)
        self.val[name] = self.val.get(name, 0) + by
        return h, self.val[name]

    def get(self, name):
        return self.S.get(name)

    def v(self, name):
        return self.val.get(name, 0)


def _hot_block(self, hc, bodies):
    nc = self.nc
    with nc.Block() as bl:
        for engname, dec in (("sy", bl.sync), ("sc", bl.scalar), ("ve", bl.vector),
                             ("te", bl.tensor), ("gp", bl.gpsimd)):
            body = bodies.get(engname)
            def wrap(eng, body=body, engname=engname):
                if body is None:
                    return
                with eng.If_eq(self.cid_regs[engname], hc):
                    body(eng)
            dec(wrap)


Builder.hot_block = _hot_block


def _ph_route(self):
    import contextlib
    nc, c, dd = self.nc, self.cfg, self.dram
    D, E, K, CAP, S = c["D"], c["E"], c["TOPK"], c["CAP"], c["S"]
    DC = self.DC
    EST = self.ESTMAX
    S_ = self.sems
    JTM = max((min(h["t0"] + h.get("hlen", self.TOWN), c["TH"]) // 128 for h in c["HOT"].values()), default=1)
    JAM = JTM * K
    DELTA = 2.0 ** -17

    st = contextlib.ExitStack()
    sb = lambda n, sh, dt=F32: st.enter_context(nc.sbuf_tensor(n, sh, dt))
    xsp = sb("xsp", [128, 2, D], BF16)
    xfmt = sb("xfmt", [128, 2, DC, 128], BF16)
    gwsb = sb("gwsb", [16, D], F32)
    gtmp = sb("gtmp", [16, D], F32)
    ghat = sb("ghat", [16, 3, D], BF16)
    gfm = sb("gfm", [128, 3, DC, E], BF16)
    lgsp = sb("lgsp", [128, JTM, E], F32)
    lg2 = sb("lg2", [128, JTM, E], F32)
    eqk = sb("eqk", [128, K, JTM, E], F32)
    idxa = sb("idxa", [128, JTM, K], F32)
    mrow = sb("mrow", [128, JTM], F32)
    ohs = sb("ohs", [128, E, JAM], F32)
    scn = sb("scn", [128, E, JAM], F32)
    csb = sb("csb", [128, E], F32)
    cbf = sb("cbf", [128, E], BF16)
    basesb = sb("basesb", [128, E], F32)
    basem = sb("basem", [128, E], F32)
    bndrow = sb("bndrow", [1, E], F32)
    rg = sb("rg", [128, JAM], F32)
    rl = sb("rl", [128, JAM], F32)
    tmpa = sb("tmpa", [128, JAM], F32)
    tmpb = sb("tmpb", [128, JAM], F32)
    tokl = sb("tokl", [128, JTM], I32)
    toklf = sb("toklf", [128, JTM], F32)
    ownm = sb("ownm", [128, JTM], F32)
    wk4 = sb("wk4", [128, K, JTM], F32)
    wden = sb("wden", [128, JTM], F32)
    pairs = sb("pairs", [128, JAM, 2], F32)
    gtg = sb("gtg", [128, JAM], F32)
    gtgi = sb("gtgi", [128, JAM], I32)
    losl = sb("losl", [128, 2, DC, 128], BF16)
    lown_sb = sb("lown_sb", [128, JTM, E], F32)
    wtmp = sb("wtmp", [128, JTM * E], F32)
    bnd2 = sb("bnd2", [1, E], F32)

    for hc, hcfg in sorted(c["HOT"].items()):
        t0 = hcfg["t0"]
        cpad = hcfg["cpad"]
        hlen = hcfg.get("hlen", self.TOWN)
        TR = min(t0 + hlen, c["TH"])
        JT = TR // 128
        JA = JT * K
        p_b, j_b = t0 // JT, t0 % JT
        NLO = (TR - t0) // 128
        BC = BranchCnt(S_)
        JE = JT * E

        b0 = {n: BC.v(n) for n in ("r_d", "r_v", "r_t", "r_tf")}

        def s1_sy(sy):
            h, v = BC.inc("r_d", 16)
            sy.dma_start(gwsb[:], dd["gate_w"][:]).then_inc(h, 16)
            sy.wait_ge(h, v)

        def s1_ve(ve):
            ve.wait_ge(BC.get("r_d"), b0["r_d"] + 16)
            ve.tensor_copy(ghat[:, 2, :], gwsb[:])
            ve.drain()
            ve.tensor_reduce(gtmp[:, 0:1], gwsb[:], axis=AX.X, op=OP.add)
            ve.drain()
            ve.tensor_scalar(gtmp[:, 0:1], gtmp[:, 0:1], 1.0 / D, None, op0=OP.mult)
            ve.drain()
            ve.tensor_scalar(gwsb[:], gwsb[:], gtmp[:, 0:1].opt(), None, op0=OP.subtract)
            ve.drain()
            ve.tensor_copy(ghat[:, 0, :], gwsb[:])
            ve.drain()
            ve.tensor_copy(gtmp[:], ghat[:, 0, :])
            ve.drain()
            ve.tensor_tensor(gwsb[:], gwsb[:], gtmp[:], op=OP.subtract)
            ve.drain()
            ve.tensor_copy(ghat[:, 1, :], gwsb[:])
            h, v = BC.inc("r_v", 1)
            ve.drain().then_inc(h, 1)

        def s1_te(te):
            te.wait_ge(BC.get("r_v"), b0["r_v"] + 1)
            k = 0
            for which in range(3):
                for dc in range(DC):
                    t = k % 2
                    if k >= 2:
                        te.wait_ge(BC.get("r_tf"), b0["r_tf"] + k - 1)
                    h, v = BC.inc("r_t", 1)
                    te.transpose(self.psT[t][:, 0:16], ghat[0:16, which, dc * 128:(dc + 1) * 128],
                                 self.ident[0:16, 0:16]).then_inc(h, 1)
                    k += 1

        def s1_ve2(ve):
            k = 0
            for which in range(3):
                for dc in range(DC):
                    t = k % 2
                    ve.wait_ge(BC.get("r_t"), b0["r_t"] + k + 1)
                    h, v = BC.inc("r_tf", 1)
                    ve.tensor_copy(gfm[:, which, dc, :], self.psT[t][:, 0:16]).then_inc(h, 1)
                    k += 1
        self.hot_block(hc, dict(sy=s1_sy, ve=s1_ve))
        self.hot_block(hc, dict(te=s1_te, ve=s1_ve2))

        b2 = {n: BC.v(n) for n in ("r_gd0", "r_gd1", "r_t", "r_tf", "r_mm", "r_lge")}

        def s2_gp(g):
            for jj in range(JT):
                bslot = jj % 2
                if jj >= 2:
                    g.wait_ge(BC.get("r_tf"), b2["r_tf"] + (jj - 1) * DC)
                if t0 == 0:
                    h, v = BC.inc("r_gd%d" % bslot, 16)
                    g.dma_start(xsp[:, bslot, :],
                                bass.AP(dd["x_own"].tensor, jj * D, [[JT * D, 128], [1, D]])).then_inc(h, 16)
                else:
                    p_cut = ceil_div(t0 - jj, JT)
                    h, v = BC.inc("r_gd%d" % bslot, 16)
                    g.dma_start(xsp[0:p_cut, bslot, :],
                                bass.AP(dd["x_pre"].tensor, jj * D, [[JT * D, p_cut], [1, D]])).then_inc(h, 16)
                    h, v = BC.inc("r_gd%d" % bslot, 16)
                    g.dma_start(xsp[p_cut:128, bslot, :],
                                bass.AP(dd["x_own"].tensor, (p_cut * JT + jj - t0) * D,
                                        [[JT * D, 128 - p_cut], [1, D]])).then_inc(h, 16)

        nldma = 1 if t0 == 0 else 2

        def s2_te(te):
            for jj in range(JT):
                bslot = jj % 2
                te.wait_ge(BC.get("r_gd%d" % bslot), b2["r_gd%d" % bslot] + 16 * nldma * (jj // 2 + 1))
                for dc in range(DC):
                    k = jj * DC + dc
                    t = k % 2
                    if k >= 2:
                        te.wait_ge(BC.get("r_tf"), b2["r_tf"] + k - 1)
                    h, v = BC.inc("r_t", 1)
                    te.transpose(self.psT[t][:, :], xsp[:, bslot, dc * 128:(dc + 1) * 128],
                                 self.ident[:]).then_inc(h, 1)
                te.wait_ge(BC.get("r_tf"), b2["r_tf"] + (jj + 1) * DC)
                if jj >= 1:
                    te.wait_ge(BC.get("r_lge"), b2["r_lge"] + jj)
                h, v = BC.inc("r_mm", 1)
                for dc in range(DC):
                    for hl in range(2):
                        mmi = te.matmul(self.psF[:, 0:E], xfmt[:, jj % 2, dc, :],
                                        gfm[:, hl, dc, :],
                                        start=(dc == 0 and hl == 0),
                                        stop=(dc == DC - 1 and hl == 1))
                mmi.then_inc(h, 1)

        def s2_ve(ve):
            for jj in range(JT):
                if jj >= 2:
                    ve.wait_ge(BC.get("r_mm"), b2["r_mm"] + jj - 1)
                for dc in range(DC):
                    k = jj * DC + dc
                    t = k % 2
                    ve.wait_ge(BC.get("r_t"), b2["r_t"] + k + 1)
                    h, v = BC.inc("r_tf", 1)
                    ve.tensor_copy(xfmt[:, jj % 2, dc, :], self.psT[t][:, :]).then_inc(h, 1)
                ve.wait_ge(BC.get("r_mm"), b2["r_mm"] + jj + 1)
                h, v = BC.inc("r_lge", 1)
                ve.scalar_tensor_tensor(lgsp[:, jj, :], self.iota16[:, :], -DELTA,
                                        self.psF[:, 0:E], op0=OP.mult, op1=OP.add).then_inc(h, 1)
        self.hot_block(hc, dict(gp=s2_gp, te=s2_te, ve=s2_ve))

        def s3_ve(ve):
            ve.wait_ge(BC.get("r_lge"), b2["r_lge"] + JT)
            for k in range(K):
                ve.tensor_reduce(mrow[:, 0:JT], lgsp[:, 0:JT, :], axis=AX.X, op=OP.max)
                ve.drain()
                ve.tensor_tensor(eqk[:, k, 0:JT, :], lgsp[:, 0:JT, :],
                                 mrow[:, 0:JT].unsqueeze(2).broadcast_to([128, JT, E]),
                                 op=OP.is_equal)
                ve.drain()
                ve.tensor_tensor(lg2[:, 0:JT, :], eqk[:, k, 0:JT, :],
                                 self.iota16[:, :].unsqueeze(1).broadcast_to([128, JT, E]),
                                 op=OP.mult)
                ve.drain()
                ve.tensor_reduce(idxa[:, 0:JT, k], lg2[:, 0:JT, :], axis=AX.X, op=OP.add)
                ve.drain()
                if k < K - 1:
                    ve.tensor_scalar(lg2[:, 0:JT, :], eqk[:, k, 0:JT, :], 1e9, None, op0=OP.mult)
                    ve.drain()
                    ve.tensor_tensor(lgsp[:, 0:JT, :], lgsp[:, 0:JT, :], lg2[:, 0:JT, :],
                                     op=OP.subtract)
                    ve.drain()
            h, v = BC.inc("r_v", 1)
            ve.nop().then_inc(h, 1)
        self.hot_block(hc, dict(ve=s3_ve))

        b4 = {n: BC.v(n) for n in ("r_v", "r_mm", "r_lge", "r_g", "r_bd")}
        bb4 = b4["r_bd"]

        def s4_gp(g):
            g.wait_ge(BC.get("r_v"), b4["r_v"])
            g.iota(tokl[:, 0:JT], [[1, JT]], base=-t0, channel_multiplier=JT)
            g.drain()
            h, v = BC.inc("r_g", 1)
            g.tensor_copy(toklf[:, 0:JT], tokl[:, 0:JT]).then_inc(h, 1)

        def s4_ve(ve):
            ve.wait_ge(BC.get("r_v"), b4["r_v"])
            av = idxa[:, 0:JT, :].rearrange("p j k -> p (j k)")
            for e in range(E):
                ve.tensor_scalar(ohs[:, e, 0:JA], av, float(e), None, op0=OP.is_equal)
            ve.drain()
            ve.memset(tmpa[:, 0:JA], 0.0)
            ve.drain()
            for e in range(E):
                ve.tensor_tensor_scan(scn[:, e, 0:JA], ohs[:, e, 0:JA], tmpa[:, 0:JA],
                                      0.0, op0=OP.add, op1=OP.add)
                ve.drain()
                ve.tensor_copy(csb[:, e:e + 1], scn[:, e, JA - 1:JA])
            ve.drain()
            ve.tensor_copy(cbf[:, :], csb[:, :])
            h, v = BC.inc("r_v", 1)
            ve.drain().then_inc(h, 1)
            ve.wait_ge(BC.get("r_mm"), b4["r_mm"] + 1)
            ve.tensor_copy(basesb[:, :], self.psF[:, 0:E])
            h, v = BC.inc("r_v", 1)
            ve.drain().then_inc(h, 1)
            if t0 > 0:
                ve.wait_ge(BC.get("r_bd"), bb4 + 32)
                ve.tensor_tensor(bndrow[0:1, :], bndrow[0:1, :], bnd2[0:1, :], op=OP.add)
                ve.drain()
            h, v = BC.inc("r_v", 1)
            ve.nop().then_inc(h, 1)
            if t0 > 0:
                ve.wait_ge(BC.get("r_mm"), b4["r_mm"] + 2)
                ve.tensor_tensor(basem[:, :], basesb[:, :], self.psF[:, 0:E], op=OP.subtract)
                ve.drain()
            else:
                ve.tensor_copy(basem[:, :], basesb[:, :])
                ve.drain()
            ve.memset(rg[:, 0:JA], 0.0)
            ve.memset(rl[:, 0:JA], 0.0)
            ve.drain()
            for e in range(E):
                ve.scalar_tensor_tensor(tmpb[:, 0:JA], scn[:, e, 0:JA], basesb[:, e:e + 1].opt(),
                                        ohs[:, e, 0:JA], op0=OP.add, op1=OP.mult)
                ve.drain()
                ve.tensor_tensor(rg[:, 0:JA], rg[:, 0:JA], tmpb[:, 0:JA], op=OP.add)
                ve.drain()
                ve.scalar_tensor_tensor(tmpb[:, 0:JA], scn[:, e, 0:JA], basem[:, e:e + 1].opt(),
                                        ohs[:, e, 0:JA], op0=OP.add, op1=OP.mult)
                ve.drain()
                ve.tensor_tensor(rl[:, 0:JA], rl[:, 0:JA], tmpb[:, 0:JA], op=OP.add)
                ve.drain()
            ve.tensor_scalar(rg[:, 0:JA], rg[:, 0:JA], -1.0, None, op0=OP.add)
            ve.tensor_scalar(rl[:, 0:JA], rl[:, 0:JA], -1.0, None, op0=OP.add)
            h, v = BC.inc("r_v", 1)
            ve.drain().then_inc(h, 1)

        def s4_sy(sy):
            if t0 > 0:
                sy.wait_ge(BC.get("r_v"), b4["r_v"] + 2)
                h, v = BC.inc("r_bd", 16)
                sy.dma_start(bndrow[0:1, :], basesb[p_b:p_b + 1, :]).then_inc(h, 16)
                h, v = BC.inc("r_bd", 16)
                if j_b > 0:
                    sy.dma_start(bnd2[0:1, :],
                                 scn[p_b:p_b + 1, :, j_b * K - 1:j_b * K].rearrange("p e a -> p (e a)"),
                                 allow_slow_non_contiguous=True).then_inc(h, 16)
                else:
                    sy.dma_start(bnd2[0:1, :], self.zero128[0:1, 0:E]).then_inc(h, 16)

        def s4_te(te):
            te.wait_ge(BC.get("r_v"), b4["r_v"] + 1)
            h, v = BC.inc("r_mm", 1)
            te.matmul(self.psF[:, 0:E], self.ltri[:, :], cbf[:, :],
                      start=True, stop=True).then_inc(h, 1)
            if t0 > 0:
                te.wait_ge(BC.get("r_v"), b4["r_v"] + 3)
                h, v = BC.inc("r_mm", 1)
                te.matmul(self.psF[:, 0:E], self.ones1[:, :], bndrow[0:1, :],
                          start=True, stop=True).then_inc(h, 1)
        self.hot_block(hc, dict(gp=s4_gp, ve=s4_ve, te=s4_te, sy=s4_sy))

        b5 = {n: BC.v(n) for n in ("r_l0", "r_l1", "r_mm", "r_lge", "r_v", "r_st")}

        def s5_sy(sy):
            for ti in range(NLO):
                bslot = ti % 2
                u = ti * 128
                nb_, col = u // S, u % S
                if ti >= 2:
                    sy.wait_ge(BC.get("r_mm"), b5["r_mm"] + ti - 1)
                h, v = BC.inc("r_l%d" % bslot, 16)
                sy.dma_start(losl[:, bslot, :, :],
                             bass.AP(dd["xnfm"].tensor, nb_ * D * S + col,
                                     [[S, 128], [128 * S, DC], [1, 128]])).then_inc(h, 16)

        def s5_te(te):
            for ti in range(NLO):
                bslot = ti % 2
                te.wait_ge(BC.get("r_l%d" % bslot), b5["r_l%d" % bslot] + 16 * (ti // 2 + 1))
                if ti >= 1:
                    te.wait_ge(BC.get("r_lge"), b5["r_lge"] + ti)
                h, v = BC.inc("r_mm", 1)
                for dc in range(DC):
                    mmi = te.matmul(self.psF[:, 0:E], losl[:, bslot, dc, :], gfm[:, 2, dc, :],
                                    start=(dc == 0), stop=(dc == DC - 1))
                mmi.then_inc(h, 1)

        def s5_ve(ve):
            for ti in range(NLO):
                ve.wait_ge(BC.get("r_mm"), b5["r_mm"] + ti + 1)
                h, v = BC.inc("r_lge", 1)
                ve.tensor_copy(lown_sb[:, ti, :], self.psF[:, 0:E]).then_inc(h, 1)

        def s5_sc(sc):
            sc.wait_ge(BC.get("r_lge"), b5["r_lge"] + NLO)
            h, v = BC.inc("r_st", 16)
            sc.dma_start(bass.AP(dd["lown"].tensor, 0, [[E, 128], [128 * E, NLO], [1, E]]),
                         lown_sb[:, 0:NLO, :]).then_inc(h, 16)
            sc.wait_ge(h, v)
        self.hot_block(hc, dict(sy=s5_sy, te=s5_te, ve=s5_ve, sc=s5_sc))

        b5r = {n: BC.v(n) for n in ("r_d", "r_v", "r_st")}

        def s5r_ve(ve):
            ve.memset(lg2[:, 0:JT, :], 0.0)
            h, v = BC.inc("r_v", 1)
            ve.drain().then_inc(h, 1)

        def s5r_sy(sy):
            sy.wait_ge(BC.get("r_v"), b5r["r_v"] + 1)
            sy.wait_ge(BC.get("r_st"), b5r["r_st"])
            if t0 == 0:
                h, v = BC.inc("r_d", 16)
                sy.dma_start(lg2[:, 0:JT, :],
                             bass.AP(dd["lown"].tensor, 0,
                                     [[JT * E, 128], [E, JT], [1, E]])).then_inc(h, 16)
                sy.wait_ge(h, v)
            else:
                h, v = BC.inc("r_d", 16)
                sy.dma_start(lg2[p_b:p_b + 1, j_b:JT, :],
                             bass.AP(dd["lown"].tensor, 0, [[0, 1], [E, JT - j_b], [1, E]])).then_inc(h, 16)
                h, v = BC.inc("r_d", 16)
                sy.dma_start(lg2[p_b + 1:128, 0:JT, :],
                             bass.AP(dd["lown"].tensor, (JT - j_b) * E,
                                     [[JT * E, 127 - p_b], [E, JT], [1, E]])).then_inc(h, 16)
                sy.wait_ge(h, v)
        self.hot_block(hc, dict(ve=s5r_ve, sy=s5r_sy))

        b6 = {n: BC.v(n) for n in ("r_d", "r_v", "r_a")}

        def s6_ve(ve):
            ve.wait_ge(BC.get("r_d"), b6["r_d"])
            ve.tensor_reduce(mrow[:, 0:JT], lg2[:, 0:JT, :], axis=AX.X, op=OP.max)
            ve.drain()
            ve.tensor_tensor(lg2[:, 0:JT, :], lg2[:, 0:JT, :],
                             mrow[:, 0:JT].unsqueeze(2).broadcast_to([128, JT, E]),
                             op=OP.subtract)
            h, v = BC.inc("r_v", 1)
            ve.drain().then_inc(h, 1)
            ve.wait_ge(BC.get("r_a"), b6["r_a"] + 1)
            for k in range(K):
                ve.tensor_tensor(wtmp[:, 0:JE].rearrange("p (j e) -> p j e", e=E),
                                 eqk[:, k, 0:JT, :],
                                 lg2[:, 0:JT, :], op=OP.mult)
                ve.drain()
                ve.tensor_reduce(wk4[:, k, 0:JT],
                                 wtmp[:, 0:JE].rearrange("p (j e) -> p j e", e=E),
                                 axis=AX.X, op=OP.add)
                ve.drain()
            ve.tensor_tensor(wden[:, 0:JT], wk4[:, 0, 0:JT], wk4[:, 1, 0:JT], op=OP.add)
            ve.drain()
            for k in range(2, K):
                ve.tensor_tensor(wden[:, 0:JT], wden[:, 0:JT], wk4[:, k, 0:JT], op=OP.add)
                ve.drain()
            ve.reciprocal(wden[:, 0:JT], wden[:, 0:JT])
            ve.drain()
            for k in range(K):
                ve.tensor_tensor(wk4[:, k, 0:JT], wk4[:, k, 0:JT], wden[:, 0:JT], op=OP.mult)
            h, v = BC.inc("r_v", 1)
            ve.drain().then_inc(h, 1)

        def s6_sc(sc):
            sc.wait_ge(BC.get("r_v"), b6["r_v"] + 1)
            h, v = BC.inc("r_a", 1)
            sc.activation(lg2[:, 0:JT, :], lg2[:, 0:JT, :], AF.Exp).then_inc(h, 1)
        self.hot_block(hc, dict(ve=s6_ve, sc=s6_sc))

        b7 = {n: BC.v(n) for n in ("r_v", "r_g", "r_sc")}

        def s7_ve(ve):
            ve.wait_ge(BC.get("r_v"), b7["r_v"])
            ve.wait_ge(BC.get("r_g"), b7["r_g"])
            ve.tensor_scalar(ownm[:, 0:JT], tokl[:, 0:JT], 0.0, None, op0=OP.is_ge)
            ve.drain()
            ve.tensor_scalar(mrow[:, 0:JT], tokl[:, 0:JT], float(TR - t0), None, op0=OP.is_lt)
            ve.drain()
            ve.tensor_tensor(ownm[:, 0:JT], ownm[:, 0:JT], mrow[:, 0:JT], op=OP.mult)
            ve.drain()
            ve.tensor_scalar(tmpa[:, 0:JA], rg[:, 0:JA], float(CAP), None, op0=OP.is_lt)
            ve.drain()
            ve.tensor_tensor(tmpa[:, 0:JA].rearrange("p (j k) -> p j k", k=K),
                             tmpa[:, 0:JA].rearrange("p (j k) -> p j k", k=K),
                             ownm[:, 0:JT].unsqueeze(2).broadcast_to([128, JT, K]),
                             op=OP.mult)
            ve.drain()
            ve.tensor_scalar(tmpb[:, 0:JA], rl[:, 0:JA], 0.0, float(cpad), op0=OP.max, op1=OP.min)
            ve.drain()
            ve.scalar_tensor_tensor(tmpb[:, 0:JA], tmpb[:, 0:JA], -float(cpad),
                                    tmpa[:, 0:JA], op0=OP.add, op1=OP.mult)
            ve.drain()
            ve.tensor_scalar(tmpb[:, 0:JA], tmpb[:, 0:JA], float(cpad), None, op0=OP.add)
            ve.drain()
            ve.scalar_tensor_tensor(gtg[:, 0:JA],
                                    idxa[:, 0:JT, :].rearrange("p j k -> p (j k)"),
                                    float(EST), tmpb[:, 0:JA], op0=OP.mult, op1=OP.add)
            ve.drain()
            ve.tensor_copy(gtgi[:, 0:JA], gtg[:, 0:JA])
            ve.drain()
            for k in range(K):
                ve.tensor_copy(pairs[:, 0:JA, 0].rearrange("p (j k) -> p j k", k=K)[:, :, k],
                               wk4[:, k, 0:JT])
            ve.tensor_copy(pairs[:, 0:JA, 1].rearrange("p (j k) -> p j k", k=K),
                           toklf[:, 0:JT].unsqueeze(2).broadcast_to([128, JT, K]))
            h, v = BC.inc("r_v", 1)
            ve.drain().then_inc(h, 1)

        def s7_gp(g):
            g.wait_ge(BC.get("r_v"), b7["r_v"] + 1)
            for col in range(JA):
                h, v = BC.inc("r_sc", 16)
                g.indirect_dma_start(dd["etab"][:, :],
                                     IndirectOffsetOnAxis(gtgi[:, col:col + 1], 0),
                                     pairs[:, col, :], None).then_inc(h, 16)
                g.wait_ge(h, v)
        self.hot_block(hc, dict(ve=s7_ve, gp=s7_gp))
    st.close()


Builder.ph_route = _ph_route


def _ph_moe(self):
    import contextlib
    nc, c, dd = self.nc, self.cfg, self.dram
    D, E, I, NT, S = c["D"], c["E"], c["I"], c["NT"], c["S"]
    DC, IC = self.DC, self.IC
    EST = self.ESTMAX
    ESTC = EST // 128
    R128 = self.ETROWS // 128
    S_ = self.sems
    T = self.TOWN
    CPM = self.CPADMAX

    st = contextlib.ExitStack()
    sb = lambda n, sh, dt=F32: st.enter_context(nc.sbuf_tensor(n, sh, dt))
    etw = sb("etw", [128, R128, 2], F32)
    toki = sb("toki", [128, R128], I32)
    tsci = sb("tsci", [128, R128], I32)
    sef = sb("sef", [128, R128], F32)
    mmask = sb("mmask", [128, R128], F32)
    ttmp = sb("ttmp", [128, R128], F32)
    wun = sb("wun", [128, 2, DC * I], BF16)
    xgr = sb("xgr", [128, 2, D], BF16)
    xgfm = sb("xgfm", [128, DC, CPM], BF16)
    gsil = sb("gsil", [128, IC, CPM], BF16)
    msb = sb("msb", [128, IC, CPM], BF16)
    eout = sb("eout", [128, 2, D], F32)

    for hc, hcfg in sorted(c["HOT"].items()):
        cpad = hcfg["cpad"]
        NST = cpad // 128
        NTC = ceil_div(cpad, NT)
        BC = BranchCnt(S_)
        b0 = {n: BC.v(n) for n in ("m_et", "m_v", "m_g")}

        def et_sy(sy):
            h, v = BC.inc("m_et", 16)
            sy.dma_start(etw[:, :, :],
                         dd["etab"].rearrange("(a p) b -> p a b", p=128)).then_inc(h, 16)
            sy.wait_ge(h, v)

        def et_gp(g):
            h, v = BC.inc("m_g", 1)
            g.iota(ttmp[:, 0:R128].bitcast(I32), [[0, E], [128, ESTC]], base=0,
                   channel_multiplier=1)
            g.drain().then_inc(h, 1)

        def et_ve(ve):
            ve.wait_ge(BC.get("m_et"), b0["m_et"] + 16)
            ve.wait_ge(BC.get("m_g"), b0["m_g"] + 1)
            ve.tensor_copy(sef[:, 0:R128], ttmp[:, 0:R128].bitcast(I32))
            ve.drain()
            ve.tensor_copy(toki[:, :], etw[:, :, 1])
            ve.drain()
            ve.tensor_scalar(mmask[:, :], etw[:, :, 0], 0.0, None, op0=OP.is_gt)
            ve.drain()
            ve.tensor_scalar(sef[:, :], sef[:, :], float(T), None, op0=OP.add)
            ve.drain()
            ve.tensor_tensor(ttmp[:, :], etw[:, :, 1], sef[:, :], op=OP.subtract)
            ve.drain()
            ve.tensor_tensor(ttmp[:, :], ttmp[:, :], mmask[:, :], op=OP.mult)
            ve.drain()
            ve.tensor_tensor(ttmp[:, :], ttmp[:, :], sef[:, :], op=OP.add)
            ve.drain()
            ve.tensor_copy(tsci[:, :], ttmp[:, :])
            h, v = BC.inc("m_v", 1)
            ve.drain().then_inc(h, 1)
        self.hot_block(hc, dict(sy=et_sy, gp=et_gp, ve=et_ve))

        bigb = {n: BC.v(n) for n in
                ("m_wl0", "m_wl1", "m_gd0", "m_gd1", "m_tr", "m_tf", "m_mm",
                 "m_ef", "m_sl", "m_sd")}
        NJC = D // NT
        mm_pe = IC * NTC * 2 + NST * NJC
        tr_pe = NST * DC
        ef_pe = IC * NTC + NST * NJC
        sl_pe = IC * NTC
        sd_pe = NST

        def big_gp(g):
            for e in range(E):
                mm0 = bigb["m_mm"] + e * mm_pe
                for wt, slot in ((dd["ew_g"], 0), (dd["ew_u"], 1)):
                    if e > 0:
                        g.wait_ge(BC.get("m_mm"), mm0)
                    for rr in range(DC):
                        h, v = BC.inc("m_wl%d" % slot, 16)
                        g.dma_start(wun[:, slot, :].rearrange("p (a b) -> p a b", a=DC)[:, rr, :],
                                    wt[e, rr * 128:(rr + 1) * 128, :]).then_inc(h, 16)
                if e == 0:
                    g.wait_ge(BC.get("m_v"), b0["m_v"] + 1)
                for stile in range(NST):
                    kgl = e * NST + stile
                    slot = kgl % 2
                    if kgl >= 2:
                        g.wait_ge(BC.get("m_tr"), bigb["m_tr"] + (kgl - 1) * DC)
                    h, v = BC.inc("m_gd%d" % slot, 16)
                    g.indirect_dma_start(xgr[:, slot, :], None, dd["xn"][:, :],
                                         IndirectOffsetOnAxis(toki[:, e * ESTC + stile:e * ESTC + stile + 1], 0)
                                         ).then_inc(h, 16)
                g.wait_ge(BC.get("m_mm"), mm0 + IC * NTC)
                for rr in range(IC):
                    h, v = BC.inc("m_wl0", 16)
                    g.dma_start(wun[:, 0, :].rearrange("p (a b) -> p a b", a=IC)[:, rr, :],
                                dd["ew_d"][e, rr * 128:(rr + 1) * 128, :]).then_inc(h, 16)
                ef0 = bigb["m_ef"] + e * ef_pe
                for stile in range(NST):
                    slot = (e * NST + stile) % 2
                    if stile == 0 and e > 0:
                        g.wait_ge(BC.get("m_sd"), bigb["m_sd"] + 16 * NST * e)
                    g.wait_ge(BC.get("m_ef"), ef0 + IC * NTC + (stile + 1) * NJC)
                    h, v = BC.inc("m_sd", 16)
                    g.indirect_dma_start(dd["moe"][:, :],
                                         IndirectOffsetOnAxis(tsci[:, e * ESTC + stile:e * ESTC + stile + 1], 0),
                                         eout[:, slot, :], None,
                                         compute_op=OP.add).then_inc(h, 16)
            g.wait_ge(BC.get("m_sd"), BC.v("m_sd"))

        def big_te(te):
            for e in range(E):
                mm0 = bigb["m_mm"] + e * mm_pe
                tr0 = bigb["m_tr"] + e * tr_pe
                ef0 = bigb["m_ef"] + e * ef_pe
                sl0 = bigb["m_sl"] + e * sl_pe
                for stile in range(NST):
                    kgl = e * NST + stile
                    slot = kgl % 2
                    te.wait_ge(BC.get("m_gd%d" % slot), bigb["m_gd%d" % slot] + 16 * (kgl // 2 + 1))
                    for dc in range(DC):
                        kk = kgl * DC + dc
                        t = kk % 2
                        if kk >= 2:
                            te.wait_ge(BC.get("m_tf"), bigb["m_tf"] + kk - 1)
                        h, v = BC.inc("m_tr", 1)
                        te.transpose(self.psT[t][:, :], xgr[:, slot, dc * 128:(dc + 1) * 128],
                                     self.ident[:]).then_inc(h, 1)
                te.wait_ge(BC.get("m_wl0"), bigb["m_wl0"] + 16 * (DC + IC) * e + 16 * DC)
                te.wait_ge(BC.get("m_tf"), bigb["m_tf"] + (e * NST + NST) * DC)
                k = 0
                for ic in range(IC):
                    for tc in range(NTC):
                        n = min(NT, cpad - tc * NT)
                        ps = [self.psA, self.psB][k % 2]
                        if k >= 2:
                            te.wait_ge(BC.get("m_sl"), sl0 + k - 1)
                        elif e > 0:
                            te.wait_ge(BC.get("m_ef"), ef0)
                        h, v = BC.inc("m_mm", 1)
                        for dc in range(DC):
                            mmi = te.matmul(ps[:, 0:n],
                                            wun[:, 0, :].rearrange("p (a b) -> p a b", a=DC)[:, dc, ic * 128:(ic + 1) * 128],
                                            xgfm[:, dc, tc * NT:tc * NT + n],
                                            start=(dc == 0), stop=(dc == DC - 1))
                        mmi.then_inc(h, 1)
                        k += 1
                te.wait_ge(BC.get("m_wl1"), bigb["m_wl1"] + 16 * DC * (e + 1))
                te.wait_ge(BC.get("m_sl"), sl0 + IC * NTC)
                k2 = 0
                for ic in range(IC):
                    for tc in range(NTC):
                        n = min(NT, cpad - tc * NT)
                        ps = [self.psA, self.psB][(k + k2) % 2]
                        if k2 >= 2:
                            te.wait_ge(BC.get("m_ef"), ef0 + k2 - 1)
                        h, v = BC.inc("m_mm", 1)
                        for dc in range(DC):
                            mmi = te.matmul(ps[:, 0:n],
                                            wun[:, 1, :].rearrange("p (a b) -> p a b", a=DC)[:, dc, ic * 128:(ic + 1) * 128],
                                            xgfm[:, dc, tc * NT:tc * NT + n],
                                            start=(dc == 0), stop=(dc == DC - 1))
                        mmi.then_inc(h, 1)
                        k2 += 1
                te.wait_ge(BC.get("m_wl0"), bigb["m_wl0"] + 16 * (DC + IC) * (e + 1))
                te.wait_ge(BC.get("m_ef"), ef0 + IC * NTC)
                k3 = 0
                for stile in range(NST):
                    for jc in range(NJC):
                        ps = [self.psA, self.psB][(k + k2 + k3) % 2]
                        if k3 >= 2:
                            te.wait_ge(BC.get("m_ef"), ef0 + IC * NTC + k3 - 1)
                        h, v = BC.inc("m_mm", 1)
                        for ic in range(IC):
                            mmi = te.matmul(ps[:, 0:NT],
                                            msb[:, ic, stile * 128:(stile + 1) * 128],
                                            wun[:, 0, :].rearrange("p (a b) -> p a b", a=IC)[:, ic, jc * NT:(jc + 1) * NT],
                                            start=(ic == 0), stop=(ic == IC - 1))
                        mmi.then_inc(h, 1)
                        k3 += 1

        def big_sc(sc):
            for e in range(E):
                mm0 = bigb["m_mm"] + e * mm_pe
                k = 0
                for ic in range(IC):
                    for tc in range(NTC):
                        n = min(NT, cpad - tc * NT)
                        ps = [self.psA, self.psB][k % 2]
                        sc.wait_ge(BC.get("m_mm"), mm0 + k + 1)
                        h, v = BC.inc("m_sl", 1)
                        sc.activation(gsil[:, ic, tc * NT:tc * NT + n], ps[:, 0:n],
                                      AF.Silu).then_inc(h, 1)
                        k += 1

        def big_ve(ve):
            for e in range(E):
                mm0 = bigb["m_mm"] + e * mm_pe
                tr0 = bigb["m_tr"] + e * tr_pe
                sl0 = bigb["m_sl"] + e * sl_pe
                sd0 = bigb["m_sd"] + e * sd_pe * 16
                for stile in range(NST):
                    for dc in range(DC):
                        kgl = e * NST + stile
                        kk = kgl * DC + dc
                        t = kk % 2
                        ve.wait_ge(BC.get("m_tr"), bigb["m_tr"] + kk + 1)
                        if e > 0 and stile == 0 and dc == 0:
                            ve.wait_ge(BC.get("m_mm"), mm0 - NST * NJC)
                        h, v = BC.inc("m_tf", 1)
                        ve.tensor_copy(xgfm[:, dc, stile * 128:(stile + 1) * 128],
                                       self.psT[t][:, :]).then_inc(h, 1)
                j = 0
                for ic in range(IC):
                    for tc in range(NTC):
                        n = min(NT, cpad - tc * NT)
                        ps = [self.psA, self.psB][(IC * NTC + j) % 2]
                        ve.wait_ge(BC.get("m_mm"), mm0 + IC * NTC + j + 1)
                        ve.wait_ge(BC.get("m_sl"), sl0 + j + 1)
                        h, v = BC.inc("m_ef", 1)
                        ve.tensor_tensor(msb[:, ic, tc * NT:tc * NT + n], ps[:, 0:n],
                                         gsil[:, ic, tc * NT:tc * NT + n], op=OP.mult).then_inc(h, 1)
                        j += 1
                j3 = 0
                for stile in range(NST):
                    slot = (e * NST + stile) % 2
                    if e * NST + stile >= 2:
                        ve.wait_ge(BC.get("m_sd"), bigb["m_sd"] + 16 * (e * NST + stile - 1))
                    for jc in range(NJC):
                        ps = [self.psA, self.psB][(2 * IC * NTC + j3) % 2]
                        ve.wait_ge(BC.get("m_mm"), mm0 + 2 * IC * NTC + j3 + 1)
                        h, v = BC.inc("m_ef", 1)
                        ve.tensor_scalar(eout[:, slot, jc * NT:(jc + 1) * NT], ps[:, 0:NT],
                                         etw[:, e * ESTC + stile, 0:1].opt(), None,
                                         op0=OP.mult).then_inc(h, 1)
                        j3 += 1
        self.hot_block(hc, dict(gp=big_gp, te=big_te, sc=big_sc, ve=big_ve))
    st.close()


Builder.ph_moe = _ph_moe



_PROGRAM_CACHE = {}


def kernel(L0_attn_x, L0_attn_wq, L0_attn_bq, L0_attn_wk, L0_attn_bk,
           L0_attn_wv, L0_attn_bv, L0_attn_wo, L0_attn_bo,
           L0_ffn_gate_w, L0_ffn_w_gate, L0_ffn_w_up, L0_ffn_w_down,
           L0_ffn_sw_gate, L0_ffn_sw_up, L0_ffn_sw_down):
    cfg = full_cfg()
    NC = cfg["NCORES"]
    NB = cfg["NB"]
    B, S, D = cfg["B"], cfg["S"], cfg["D"]
    key = "full"
    if key not in _PROGRAM_CACHE:
        _PROGRAM_CACHE[key] = build_program(cfg)
    nc, b = _PROGRAM_CACHE[key]
    T = b.TOWN

    asf = lambda a: np.ascontiguousarray(np.asarray(a, dtype=np.float32))
    x = asf(L0_attn_x).reshape(B * S, D)
    w_qkvo = np.stack([asf(L0_attn_wq), asf(L0_attn_wk), asf(L0_attn_wv), asf(L0_attn_wo)])
    b_qkvo = np.stack([asf(L0_attn_bq), asf(L0_attn_bk), asf(L0_attn_bv), asf(L0_attn_bo)])
    consts = host_constants_inmap(cfg)
    xpre = np.ascontiguousarray(x[0:b.TPRE])
    common = dict(
        w_qkvo=w_qkvo, b_qkvo=b_qkvo, gate_w=asf(L0_ffn_gate_w),
        sw_g=asf(L0_ffn_sw_gate), sw_u=asf(L0_ffn_sw_up), sw_d=asf(L0_ffn_sw_down),
        ew_g=asf(L0_ffn_w_gate), ew_u=asf(L0_ffn_w_up), ew_d=asf(L0_ffn_w_down),
        x_pre=xpre, **consts,
    )
    xb = x.reshape(B, S, D)
    in_maps = []
    for c_ in range(NC):
        im = dict(common)
        im["cid"] = np.array([[c_]], np.int32)
        im["x_own"] = np.ascontiguousarray(
            np.concatenate([xb[c_], xb[c_ + NC]], 0))
        in_maps.append(im)
    res = run_bass_kernel_spmd(nc, in_maps, core_ids=list(range(NC)))
    out = np.empty((B, S, D), np.float32)
    for c_ in range(NC):
        o = res.results[c_]["out"]
        out[c_] = o[0:S]
        out[c_ + NC] = o[S:2 * S]
    return out


# version 03
# speedup vs baseline: 1.4418x; 1.1220x over previous
import math
import numpy as np
import concourse.bass as bass
import concourse.mybir as mybir
from concourse.bass import IndirectOffsetOnAxis
from concourse.bass_utils import run_bass_kernel_spmd

F32 = mybir.dt.float32
BF16 = mybir.dt.bfloat16
I32 = mybir.dt.int32
I16 = mybir.dt.int16
AF = mybir.ActivationFunctionType
OP = mybir.AluOpType
AX = mybir.AxisListType


def full_cfg():
    return dict(
        D=2048, HD=128, NH=16, S=2048, B=16,
        E=16, TOPK=4, CAP=1024, I=1408, ISH=2816,
        NCORES=8, NB=2,
        TH=5120,
        QT=512,
        KG=512,
        NT=512,
        HOT={0: dict(t0=0, cpad=640, hlen=2048),
             1: dict(t0=2048, cpad=640, hlen=2048),
             2: dict(t0=4096, cpad=256, hlen=1024)},
        LN_EPS=1e-5,
    )


def mini_cfg():
    return dict(
        D=256, HD=128, NH=2, S=256, B=4,
        E=16, TOPK=4, CAP=24, I=256, ISH=384,
        NCORES=2, NB=2,
        TH=384,
        QT=128, KG=256, NT=128,
        HOT={0: dict(t0=0, cpad=32, hlen=256), 1: dict(t0=256, cpad=16, hlen=384)},
        LN_EPS=1e-5,
    )



def build_constants(cfg):
    QT = cfg["QT"]
    c = {}
    c["identity"] = np.eye(128, dtype=np.float32)
    rs = list(range(-(cfg["S"] - QT) // 128, QT // 128))
    ki = np.arange(128)[:, None]
    qj = np.arange(QT)[None, :]
    bt = []
    for r in rs:
        v = (ki - qj + 128 * r).astype(np.float32)
        v = np.where(v <= 0, v, -1e30).astype(np.float32)
        bt.append(v)
    c["btiles"] = np.stack(bt, 0)
    c["brs"] = rs
    c["ltri"] = np.triu(np.ones((128, 128), np.float32), 1)
    c["ones128"] = np.ones((128, 1), np.float32)
    c["iota16"] = np.arange(cfg["E"], dtype=np.float32)[None, :]
    H = cfg["NH"]
    start = 2.0 ** (-8.0 / H)
    c["slopes"] = np.array([start * start ** i for i in range(H)], np.float32)
    return c



class Sems:

    def __init__(self, nc, ctx):
        self.nc = nc
        self.ctx = ctx
        self.val = {}
        self.h = {}

    def get(self, name):
        if name not in self.h:
            self.h[name] = self.ctx.enter_context(self.nc.semaphore(name))
            self.val[name] = 0
        return self.h[name]

    def inc(self, name, by):
        self.get(name)
        self.val[name] += by
        return self.h[name], self.val[name]


def ceil_div(a, b):
    return (a + b - 1) // b



class Builder:
    def __init__(self, nc, cfg, stack):
        self.nc = nc
        self.cfg = cfg
        self.stack = stack
        self.sems = Sems(nc, stack)
        c = cfg
        self.TOWN = c["NB"] * c["S"]
        self.TPRE = max((h["t0"] for h in c["HOT"].values()), default=0)
        self.DC = c["D"] // 128
        self.IC = c["I"] // 128
        self.ISC = c["ISH"] // 128
        self.CPADMAX = max((h["cpad"] for h in c["HOT"].values()), default=128)
        self.ESTMAX = ceil_div(self.CPADMAX + 1, 128) * 128
        self.ETROWS = c["E"] * self.ESTMAX
        self.dram = {}
        self.declare_io()
        self.cid_regs = {}
        self._alloc_psum()

    def sb(self, name, shape, dtype):
        return self.stack.enter_context(self.nc.sbuf_tensor(name, shape, dtype))

    def ps(self, name, shape, dtype=F32):
        return self.stack.enter_context(self.nc.psum_tensor(name, shape, dtype))

    def declare_io(self):
        nc, c = self.nc, self.cfg
        D, E, I, ISH, S, NB = c["D"], c["E"], c["I"], c["ISH"], c["S"], c["NB"]
        QT = c["QT"]
        T = self.TOWN
        dd = self.dram
        ein = lambda n, sh, dt=F32: nc.dram_tensor(n, sh, dt, kind="ExternalInput").ap()
        intl = lambda n, sh, dt: nc.dram_tensor(n, sh, dt, kind="Internal").ap()
        dd["cid"] = ein("cid", [1, 1], I32)
        dd["x_own"] = ein("x_own", [T, D])
        if self.TPRE:
            dd["x_pre"] = ein("x_pre", [self.TPRE, D])
        dd["w_qkvo"] = ein("w_qkvo", [4, D, D])
        dd["b_qkvo"] = ein("b_qkvo", [4, D])
        dd["gate_w"] = ein("gate_w", [E, D])
        dd["sw_g"] = ein("sw_g", [D, ISH])
        dd["sw_u"] = ein("sw_u", [D, ISH])
        dd["sw_d"] = ein("sw_d", [ISH, D])
        dd["ew_g"] = ein("ew_g", [E, D, I])
        dd["ew_u"] = ein("ew_u", [E, D, I])
        dd["ew_d"] = ein("ew_d", [E, I, D])
        nr = c["S"] // 128
        dd["c_bt"] = ein("c_bt", [nr, 128, QT])
        dd["c_id"] = ein("c_id", [128, 128])
        dd["c_ltri"] = ein("c_ltri", [128, 128])
        dd["c_iota16"] = ein("c_iota16", [128, E])
        dd["c_slopes"] = ein("c_slopes", [1, c["NH"]])
        dd["out"] = nc.dram_tensor("out", [T, D], F32, kind="ExternalOutput").ap()
        dd["xn"] = intl("xn_h", [T, D], BF16)
        dd["xnfm"] = intl("xnfm_h", [NB, D, S], BF16)
        dd["qf"] = intl("qf_h", [NB, D, S], BF16)
        dd["kf"] = intl("kf_h", [NB, D, S], BF16)
        dd["vt"] = intl("vt_h", [NB, S, D], BF16)
        dd["aof"] = intl("aof_h", [NB, D, S], BF16)
        dd["oattn"] = intl("oattn_h", [T, D], BF16)
        dd["gsh"] = intl("gsh_h", [NB, ISH, S], BF16)
        dd["msh"] = intl("msh_h", [NB, ISH, S], BF16)
        dd["lown"] = intl("lown_h", [T, E], F32)
        dd["moe"] = intl("moe_h", [T + self.CPADMAX, D], F32)
        dd["etab"] = intl("etab_h", [self.ETROWS, 2], F32)

    def barrier(self, tag):
        return

    def ph_setup(self):
        nc, c, dd = self.nc, self.cfg, self.dram
        D, E, NH = c["D"], c["E"], c["NH"]
        DC = self.DC
        NR = c["S"] // 128
        QT = c["QT"]
        self.cid_sb = self.sb("cid_sb", [1, 1], I32)
        self.ident = self.sb("ident", [128, 128], BF16)
        self.ltri = self.sb("ltri", [128, 128], BF16)
        self.iota16 = self.sb("iota16", [128, E], F32)
        self.slopes = self.sb("slopes", [1, NH], F32)
        self.bias_sb = self.sb("bias_sb", [128, 4, DC], F32)
        self.bqs_sb = self.sb("bqs_sb", [128, DC], F32)
        self.ones1 = self.sb("ones1", [1, 128], F32)
        self.zero128 = self.sb("zero128", [128, 2048 if c["D"] >= 2048 else c["D"]], F32)
        with nc.Block() as bl:
            @bl.sync
            def _(sy):
                h, v = self.sems.inc("su_d", 16); sy.dma_start(self.cid_sb[:], dd["cid"][:]).then_inc(h, 16)
                h, v = self.sems.inc("su_d", 16); sy.dma_start(self.iota16[:], dd["c_iota16"][:]).then_inc(h, 16)
                h, v = self.sems.inc("su_d", 16); sy.dma_start(self.slopes[:], dd["c_slopes"][:]).then_inc(h, 16)
                h, v = self.sems.inc("su_d", 16)
                sy.dma_start(self.bias_sb[:], bass.AP(dd["b_qkvo"].tensor, 0,
                             [[1, 128], [D, 4], [128, DC]]),
                             allow_slow_non_contiguous=True).then_inc(h, 16)

            @bl.gpsimd
            def _(g):
                h, v = self.sems.inc("su_g", 16); g.dma_start(self.ident[:], dd["c_id"][:]).then_inc(h, 16)
                h, v = self.sems.inc("su_g", 16); g.dma_start(self.ltri[:], dd["c_ltri"][:]).then_inc(h, 16)
            @bl.vector
            def _(ve):
                ve.wait_ge(self.sems.get("su_d"), self.sems.val["su_d"])
                ve.memset(self.ones1[:], 1.0)
                ve.memset(self.zero128[:], 0.0)
                ve.tensor_scalar(self.bqs_sb[:], self.bias_sb[:, 0, :],
                                 1.0 / math.sqrt(c["HD"]), None, op0=OP.mult)
                h, v = self.sems.inc("su_v", 1)
                ve.drain().then_inc(h, 1)
        with nc.Block() as bl:
            def mk(engname, dec):
                def body(eng):
                    eng.wait_ge(self.sems.get("su_d"), self.sems.val["su_d"])
                    reg = eng.alloc_register("cidreg_" + engname)
                    eng.load(reg, self.cid_sb[0:1, 0:1])
                    self.cid_regs[engname] = reg
                dec(body)
            mk("sy", bl.sync); mk("sc", bl.scalar); mk("ve", bl.vector)
            mk("te", bl.tensor); mk("gp", bl.gpsimd)
        with nc.Block() as bl:
            @bl.gpsimd
            def _(g):
                g.wait_ge(self.sems.get("su_g"), self.sems.val["su_g"])
        self.barrier("setup")

    def ph_ln(self):
        import contextlib
        nc, c, dd = self.nc, self.cfg, self.dram
        D = c["D"]
        NTILE = self.TOWN // 128
        st_ln = contextlib.ExitStack()
        lsb = lambda n, sh, dt: st_ln.enter_context(nc.sbuf_tensor(n, sh, dt))
        xa = lsb("ln_x", [128, 2, D], F32)
        xn = lsb("ln_xn", [128, 2, D], BF16)
        st6 = lsb("ln_st", [128, 2, ceil_div(D, 512) * 6], F32)
        mv = lsb("ln_mv", [128, 2, 2], F32)
        sd = lsb("ln_sd", [128, 2, 1], F32)
        inv = lsb("ln_inv", [128, 2, 1], F32)
        ngm = lsb("ln_ngm", [128, 2, 1], F32)
        nst = ceil_div(D, 512)
        S = self.sems
        with nc.Block() as bl:
            @bl.sync
            def _(sy):
                for i in range(NTILE):
                    b = i % 2
                    if i >= 2:
                        sy.wait_ge(S.get("ln_c"), i - 1)
                    h, v = S.inc("ln_l%d" % b, 16)
                    sy.dma_start(xa[:, b, :], dd["x_own"][i * 128:(i + 1) * 128, :]).then_inc(h, 16)
            @bl.vector
            def _(ve):
                for i in range(NTILE):
                    b = i % 2
                    ve.wait_ge(S.get("ln_l%d" % b), 16 * (i // 2 + 1))
                    for j in range(nst):
                        n = min(512, D - j * 512)
                        ve.bn_stats(st6[:, b, j * 6:(j + 1) * 6], xa[:, b, j * 512:j * 512 + n])
                    ve.drain()
                    ve.bn_aggr(mv[:, b, :], st6[:, b, :])
                    ve.drain()
                    h, v = S.inc("ln_s", 1)
                    ve.tensor_scalar(mv[:, b, 1:2], mv[:, b, 1:2], c["LN_EPS"], None, op0=OP.add).then_inc(h, 1)
                    ve.wait_ge(S.get("ln_sd"), i + 1)
                    if i >= 2:
                        ve.wait_ge(S.get("ln_st%d" % b), 16 * ((i - 2) // 2 + 1))
                    ve.reciprocal(inv[:, b, :], sd[:, b, :])
                    ve.drain()
                    ve.tensor_scalar(ngm[:, b, :], mv[:, b, 0:1], inv[:, b, :].opt(), -1.0,
                                     op0=OP.mult, op1=OP.mult)
                    ve.drain()
                    h, v = S.inc("ln_c", 1)
                    ve.tensor_scalar(xn[:, b, :], xa[:, b, :], inv[:, b, :].opt(),
                                     ngm[:, b, :].opt(), op0=OP.mult, op1=OP.add).then_inc(h, 1)
            @bl.scalar
            def _(sc):
                for i in range(NTILE):
                    b = i % 2
                    sc.wait_ge(S.get("ln_s"), i + 1)
                    h, v = S.inc("ln_sd", 1)
                    sc.activation(sd[:, b, :], mv[:, b, 1:2], AF.Sqrt).then_inc(h, 1)
                    sc.wait_ge(S.get("ln_c"), i + 1)
                    h, v = S.inc("ln_st%d" % b, 16)
                    sc.dma_start(dd["xn"][i * 128:(i + 1) * 128, :], xn[:, b, :]).then_inc(h, 16)
                sc.wait_ge(S.get("ln_st0"), S.val["ln_st0"])
                sc.wait_ge(S.get("ln_st1"), S.val["ln_st1"])
            @bl.tensor
            def _(te):
                pass
            @bl.gpsimd
            def _(g):
                pass
        st_ln.close()
        self.barrier("ln")

def build_program(cfg, phases=("setup", "ln", "att", "route", "moe", "shared")):
    import contextlib
    nc = bass.Bass("TRN2", target_bir_lowering=False)
    stack = contextlib.ExitStack()
    b = Builder(nc, cfg, stack)
    with stack:
        if "setup" in phases:
            b.ph_setup()
            b.ph_prezero()
        if "ln" in phases:
            b.ph_ln()
        if "att" in phases:
            b.ph_att()
        if "route" in phases:
            b.ph_route()
        if "moe" in phases:
            b.ph_moe()
        if "shared" in phases:
            b.ph_shared()
        b.finalize()
    return nc, b


def host_constants_inmap(cfg):
    c = build_constants(cfg)
    return {
        "c_bt": c["btiles"],
        "c_id": c["identity"],
        "c_ltri": c["ltri"],
        "c_iota16": np.tile(c["iota16"], (128, 1)).astype(np.float32),
        "c_slopes": c["slopes"][None, :],
    }


def _finalize_stub(self):
    nc = self.nc
    with nc.Block() as bl:
        @bl.sync
        def _(sy):
            for q_ in range(4):
                n = "out_d%d" % q_
                if self.sems.val.get(n):
                    sy.wait_ge(self.sems.get(n), self.sems.val[n])


Builder.finalize = _finalize_stub


def _alloc_psum(self):
    if hasattr(self, "psA"):
        return
    self.psA = self.ps("psA", [128, 512])
    self.psB = self.ps("psB", [128, 512])
    self.psC = self.ps("psC", [128, 512])
    self.psE = self.ps("psE", [128, 512])
    self.psD = self.ps("psD", [128, 512])
    self.psF = self.ps("psF", [128, 512])
    self.psT = [self.ps("psT0", [128, 128], BF16), self.ps("psT1", [128, 128], BF16)]


def _ph_att(self):
    import contextlib
    nc, c, dd = self.nc, self.cfg, self.dram
    D, S, NB, NH, HD = c["D"], c["S"], c["NB"], c["NH"], c["HD"]
    QT, KG, NT = c["QT"], c["KG"], c["NT"]
    DC = self.DC
    NQT = QT // 128
    S_ = self.sems
    self._alloc_psum()
    rs0 = -(S - QT) // 128
    start = 2.0 ** (-8.0 / NH)
    slopes = [start * start ** i for i in range(NH)]
    S_ = self.sems

    NR = S // 128
    st_att = contextlib.ExitStack()
    self.btiles = st_att.enter_context(nc.sbuf_tensor("btiles", [128, NR, QT], F32))
    self.bvrep = st_att.enter_context(nc.sbuf_tensor("bvrep", [128, D], F32))
    self.borep = st_att.enter_context(nc.sbuf_tensor("borep", [128, D], F32))
    st_brow = contextlib.ExitStack()
    self.brow = st_brow.enter_context(nc.sbuf_tensor("brow", [1, 2, D], F32))
    with nc.Block() as bl:
        @bl.sync
        def _(sy):
            h, v = S_.inc("su_d", 16)
            sy.dma_start(self.btiles[:], dd["c_bt"].rearrange("n p q -> p n q")).then_inc(h, 16)
            h, v = S_.inc("su_d", 16)
            sy.dma_start(self.brow[:], bass.AP(dd["b_qkvo"].tensor, 2 * D,
                         [[2 * D, 1], [D, 2], [1, D]])).then_inc(h, 16)
            sy.wait_ge(h, v)
    base_mm = S_.val.get("su_mm", 0)
    base_v2 = S_.val.get("su_ve2", 0)
    with nc.Block() as bl:
        @bl.tensor
        def _(te):
            te.wait_ge(S_.get("su_d"), S_.val["su_d"])
            k = 0
            for w, dst in ((0, self.bvrep), (1, self.borep)):
                for jc in range(ceil_div(D, 512)):
                    n = min(512, D - jc * 512)
                    k += 1
                    if k > 1:
                        te.wait_ge(S_.get("su_ve2"), base_v2 + k - 1)
                    h, v = S_.inc("su_mm", 1)
                    te.matmul(self.psF[:, 0:n], self.ones1[:],
                              self.brow[:, w, jc * 512:jc * 512 + n],
                              start=True, stop=True).then_inc(h, 1)
        @bl.vector
        def _(ve):
            k = 0
            for w, dst in ((0, self.bvrep), (1, self.borep)):
                for jc in range(ceil_div(D, 512)):
                    n = min(512, D - jc * 512)
                    k += 1
                    ve.wait_ge(S_.get("su_mm"), base_mm + k)
                    h, v = S_.inc("su_ve2", 1)
                    ve.tensor_copy(dst[:, jc * 512:jc * 512 + n], self.psF[:, 0:n]).then_inc(h, 1)
    st_brow.close()

    for nb in range(NB):
        tag = "a%d" % nb
        st_b = contextlib.ExitStack()
        xnfm = st_b.enter_context(nc.sbuf_tensor("xnfm%d" % nb, [128, DC, S], BF16))
        xt = st_b.enter_context(nc.sbuf_tensor("xt%d" % nb, [128, 2, D], BF16))
        BA = {n: S_.val.get(n, 0) for n in ("t_xl0", "t_xl1", "t_tr", "t_trf", "t_fmst")}
        with nc.Block() as bl:
            @bl.sync
            def _(sy):
                for i in range(S // 128):
                    b = i % 2
                    if i >= 2:
                        sy.wait_ge(S_.get("t_trf"), BA["t_trf"] + (i - 2) * DC + DC)
                    h, v = S_.inc("t_xl%d" % b, 16)
                    sy.dma_start(xt[:, b, :], dd["xn"][nb * S + i * 128: nb * S + (i + 1) * 128, :]).then_inc(h, 16)
            @bl.tensor
            def _(te):
                TtlA = [self.psT[0][:, :], self.psT[1][:, :],
                        self.psC.bitcast(BF16)[:, 0:128], self.psE.bitcast(BF16)[:, 0:128],
                        self.psD.bitcast(BF16)[:, 0:128], self.psF.bitcast(BF16)[:, 0:128]]
                for i in range(S // 128):
                    b = i % 2
                    te.wait_ge(S_.get("t_xl%d" % b), BA["t_xl%d" % b] + 16 * (i // 2 + 1))
                    for dc in range(DC):
                        idx = i * DC + dc
                        if idx >= 6:
                            te.wait_ge(S_.get("t_trf"), BA["t_trf"] + idx - 5)
                        h, v = S_.inc("t_tr", 1)
                        te.transpose(TtlA[idx % 6], xt[:, b, dc * 128:(dc + 1) * 128], self.ident[:]).then_inc(h, 1)
            @bl.vector
            def _(ve):
                TtlA = [self.psT[0][:, :], self.psT[1][:, :],
                        self.psC.bitcast(BF16)[:, 0:128], self.psE.bitcast(BF16)[:, 0:128],
                        self.psD.bitcast(BF16)[:, 0:128], self.psF.bitcast(BF16)[:, 0:128]]
                for i in range(S // 128):
                    for dc in range(DC):
                        idx = i * DC + dc
                        ve.wait_ge(S_.get("t_tr"), BA["t_tr"] + idx + 1)
                        h, v = S_.inc("t_trf", 1)
                        ve.tensor_copy(xnfm[:, dc, i * 128:(i + 1) * 128], TtlA[idx % 6]).then_inc(h, 1)
            @bl.scalar
            def _(sc):
                sc.wait_ge(S_.get("t_trf"), BA["t_trf"] + (S // 128) * DC)
                for dc in range(DC):
                    h, v = S_.inc("t_fmst", 16)
                    sc.dma_start(dd["xnfm"][nb, dc * 128:(dc + 1) * 128, :], xnfm[:, dc, :]).then_inc(h, 16)
                sc.wait_ge(S_.get("t_fmst"), S_.val["t_fmst"])
            @bl.gpsimd
            def _(g):
                pass

        w_sb = st_b.enter_context(nc.sbuf_tensor("wsb%d" % nb, [128, DC, D], BF16))
        ev = st_b.enter_context(nc.sbuf_tensor("ev%d" % nb, [128, 4, NT], BF16))
        projs = [("q", 0, dd["qf"]), ("k", 1, dd["kf"]), ("v", 2, dd["vt"])]
        prev_mm = None
        for pname, wi, dst in projs:
            BP = {n: S_.val.get(n, 0) for n in
                  ("p_wl", "p_mm", "p_ef", "p_st0", "p_st1", "p_st2", "p_st3")}
            with nc.Block() as bl:
                pmm = prev_mm
                @bl.gpsimd
                def _(g, pmm=pmm):
                    if pmm is not None:
                        g.wait_ge(S_.get(pmm), S_.val[pmm])
                    for dc in range(DC):
                        h, v = S_.inc("p_wl", 16)
                        g.dma_start(w_sb[:, dc, :], dd["w_qkvo"][wi, dc * 128:(dc + 1) * 128, :]).then_inc(h, 16)
                @bl.tensor
                def _(te):
                    te.wait_ge(S_.get("p_wl"), BP["p_wl"] + 16 * DC)
                    nmm = 0
                    if pname in ("q", "k"):
                        for oc in range(DC):
                            for tg in range(S // NT):
                                pb = (oc * (S // NT) + tg) % 2
                                ps = [self.psA, self.psB][pb]
                                if oc * (S // NT) + tg >= 2:
                                    te.wait_ge(S_.get("p_ef"), BP["p_ef"] + oc * (S // NT) + tg - 2 + 1)
                                h, v = S_.inc("p_mm", 1)
                                for dc in range(DC):
                                    mmi = te.matmul(ps[:, 0:NT], w_sb[:, dc, oc * 128:(oc + 1) * 128],
                                                    xnfm[:, dc, tg * NT:(tg + 1) * NT],
                                                    start=(dc == 0), stop=(dc == DC - 1))
                                    if dc == DC - 1:
                                        mmi.then_inc(h, 1)
                    else:
                        for tt in range(S // 128):
                            for jc in range(D // NT):
                                pb = (tt * (D // NT) + jc) % 2
                                ps = [self.psA, self.psB][pb]
                                if tt * (D // NT) + jc >= 2:
                                    te.wait_ge(S_.get("p_ef"), BP["p_ef"] + tt * (D // NT) + jc - 2 + 1)
                                h, v = S_.inc("p_mm", 1)
                                for dc in range(DC):
                                    mmi = te.matmul(ps[:, 0:NT], xnfm[:, dc, tt * 128:(tt + 1) * 128],
                                                    w_sb[:, dc, jc * NT:(jc + 1) * NT],
                                                    start=(dc == 0), stop=(dc == DC - 1))
                                    if dc == DC - 1:
                                        mmi.then_inc(h, 1)
                if pname in ("q", "k"):
                    @bl.scalar
                    def _(sc):
                        k = 0
                        for oc in range(DC):
                            for tg in range(S // NT):
                                pb = k % 2
                                ps = [self.psA, self.psB][pb]
                                eb = k % 4
                                sc.wait_ge(S_.get("p_mm"), BP["p_mm"] + k + 1)
                                if k >= 4:
                                    sc.wait_ge(S_.get("p_st%d" % eb), BP["p_st%d" % eb] + 16 * ((k - 4) // 4 + 1))
                                scale = 1.0 / math.sqrt(HD) if pname == "q" else 1.0
                                bias = self.bqs_sb[:, oc:oc + 1] if pname == "q" else self.bias_sb[:, 1, oc:oc + 1]
                                h, v = S_.inc("p_ef", 1)
                                sc.activation(ev[:, eb, :], ps[:, 0:NT], AF.Identity,
                                              bias=bias, scale=scale).then_inc(h, 1)
                                k += 1
                    @bl.sync
                    def _(sy):
                        k = 0
                        for oc in range(DC):
                            for tg in range(S // NT):
                                eb = k % 4
                                sy.wait_ge(S_.get("p_ef"), BP["p_ef"] + k + 1)
                                h, v = S_.inc("p_st%d" % eb, 16)
                                sy.dma_start(dst[nb, oc * 128:(oc + 1) * 128, tg * NT:(tg + 1) * NT],
                                             ev[:, eb, :]).then_inc(h, 16)
                                k += 1
                        for q_ in range(4):
                            sy.wait_ge(S_.get("p_st%d" % q_), S_.val.get("p_st%d" % q_, 0))
                else:
                    @bl.vector
                    def _(ve):
                        k = 0
                        for tt in range(S // 128):
                            for jc in range(D // NT):
                                pb = k % 2
                                ps = [self.psA, self.psB][pb]
                                eb = k % 4
                                ve.wait_ge(S_.get("p_mm"), BP["p_mm"] + k + 1)
                                if k >= 4:
                                    ve.wait_ge(S_.get("p_st%d" % eb), BP["p_st%d" % eb] + 16 * ((k - 4) // 4 + 1))
                                h, v = S_.inc("p_ef", 1)
                                ve.tensor_tensor(ev[:, eb, :], ps[:, 0:NT],
                                                 self.bvrep[:, jc * NT:(jc + 1) * NT], op=OP.add).then_inc(h, 1)
                                k += 1
                    @bl.scalar
                    def _(sc):
                        k = 0
                        for tt in range(S // 128):
                            for jc in range(D // NT):
                                eb = k % 4
                                sc.wait_ge(S_.get("p_ef"), BP["p_ef"] + k + 1)
                                h, v = S_.inc("p_st%d" % eb, 16)
                                sc.dma_start(dst[nb, tt * 128:(tt + 1) * 128, jc * NT:(jc + 1) * NT],
                                             ev[:, eb, :]).then_inc(h, 16)
                                k += 1
                        for q_ in range(4):
                            sc.wait_ge(S_.get("p_st%d" % q_), S_.val.get("p_st%d" % q_, 0))
            prev_mm = "p_mm"
        st_b.close()
        self.barrier(tag + "proj")
        self._att_inner(nb)
        self._att_wo(nb)
        self.barrier(tag + "end")
    st_att.close()


Builder.ph_att = _ph_att
Builder._alloc_psum = _alloc_psum


def _att_inner(self, nb):
    import contextlib
    nc, c, dd = self.nc, self.cfg, self.dram
    D, S, NH, HD = c["D"], c["S"], c["NH"], c["HD"]
    QT, KG = c["QT"], c["KG"]
    DC = self.DC
    assert NH == DC and HD == 128
    NQT = QT // 128
    NKG = KG // 128
    S_ = self.sems
    rs0 = -(S - QT) // 128
    start = 2.0 ** (-8.0 / NH)
    slopes = [start * start ** i for i in range(NH)]

    st = contextlib.ExitStack()
    qsup = st.enter_context(nc.sbuf_tensor("qsup%d" % nb, [128, DC, QT], BF16))
    kgrp = st.enter_context(nc.sbuf_tensor("kgrp%d" % nb, [128, 2, DC, KG], BF16))
    vgrp = st.enter_context(nc.sbuf_tensor("vgrp%d" % nb, [128, 2, NKG, D], BF16))
    Psb = st.enter_context(nc.sbuf_tensor("Psb%d" % nb, [128, 4, QT], BF16))
    Sb = st.enter_context(nc.sbuf_tensor("Sbb%d" % nb, [128, 4, QT], F32))
    aoac = st.enter_context(nc.sbuf_tensor("aoac%d" % nb, [128, NQT, D], F32))
    aosc = st.enter_context(nc.sbuf_tensor("aosc%d" % nb, [128, 6, 128], BF16))
    aostg = st.enter_context(nc.sbuf_tensor("aostg%d" % nb, [128, 2, QT], BF16))
    rsb = st.enter_context(nc.sbuf_tensor("rsb%d" % nb, [128, NH * NQT], F32))
    onesb = st.enter_context(nc.sbuf_tensor("onesb%d" % nb, [128, 1], BF16))

    with nc.Block() as bl:
        @bl.vector
        def _(ve):
            ve.memset(onesb[:], 1.0)
            h, v = S_.inc("su_v", 1)
            ve.drain().then_inc(h, 1)

    for qi in range(S // QT):
        kvn = QT * (qi + 1)
        nkg = ceil_div(kvn, KG)
        steps = []
        for kg in range(nkg):
            kgn = min(KG, kvn - kg * KG)
            for h in range(NH):
                for kt in range(kgn // 128):
                    ktg = kg * NKG + kt
                    steps.append(dict(kg=kg, h=h, kt=kt, ktg=ktg,
                                      bidx=ktg - qi * NQT - rs0,
                                      gfirst=(kt == 0), glast=(kt == kgn // 128 - 1),
                                      kgn=kgn))
        ns = len(steps)
        groups = []
        for kg in range(nkg):
            for h in range(NH):
                groups.append((kg, h))
        ng = len(groups)
        gidx = {g: i for i, g in enumerate(groups)}
        for i, s in enumerate(steps):
            s["g"] = gidx[(s["kg"], s["h"])]
        glast_step = {}
        for i, s in enumerate(steps):
            glast_step[s["g"]] = i
        kg_last_step = {}
        for i, s in enumerate(steps):
            kg_last_step[s["kg"]] = i

        B = {n: S_.val.get(n, 0) for n in
             ("cq_l", "ck_l0", "ck_l1", "sc_mm", "scf", "expd", "pv", "ao_f", "rc_d",
              "as_r", "tr2", "tg_r", "st_ao", "tl_d", "su_v")}
        ntl = NH * NQT

        with nc.Block() as bl:
            @bl.sync
            def _(sy):
                if qi > 0:
                    sy.wait_ge(S_.get("sc_mm"), B["sc_mm"])
                h, v = S_.inc("cq_l", 16)
                sy.dma_start(qsup[:, :, :],
                             bass.AP(dd["qf"].tensor, nb * D * S + qi * QT,
                                     [[S, 128], [128 * S, DC], [1, QT]])).then_inc(h, 16)
                for kg in range(nkg):
                    kgn = min(KG, kvn - kg * KG)
                    ks = kg % 2
                    if qi > 0 and kg < 2:
                        sy.wait_ge(S_.get("pv"), B["pv"])
                    elif kg >= 2:
                        sy.wait_ge(S_.get("pv"), B["pv"] + kg_last_step[kg - 2] + 1)
                    h, v = S_.inc("ck_l%d" % ks, 32)
                    sy.dma_start(kgrp[:, ks, :, 0:kgn],
                                 bass.AP(dd["kf"].tensor, nb * D * S + kg * KG,
                                         [[S, 128], [128 * S, DC], [1, kgn]])).then_inc(h, 16)
                    sy.dma_start(vgrp[:, ks, 0:kgn // 128, :],
                                 bass.AP(dd["vt"].tensor, nb * S * D + kg * KG * D,
                                         [[D, 128], [128 * D, kgn // 128], [1, D]])).then_inc(h, 16)
            @bl.tensor
            def _(te):
                te.wait_ge(S_.get("su_v"), 1)
                te.wait_ge(S_.get("cq_l"), B["cq_l"] + 16)

                def emit_scores(i):
                    s = steps[i]
                    psS = [self.psC, self.psE][i % 2]
                    ks = s["kg"] % 2
                    te.wait_ge(S_.get("ck_l%d" % ks), B["ck_l%d" % ks] + 32 * (s["kg"] // 2 + 1))
                    if i >= 2:
                        te.wait_ge(S_.get("scf"), B["scf"] + i - 1)
                    h, v = S_.inc("sc_mm", 1)
                    te.matmul(psS[:, 0:QT], kgrp[:, s["kg"] % 2, s["h"], s["kt"] * 128:(s["kt"] + 1) * 128],
                              qsup[:, s["h"], :], start=True, stop=True).then_inc(h, 1)

                if B["rc_d"] > 0:
                    te.wait_ge(S_.get("rc_d"), B["rc_d"])
                for i in range(min(2, ns)):
                    emit_scores(i)
                for i, s in enumerate(steps):
                    te.wait_ge(S_.get("expd"), B["expd"] + i + 1)
                    g = s["g"]
                    pa = [self.psA, self.psB][g % 2]
                    if s["gfirst"] and g >= 2:
                        te.wait_ge(S_.get("ao_f"), B["ao_f"] + g - 1)
                    h, v = S_.inc("pv", 1)
                    for qt in range(NQT):
                        lp = Psb[:, i % 4, qt * 128:(qt + 1) * 128]
                        te.matmul(pa[:, qt * 128:(qt + 1) * 128], lp,
                                  vgrp[:, s["kg"] % 2, s["ktg"] - s["kg"] * NKG, s["h"] * HD:(s["h"] + 1) * HD],
                                  start=(s["gfirst"] and qt == 0),
                                  stop=(s["glast"] and qt == NQT - 1))
                        mmi = te.matmul(self.psD[:, s["h"] * NQT + qt:s["h"] * NQT + qt + 1], lp,
                                        onesb[:, :], start=(i == 0 and qt == 0),
                                        stop=(i == ns - 1 and qt == NQT - 1))
                    mmi.then_inc(h, 1)
                    if i + 2 < ns:
                        emit_scores(i + 2)
                Ttl = [self.psT[0][:, :], self.psT[1][:, :],
                       self.psC.bitcast(BF16)[:, 0:128], self.psE.bitcast(BF16)[:, 0:128],
                       self.psD.bitcast(BF16)[:, 0:128], self.psF.bitcast(BF16)[:, 0:128]]
                for j in range(ntl):
                    te.wait_ge(S_.get("as_r"), B["as_r"] + j + 1)
                    if j >= 6:
                        te.wait_ge(S_.get("tg_r"), B["tg_r"] + j - 5)
                    h, v = S_.inc("tr2", 1)
                    te.transpose(Ttl[j % 6], aosc[:, j % 6, :], self.ident[:]).then_inc(h, 1)
            @bl.vector
            def _(ve):
                for i, s in enumerate(steps):
                    pb = i % 2
                    psS = [self.psC, self.psE][pb]
                    ve.wait_ge(S_.get("sc_mm"), B["sc_mm"] + i + 1)
                    if i >= 4:
                        ve.wait_ge(S_.get("expd"), B["expd"] + i - 3)
                    h, v = S_.inc("scf", 1)
                    ve.scalar_tensor_tensor(Sb[:, i % 4, :], self.btiles[:, s["bidx"], 0:QT],
                                            float(slopes[s["h"]]), psS[:, 0:QT],
                                            op0=OP.mult, op1=OP.add).then_inc(h, 1)
                    if s["glast"]:
                        g = s["g"]
                        kg, hh = groups[g]
                        pa = [self.psA, self.psB][g % 2]
                        ve.wait_ge(S_.get("pv"), B["pv"] + i + 1)
                        if qi > 0 and kg == 0:
                            ve.wait_ge(S_.get("tl_d"), B["tl_d"])
                        h, v = S_.inc("ao_f", 1)
                        for qt in range(NQT):
                            if kg == 0:
                                ii = ve.tensor_copy(aoac[:, qt, hh * HD:(hh + 1) * HD],
                                                    pa[:, qt * 128:(qt + 1) * 128])
                            else:
                                ii = ve.tensor_tensor(aoac[:, qt, hh * HD:(hh + 1) * HD],
                                                      aoac[:, qt, hh * HD:(hh + 1) * HD],
                                                      pa[:, qt * 128:(qt + 1) * 128], op=OP.add)
                        ii.then_inc(h, 1)
                ve.wait_ge(S_.get("pv"), B["pv"] + ns)
                ve.reciprocal(rsb[:, 0:ntl], self.psD[:, 0:ntl])
                h, v = S_.inc("rc_d", 1)
                ve.drain().then_inc(h, 1)
                Ttlv = [self.psT[0][:, :], self.psT[1][:, :],
                        self.psC.bitcast(BF16)[:, 0:128], self.psE.bitcast(BF16)[:, 0:128],
                        self.psD.bitcast(BF16)[:, 0:128], self.psF.bitcast(BF16)[:, 0:128]]

                def copy_j(jj):
                    qt2 = jj % NQT
                    ve.wait_ge(S_.get("tr2"), B["tr2"] + jj + 1)
                    if jj >= 2 * NQT:
                        ve.wait_ge(S_.get("st_ao"), B["st_ao"] + 16 * (jj // NQT - 1))
                    h2, v2 = S_.inc("tg_r", 1)
                    ve.tensor_copy(aostg[:, (jj // NQT) % 2, qt2 * 128:(qt2 + 1) * 128],
                                   Ttlv[jj % 6]).then_inc(h2, 1)
                for j in range(ntl):
                    hh, qt = j // NQT, j % NQT
                    h, v = S_.inc("as_r", 1)
                    ve.tensor_scalar(aosc[:, j % 6, :], aoac[:, qt, hh * HD:(hh + 1) * HD],
                                     rsb[:, j:j + 1].opt(), None, op0=OP.mult).then_inc(h, 1)
                    if j >= 5:
                        copy_j(j - 5)
                for jj in range(max(0, ntl - 5), ntl):
                    copy_j(jj)
                h, v = S_.inc("tl_d", 1)
                ve.nop().then_inc(h, 1)
            @bl.scalar
            def _(sc):
                for i, s in enumerate(steps):
                    sc.wait_ge(S_.get("scf"), B["scf"] + i + 1)
                    if i >= 4:
                        sc.wait_ge(S_.get("pv"), B["pv"] + i - 3)
                    h, v = S_.inc("expd", 1)
                    sc.activation(Psb[:, i % 4, :], Sb[:, i % 4, :], AF.Exp).then_inc(h, 1)
                for hh in range(NH):
                    sc.wait_ge(S_.get("tg_r"), B["tg_r"] + (hh + 1) * NQT)
                    h, v = S_.inc("st_ao", 16)
                    sc.dma_start(bass.AP(dd["aof"].tensor,
                                         nb * D * S + (hh * HD) * S + qi * QT,
                                         [[S, 128], [1, QT]]),
                                 aostg[:, hh % 2, :]).then_inc(h, 16)
                sc.wait_ge(S_.get("st_ao"), S_.val["st_ao"])
            @bl.gpsimd
            def _(g2):
                pass
    st.close()


Builder._att_inner = _att_inner


def _att_wo(self, nb):
    import contextlib
    nc, c, dd = self.nc, self.cfg, self.dram
    D, S, NT = c["D"], c["S"], c["NT"]
    DC = self.DC
    S_ = self.sems
    st = contextlib.ExitStack()
    wo_sb = st.enter_context(nc.sbuf_tensor("wosb%d" % nb, [128, DC, D], BF16))
    ao_sl = st.enter_context(nc.sbuf_tensor("aosl%d" % nb, [128, 2, DC, 128], BF16))
    osb = st.enter_context(nc.sbuf_tensor("osb%d" % nb, [128, 4, NT], BF16))
    NJC = D // NT
    B = {n: S_.val.get(n, 0) for n in ("wo_l", "aos_l0", "aos_l1", "wo_mm", "wo_ef",
                                       "wo_st0", "wo_st1", "wo_st2", "wo_st3")}
    with nc.Block() as bl:
        @bl.gpsimd
        def _(g):
            for dc in range(DC):
                h, v = S_.inc("wo_l", 16)
                g.dma_start(wo_sb[:, dc, :], dd["w_qkvo"][3, dc * 128:(dc + 1) * 128, :]).then_inc(h, 16)
        @bl.sync
        def _(sy):
            for qt in range(S // 128):
                b = qt % 2
                if qt >= 2:
                    sy.wait_ge(S_.get("wo_mm"), B["wo_mm"] + (qt - 1) * NJC)
                h, v = S_.inc("aos_l%d" % b, 16)
                sy.dma_start(ao_sl[:, b, :, :],
                             bass.AP(dd["aof"].tensor, nb * D * S + qt * 128,
                                     [[S, 128], [128 * S, DC], [1, 128]])).then_inc(h, 16)
        @bl.tensor
        def _(te):
            te.wait_ge(S_.get("wo_l"), B["wo_l"] + 16 * DC)
            k = 0
            for qt in range(S // 128):
                b = qt % 2
                te.wait_ge(S_.get("aos_l%d" % b), B["aos_l%d" % b] + 16 * (qt // 2 + 1))
                for jc in range(NJC):
                    pb = k % 2
                    ps = [self.psA, self.psB][pb]
                    if k >= 2:
                        te.wait_ge(S_.get("wo_ef"), B["wo_ef"] + k - 1)
                    h, v = S_.inc("wo_mm", 1)
                    for dc in range(DC):
                        mmi = te.matmul(ps[:, 0:NT], ao_sl[:, b, dc, :],
                                        wo_sb[:, dc, jc * NT:(jc + 1) * NT],
                                        start=(dc == 0), stop=(dc == DC - 1))
                    mmi.then_inc(h, 1)
                    k += 1
        @bl.vector
        def _(ve):
            k = 0
            for qt in range(S // 128):
                for jc in range(NJC):
                    pb = k % 2
                    ps = [self.psA, self.psB][pb]
                    eb = k % 4
                    ve.wait_ge(S_.get("wo_mm"), B["wo_mm"] + k + 1)
                    if k >= 4:
                        ve.wait_ge(S_.get("wo_st%d" % eb), B["wo_st%d" % eb] + 16 * ((k - 4) // 4 + 1))
                    h, v = S_.inc("wo_ef", 1)
                    ve.tensor_tensor(osb[:, eb, :], ps[:, 0:NT],
                                     self.borep[:, jc * NT:(jc + 1) * NT], op=OP.add).then_inc(h, 1)
                    k += 1
        @bl.scalar
        def _(sc):
            k = 0
            for qt in range(S // 128):
                for jc in range(NJC):
                    eb = k % 4
                    sc.wait_ge(S_.get("wo_ef"), B["wo_ef"] + k + 1)
                    h, v = S_.inc("wo_st%d" % eb, 16)
                    sc.dma_start(dd["oattn"][nb * S + qt * 128: nb * S + (qt + 1) * 128,
                                             jc * NT:(jc + 1) * NT],
                                 osb[:, eb, :]).then_inc(h, 16)
                    k += 1
            for q_ in range(4):
                sc.wait_ge(S_.get("wo_st%d" % q_), S_.val.get("wo_st%d" % q_, 0))
    st.close()


Builder._att_wo = _att_wo


def _ph_prezero(self):
    nc, c, dd = self.nc, self.cfg, self.dram
    D = c["D"]
    S_ = self.sems
    T = self.TOWN
    with nc.Block() as bl:
        @bl.sync
        def _(sy):
            for i in range((T + self.CPADMAX) // 128):
                h, v = S_.inc("su_d", 16)
                sy.dma_start(dd["moe"][i * 128:(i + 1) * 128, :],
                             self.zero128[:, 0:D]).then_inc(h, 16)
            R = self.ETROWS
            h, v = S_.inc("su_d", 16)
            sy.dma_start(dd["etab"].rearrange("(p a) b -> p a b", p=128),
                         self.zero128[:, 0:(R // 128) * 2].rearrange("p (a b) -> p a b", b=2)).then_inc(h, 16)
            sy.wait_ge(S_.get("su_d"), S_.val["su_d"])


Builder.ph_prezero = _ph_prezero


def _ph_shared(self):
    import contextlib
    nc, c, dd = self.nc, self.cfg, self.dram
    D, S, NB, ISH, NT = c["D"], c["S"], c["NB"], c["ISH"], c["NT"]
    DC, ISC = self.DC, self.ISC
    S_ = self.sems
    NTG = S // NT

    for nb in range(NB):
        for which, wten, dsth in (("g", dd["sw_g"], dd["gsh"]), ("u", dd["sw_u"], dd["msh"])):
            tag = "sh_"
            bn = "s%d%s" % (nb, which)
            st = contextlib.ExitStack()
            w_sb = st.enter_context(nc.sbuf_tensor("swsb" + bn, [128, DC, ISH], BF16))
            xnsl = st.enter_context(nc.sbuf_tensor("xnsl" + bn, [128, 2, DC, NT], BF16))
            ev = st.enter_context(nc.sbuf_tensor("swev" + bn, [128, 4, NT], BF16))
            gsl = st.enter_context(nc.sbuf_tensor("gsl" + bn, [128, 2, NT], BF16)) if which == "u" else None
            B = {n: S_.val.get(n, 0) for n in
                 (tag + "wl", tag + "x0", tag + "x1", tag + "mm", tag + "ef",
                  tag + "st0", tag + "st1", tag + "st2", tag + "st3",
                  tag + "gl0", tag + "gl1")}
            with nc.Block() as bl:
                @bl.gpsimd
                def _(g):
                    for dc in range(DC):
                        h, v = S_.inc(tag + "wl", 16)
                        g.dma_start(w_sb[:, dc, :], wten[dc * 128:(dc + 1) * 128, :]).then_inc(h, 16)
                @bl.sync
                def _(sy):
                    def load_tile(tg):
                        b = tg % 2
                        if tg >= 2:
                            sy.wait_ge(S_.get(tag + "mm"), B[tag + "mm"] + (tg - 1) * ISC)
                        h, v = S_.inc(tag + "x%d" % b, 16)
                        sy.dma_start(xnsl[:, b, :, :],
                                     bass.AP(dd["xnfm"].tensor, nb * D * S + tg * NT,
                                             [[S, 128], [128 * S, DC], [1, NT]])).then_inc(h, 16)
                    if which == "u":
                        for tg in range(NTG):
                            load_tile(tg)
                            for ic in range(ISC):
                                kk = tg * ISC + ic
                                if kk >= 2:
                                    sy.wait_ge(S_.get(tag + "ef"), B[tag + "ef"] + kk - 1)
                                h, v = S_.inc(tag + "gl%d" % (kk % 2), 16)
                                sy.dma_start(gsl[:, kk % 2, :],
                                             bass.AP(dd["gsh"].tensor, nb * ISH * S + (ic * 128) * S + tg * NT,
                                                     [[S, 128], [1, NT]])).then_inc(h, 16)
                    else:
                        load_tile(0)
                        load_tile(1)
                        for kk in range(NTG * ISC):
                            tg2, ic = kk // ISC, kk % ISC
                            eb = kk % 4
                            sy.wait_ge(S_.get(tag + "ef"), B[tag + "ef"] + kk + 1)
                            h, v = S_.inc(tag + "st%d" % eb, 16)
                            sy.dma_start(bass.AP(dsth.tensor, nb * ISH * S + (ic * 128) * S + tg2 * NT,
                                                 [[S, 128], [1, NT]]),
                                         ev[:, eb, :]).then_inc(h, 16)
                            if (kk + 1) % ISC == 0:
                                tgn = (kk + 1) // ISC + 1
                                if 2 <= tgn < NTG:
                                    load_tile(tgn)
                        for q_ in range(4):
                            sy.wait_ge(S_.get(tag + "st%d" % q_), S_.val.get(tag + "st%d" % q_, 0))
                @bl.tensor
                def _(te):
                    te.wait_ge(S_.get(tag + "wl"), B[tag + "wl"] + 16 * DC)
                    k = 0
                    for tg in range(NTG):
                        b = tg % 2
                        te.wait_ge(S_.get(tag + "x%d" % b), B[tag + "x%d" % b] + 16 * (tg // 2 + 1))
                        for ic in range(ISC):
                            pb = k % 2
                            ps = [self.psA, self.psB][pb]
                            if k >= 2:
                                te.wait_ge(S_.get(tag + "ef"), B[tag + "ef"] + k - 1)
                            h, v = S_.inc(tag + "mm", 1)
                            for dc in range(DC):
                                mmi = te.matmul(ps[:, 0:NT], w_sb[:, dc, ic * 128:(ic + 1) * 128],
                                                xnsl[:, b, dc, :], start=(dc == 0), stop=(dc == DC - 1))
                            mmi.then_inc(h, 1)
                            k += 1
                if which == "g":
                    @bl.scalar
                    def _(sc):
                        k = 0
                        for tg in range(NTG):
                            for ic in range(ISC):
                                pb = k % 2
                                ps = [self.psA, self.psB][pb]
                                eb = k % 4
                                sc.wait_ge(S_.get(tag + "mm"), B[tag + "mm"] + k + 1)
                                if k >= 4:
                                    sc.wait_ge(S_.get(tag + "st%d" % eb), B[tag + "st%d" % eb] + 16 * ((k - 4) // 4 + 1))
                                h, v = S_.inc(tag + "ef", 1)
                                sc.activation(ev[:, eb, :], ps[:, 0:NT], AF.Silu).then_inc(h, 1)
                                k += 1
                    @bl.vector
                    def _(ve):
                        pass
                else:
                    @bl.vector
                    def _(ve):
                        k = 0
                        for tg in range(NTG):
                            for ic in range(ISC):
                                pb = k % 2
                                ps = [self.psA, self.psB][pb]
                                eb = k % 4
                                ve.wait_ge(S_.get(tag + "mm"), B[tag + "mm"] + k + 1)
                                ve.wait_ge(S_.get(tag + "gl%d" % (k % 2)), B[tag + "gl%d" % (k % 2)] + 16 * (k // 2 + 1))
                                if k >= 4:
                                    ve.wait_ge(S_.get(tag + "st%d" % eb), B[tag + "st%d" % eb] + 16 * ((k - 4) // 4 + 1))
                                h, v = S_.inc(tag + "ef", 1)
                                ve.tensor_tensor(ev[:, eb, :], ps[:, 0:NT], gsl[:, k % 2, :], op=OP.mult).then_inc(h, 1)
                                k += 1
                    @bl.scalar
                    def _(sc):
                        k = 0
                        for tg in range(NTG):
                            for ic in range(ISC):
                                eb = k % 4
                                sc.wait_ge(S_.get(tag + "ef"), B[tag + "ef"] + k + 1)
                                h, v = S_.inc(tag + "st%d" % eb, 16)
                                sc.dma_start(bass.AP(dsth.tensor, nb * ISH * S + (ic * 128) * S + tg * NT,
                                                     [[S, 128], [1, NT]]),
                                             ev[:, eb, :]).then_inc(h, 16)
                                k += 1
                        for q_ in range(4):
                            sc.wait_ge(S_.get(tag + "st%d" % q_), S_.val.get(tag + "st%d" % q_, 0))
            st.close()

    st = contextlib.ExitStack()
    swd = st.enter_context(nc.sbuf_tensor("swd", [128, ISC, D], BF16))
    msl = st.enter_context(nc.sbuf_tensor("msl", [128, 2, ISC, 128], BF16))
    xsl = st.enter_context(nc.sbuf_tensor("xsl", [128, 2, D], F32))
    osl = st.enter_context(nc.sbuf_tensor("osl", [128, 2, D], BF16))
    mosl = st.enter_context(nc.sbuf_tensor("mosl", [128, 2, D], F32))
    acc = st.enter_context(nc.sbuf_tensor("acc", [128, 2, NT], F32))
    oev = st.enter_context(nc.sbuf_tensor("oev", [128, 4, NT], F32))
    NJC = D // NT
    B = {n: S_.val.get(n, 0) for n in
         ("y_wl", "y_m0", "y_m1", "y_x0", "y_x1", "y_mm", "y_ef", "y_ac",
          "out_d0", "out_d1", "out_d2", "out_d3")}
    with nc.Block() as bl:
        @bl.gpsimd
        def _(g):
            for ic in range(ISC):
                h, v = S_.inc("y_wl", 16)
                g.dma_start(swd[:, ic, :], dd["sw_d"][ic * 128:(ic + 1) * 128, :]).then_inc(h, 16)
        @bl.sync
        def _(sy):
            for qt in range(self.TOWN // 128):
                b = qt % 2
                nb = qt // (S // 128)
                qtl = qt % (S // 128)
                if qt >= 2:
                    sy.wait_ge(S_.get("y_mm"), B["y_mm"] + (qt - 1) * NJC)
                    sy.wait_ge(S_.get("y_ac"), B["y_ac"] + (qt - 1) * NJC)
                h, v = S_.inc("y_m%d" % b, 16)
                sy.dma_start(msl[:, b, :, :],
                             bass.AP(dd["msh"].tensor, nb * ISH * S + qtl * 128,
                                     [[S, 128], [128 * S, ISC], [1, 128]])).then_inc(h, 16)
                h, v = S_.inc("y_x%d" % b, 48)
                sy.dma_start(xsl[:, b, :], dd["x_own"][qt * 128:(qt + 1) * 128, :]).then_inc(h, 16)
                sy.dma_start(osl[:, b, :], dd["oattn"][qt * 128:(qt + 1) * 128, :]).then_inc(h, 16)
                sy.dma_start(mosl[:, b, :], dd["moe"][qt * 128:(qt + 1) * 128, :]).then_inc(h, 16)
        @bl.tensor
        def _(te):
            te.wait_ge(S_.get("y_wl"), B["y_wl"] + 16 * ISC)
            k = 0
            for qt in range(self.TOWN // 128):
                b = qt % 2
                te.wait_ge(S_.get("y_m%d" % b), B["y_m%d" % b] + 16 * (qt // 2 + 1))
                for jc in range(NJC):
                    pb = k % 2
                    ps = [self.psA, self.psB][pb]
                    if k >= 2:
                        te.wait_ge(S_.get("y_ef"), B["y_ef"] + k - 1)
                    h, v = S_.inc("y_mm", 1)
                    for ic in range(ISC):
                        mmi = te.matmul(ps[:, 0:NT], msl[:, b, ic, :],
                                        swd[:, ic, jc * NT:(jc + 1) * NT],
                                        start=(ic == 0), stop=(ic == ISC - 1))
                    mmi.then_inc(h, 1)
                    k += 1
        @bl.vector
        def _(ve):
            k = 0
            for qt in range(self.TOWN // 128):
                b = qt % 2
                ve.wait_ge(S_.get("y_x%d" % b), B["y_x%d" % b] + 48 * (qt // 2 + 1))
                for jc in range(NJC):
                    pb = k % 2
                    ps = [self.psA, self.psB][pb]
                    ve.wait_ge(S_.get("y_mm"), B["y_mm"] + k + 1)
                    h, v = S_.inc("y_ef", 1)
                    ve.tensor_tensor(acc[:, pb, :], ps[:, 0:NT],
                                     xsl[:, b, jc * NT:(jc + 1) * NT], op=OP.add).then_inc(h, 1)
                    ve.drain()
                    ve.tensor_tensor(acc[:, pb, :], acc[:, pb, :],
                                     osl[:, b, jc * NT:(jc + 1) * NT], op=OP.add)
                    ve.drain()
                    eb = k % 4
                    if k >= 4:
                        ve.wait_ge(S_.get("out_d%d" % eb), B["out_d%d" % eb] + 16 * ((k - 4) // 4 + 1))
                    h, v = S_.inc("y_ac", 1)
                    ve.tensor_tensor(oev[:, eb, :], acc[:, pb, :],
                                     mosl[:, b, jc * NT:(jc + 1) * NT], op=OP.add).then_inc(h, 1)
                    k += 1
        @bl.scalar
        def _(sc):
            k = 0
            for qt in range(self.TOWN // 128):
                for jc in range(NJC):
                    eb = k % 4
                    sc.wait_ge(S_.get("y_ac"), B["y_ac"] + k + 1)
                    h, v = S_.inc("out_d%d" % eb, 16)
                    sc.dma_start(dd["out"][qt * 128:(qt + 1) * 128, jc * NT:(jc + 1) * NT],
                                 oev[:, eb, :]).then_inc(h, 16)
                    k += 1
            for q_ in range(4):
                sc.wait_ge(S_.get("out_d%d" % q_), S_.val.get("out_d%d" % q_, 0))
    st.close()


Builder.ph_shared = _ph_shared


class BranchCnt:

    def __init__(self, sems):
        self.S = sems
        self.val = {}

    def inc(self, name, by):
        h = self.S.get(name)
        self.val[name] = self.val.get(name, 0) + by
        return h, self.val[name]

    def get(self, name):
        return self.S.get(name)

    def v(self, name):
        return self.val.get(name, 0)


def _hot_block(self, hc, bodies):
    nc = self.nc
    with nc.Block() as bl:
        for engname, dec in (("sy", bl.sync), ("sc", bl.scalar), ("ve", bl.vector),
                             ("te", bl.tensor), ("gp", bl.gpsimd)):
            body = bodies.get(engname)
            def wrap(eng, body=body, engname=engname):
                if body is None:
                    return
                with eng.If_eq(self.cid_regs[engname], hc):
                    body(eng)
            dec(wrap)


Builder.hot_block = _hot_block


def _ph_route(self):
    import contextlib
    nc, c, dd = self.nc, self.cfg, self.dram
    D, E, K, CAP, S = c["D"], c["E"], c["TOPK"], c["CAP"], c["S"]
    DC = self.DC
    EST = self.ESTMAX
    S_ = self.sems
    JTM = max((min(h["t0"] + h.get("hlen", self.TOWN), c["TH"]) // 128 for h in c["HOT"].values()), default=1)
    JAM = JTM * K
    DELTA = 2.0 ** -17

    st = contextlib.ExitStack()
    sb = lambda n, sh, dt=F32: st.enter_context(nc.sbuf_tensor(n, sh, dt))
    xsp = sb("xsp", [128, 2, D], BF16)
    xfmt = sb("xfmt", [128, 2, DC, 128], BF16)
    gwsb = sb("gwsb", [16, D], F32)
    gtmp = sb("gtmp", [16, D], F32)
    ghat = sb("ghat", [16, 3, D], BF16)
    gfm = sb("gfm", [128, 3, DC, E], BF16)
    lgsp = sb("lgsp", [128, JTM, E], F32)
    lg2 = sb("lg2", [128, JTM, E], F32)
    eqk = sb("eqk", [128, K, JTM, E], F32)
    idxa = sb("idxa", [128, JTM, K], F32)
    mrow = sb("mrow", [128, JTM], F32)
    ohs = sb("ohs", [128, E, JAM], F32)
    scn = sb("scn", [128, E, JAM], F32)
    csb = sb("csb", [128, E], F32)
    cbf = sb("cbf", [128, E], BF16)
    basesb = sb("basesb", [128, E], F32)
    basem = sb("basem", [128, E], F32)
    bndrow = sb("bndrow", [1, E], F32)
    rg = sb("rg", [128, JAM], F32)
    rl = sb("rl", [128, JAM], F32)
    tmpa = sb("tmpa", [128, JAM], F32)
    tmpb = sb("tmpb", [128, JAM], F32)
    tokl = sb("tokl", [128, JTM], I32)
    toklf = sb("toklf", [128, JTM], F32)
    ownm = sb("ownm", [128, JTM], F32)
    wk4 = sb("wk4", [128, K, JTM], F32)
    wden = sb("wden", [128, JTM], F32)
    pairs = sb("pairs", [128, JAM, 2], F32)
    gtg = sb("gtg", [128, JAM], F32)
    gtgi = sb("gtgi", [128, JAM], I32)
    losl = sb("losl", [128, 2, DC, 128], BF16)
    lown_sb = sb("lown_sb", [128, JTM, E], F32)
    wtmp = sb("wtmp", [128, JTM * E], F32)
    bnd2 = sb("bnd2", [1, E], F32)

    for hc, hcfg in sorted(c["HOT"].items()):
        t0 = hcfg["t0"]
        cpad = hcfg["cpad"]
        hlen = hcfg.get("hlen", self.TOWN)
        TR = min(t0 + hlen, c["TH"])
        JT = TR // 128
        JA = JT * K
        p_b, j_b = t0 // JT, t0 % JT
        NLO = (TR - t0) // 128
        BC = BranchCnt(S_)
        JE = JT * E

        b0 = {n: BC.v(n) for n in ("r_d", "r_v", "r_t", "r_tf")}

        def s1_sy(sy):
            h, v = BC.inc("r_d", 16)
            sy.dma_start(gwsb[:], dd["gate_w"][:]).then_inc(h, 16)
            sy.wait_ge(h, v)

        def s1_ve(ve):
            ve.wait_ge(BC.get("r_d"), b0["r_d"] + 16)
            ve.tensor_copy(ghat[:, 2, :], gwsb[:])
            ve.drain()
            ve.tensor_reduce(gtmp[:, 0:1], gwsb[:], axis=AX.X, op=OP.add)
            ve.drain()
            ve.tensor_scalar(gtmp[:, 0:1], gtmp[:, 0:1], 1.0 / D, None, op0=OP.mult)
            ve.drain()
            ve.tensor_scalar(gwsb[:], gwsb[:], gtmp[:, 0:1].opt(), None, op0=OP.subtract)
            ve.drain()
            ve.tensor_copy(ghat[:, 0, :], gwsb[:])
            ve.drain()
            ve.tensor_copy(gtmp[:], ghat[:, 0, :])
            ve.drain()
            ve.tensor_tensor(gwsb[:], gwsb[:], gtmp[:], op=OP.subtract)
            ve.drain()
            ve.tensor_copy(ghat[:, 1, :], gwsb[:])
            h, v = BC.inc("r_v", 1)
            ve.drain().then_inc(h, 1)

        def s1_te(te):
            te.wait_ge(BC.get("r_v"), b0["r_v"] + 1)
            k = 0
            for which in range(3):
                for dc in range(DC):
                    t = k % 2
                    if k >= 2:
                        te.wait_ge(BC.get("r_tf"), b0["r_tf"] + k - 1)
                    h, v = BC.inc("r_t", 1)
                    te.transpose(self.psT[t][:, 0:16], ghat[0:16, which, dc * 128:(dc + 1) * 128],
                                 self.ident[0:16, 0:16]).then_inc(h, 1)
                    k += 1

        def s1_ve2(ve):
            k = 0
            for which in range(3):
                for dc in range(DC):
                    t = k % 2
                    ve.wait_ge(BC.get("r_t"), b0["r_t"] + k + 1)
                    h, v = BC.inc("r_tf", 1)
                    ve.tensor_copy(gfm[:, which, dc, :], self.psT[t][:, 0:16]).then_inc(h, 1)
                    k += 1
        self.hot_block(hc, dict(sy=s1_sy, ve=s1_ve))
        self.hot_block(hc, dict(te=s1_te, ve=s1_ve2))

        b2 = {n: BC.v(n) for n in ("r_gd0", "r_gd1", "r_t", "r_tf", "r_mm", "r_lge")}

        def s2_gp(g):
            for jj in range(JT):
                bslot = jj % 2
                if jj >= 2:
                    g.wait_ge(BC.get("r_tf"), b2["r_tf"] + (jj - 1) * DC)
                if t0 == 0:
                    h, v = BC.inc("r_gd%d" % bslot, 16)
                    g.dma_start(xsp[:, bslot, :],
                                bass.AP(dd["x_own"].tensor, jj * D, [[JT * D, 128], [1, D]])).then_inc(h, 16)
                else:
                    p_cut = ceil_div(t0 - jj, JT)
                    h, v = BC.inc("r_gd%d" % bslot, 16)
                    g.dma_start(xsp[0:p_cut, bslot, :],
                                bass.AP(dd["x_pre"].tensor, jj * D, [[JT * D, p_cut], [1, D]])).then_inc(h, 16)
                    h, v = BC.inc("r_gd%d" % bslot, 16)
                    g.dma_start(xsp[p_cut:128, bslot, :],
                                bass.AP(dd["x_own"].tensor, (p_cut * JT + jj - t0) * D,
                                        [[JT * D, 128 - p_cut], [1, D]])).then_inc(h, 16)

        nldma = 1 if t0 == 0 else 2

        def s2_te(te):
            for jj in range(JT):
                bslot = jj % 2
                te.wait_ge(BC.get("r_gd%d" % bslot), b2["r_gd%d" % bslot] + 16 * nldma * (jj // 2 + 1))
                for dc in range(DC):
                    k = jj * DC + dc
                    t = k % 2
                    if k >= 2:
                        te.wait_ge(BC.get("r_tf"), b2["r_tf"] + k - 1)
                    h, v = BC.inc("r_t", 1)
                    te.transpose(self.psT[t][:, :], xsp[:, bslot, dc * 128:(dc + 1) * 128],
                                 self.ident[:]).then_inc(h, 1)
                te.wait_ge(BC.get("r_tf"), b2["r_tf"] + (jj + 1) * DC)
                if jj >= 1:
                    te.wait_ge(BC.get("r_lge"), b2["r_lge"] + jj)
                h, v = BC.inc("r_mm", 1)
                for dc in range(DC):
                    for hl in range(2):
                        mmi = te.matmul(self.psF[:, 0:E], xfmt[:, jj % 2, dc, :],
                                        gfm[:, hl, dc, :],
                                        start=(dc == 0 and hl == 0),
                                        stop=(dc == DC - 1 and hl == 1))
                mmi.then_inc(h, 1)

        def s2_ve(ve):
            for jj in range(JT):
                if jj >= 2:
                    ve.wait_ge(BC.get("r_mm"), b2["r_mm"] + jj - 1)
                for dc in range(DC):
                    k = jj * DC + dc
                    t = k % 2
                    ve.wait_ge(BC.get("r_t"), b2["r_t"] + k + 1)
                    h, v = BC.inc("r_tf", 1)
                    ve.tensor_copy(xfmt[:, jj % 2, dc, :], self.psT[t][:, :]).then_inc(h, 1)
                ve.wait_ge(BC.get("r_mm"), b2["r_mm"] + jj + 1)
                h, v = BC.inc("r_lge", 1)
                ve.scalar_tensor_tensor(lgsp[:, jj, :], self.iota16[:, :], -DELTA,
                                        self.psF[:, 0:E], op0=OP.mult, op1=OP.add).then_inc(h, 1)
        self.hot_block(hc, dict(gp=s2_gp, te=s2_te, ve=s2_ve))

        def s3_ve(ve):
            ve.wait_ge(BC.get("r_lge"), b2["r_lge"] + JT)
            for k in range(K):
                ve.tensor_reduce(mrow[:, 0:JT], lgsp[:, 0:JT, :], axis=AX.X, op=OP.max)
                ve.drain()
                ve.tensor_tensor(eqk[:, k, 0:JT, :], lgsp[:, 0:JT, :],
                                 mrow[:, 0:JT].unsqueeze(2).broadcast_to([128, JT, E]),
                                 op=OP.is_equal)
                ve.drain()
                ve.tensor_tensor(lg2[:, 0:JT, :], eqk[:, k, 0:JT, :],
                                 self.iota16[:, :].unsqueeze(1).broadcast_to([128, JT, E]),
                                 op=OP.mult)
                ve.drain()
                ve.tensor_reduce(idxa[:, 0:JT, k], lg2[:, 0:JT, :], axis=AX.X, op=OP.add)
                ve.drain()
                if k < K - 1:
                    ve.tensor_scalar(lg2[:, 0:JT, :], eqk[:, k, 0:JT, :], 1e9, None, op0=OP.mult)
                    ve.drain()
                    ve.tensor_tensor(lgsp[:, 0:JT, :], lgsp[:, 0:JT, :], lg2[:, 0:JT, :],
                                     op=OP.subtract)
                    ve.drain()
            h, v = BC.inc("r_v", 1)
            ve.nop().then_inc(h, 1)
        self.hot_block(hc, dict(ve=s3_ve))

        b4 = {n: BC.v(n) for n in ("r_v", "r_mm", "r_lge", "r_g", "r_bd")}
        bb4 = b4["r_bd"]

        def s4_gp(g):
            g.wait_ge(BC.get("r_v"), b4["r_v"])
            g.iota(tokl[:, 0:JT], [[1, JT]], base=-t0, channel_multiplier=JT)
            g.drain()
            h, v = BC.inc("r_g", 1)
            g.tensor_copy(toklf[:, 0:JT], tokl[:, 0:JT]).then_inc(h, 1)

        def s4_ve(ve):
            ve.wait_ge(BC.get("r_v"), b4["r_v"])
            av = idxa[:, 0:JT, :].rearrange("p j k -> p (j k)")
            for e in range(E):
                ve.tensor_scalar(ohs[:, e, 0:JA], av, float(e), None, op0=OP.is_equal)
            ve.drain()
            ve.memset(tmpa[:, 0:JA], 0.0)
            ve.drain()
            for e in range(E):
                ve.tensor_tensor_scan(scn[:, e, 0:JA], ohs[:, e, 0:JA], tmpa[:, 0:JA],
                                      0.0, op0=OP.add, op1=OP.add)
                ve.drain()
                ve.tensor_copy(csb[:, e:e + 1], scn[:, e, JA - 1:JA])
            ve.drain()
            ve.tensor_copy(cbf[:, :], csb[:, :])
            h, v = BC.inc("r_v", 1)
            ve.drain().then_inc(h, 1)
            ve.wait_ge(BC.get("r_mm"), b4["r_mm"] + 1)
            ve.tensor_copy(basesb[:, :], self.psF[:, 0:E])
            h, v = BC.inc("r_v", 1)
            ve.drain().then_inc(h, 1)
            if t0 > 0:
                ve.wait_ge(BC.get("r_bd"), bb4 + 32)
                ve.tensor_tensor(bndrow[0:1, :], bndrow[0:1, :], bnd2[0:1, :], op=OP.add)
                ve.drain()
            h, v = BC.inc("r_v", 1)
            ve.nop().then_inc(h, 1)
            if t0 > 0:
                ve.wait_ge(BC.get("r_mm"), b4["r_mm"] + 2)
                ve.tensor_tensor(basem[:, :], basesb[:, :], self.psF[:, 0:E], op=OP.subtract)
                ve.drain()
            else:
                ve.tensor_copy(basem[:, :], basesb[:, :])
                ve.drain()
            ve.memset(rg[:, 0:JA], 0.0)
            ve.memset(rl[:, 0:JA], 0.0)
            ve.drain()
            for e in range(E):
                ve.scalar_tensor_tensor(tmpb[:, 0:JA], scn[:, e, 0:JA], basesb[:, e:e + 1].opt(),
                                        ohs[:, e, 0:JA], op0=OP.add, op1=OP.mult)
                ve.drain()
                ve.tensor_tensor(rg[:, 0:JA], rg[:, 0:JA], tmpb[:, 0:JA], op=OP.add)
                ve.drain()
                ve.scalar_tensor_tensor(tmpb[:, 0:JA], scn[:, e, 0:JA], basem[:, e:e + 1].opt(),
                                        ohs[:, e, 0:JA], op0=OP.add, op1=OP.mult)
                ve.drain()
                ve.tensor_tensor(rl[:, 0:JA], rl[:, 0:JA], tmpb[:, 0:JA], op=OP.add)
                ve.drain()
            ve.tensor_scalar(rg[:, 0:JA], rg[:, 0:JA], -1.0, None, op0=OP.add)
            ve.tensor_scalar(rl[:, 0:JA], rl[:, 0:JA], -1.0, None, op0=OP.add)
            h, v = BC.inc("r_v", 1)
            ve.drain().then_inc(h, 1)

        def s4_sy(sy):
            if t0 > 0:
                sy.wait_ge(BC.get("r_v"), b4["r_v"] + 2)
                h, v = BC.inc("r_bd", 16)
                sy.dma_start(bndrow[0:1, :], basesb[p_b:p_b + 1, :]).then_inc(h, 16)
                h, v = BC.inc("r_bd", 16)
                if j_b > 0:
                    sy.dma_start(bnd2[0:1, :],
                                 scn[p_b:p_b + 1, :, j_b * K - 1:j_b * K].rearrange("p e a -> p (e a)"),
                                 allow_slow_non_contiguous=True).then_inc(h, 16)
                else:
                    sy.dma_start(bnd2[0:1, :], self.zero128[0:1, 0:E]).then_inc(h, 16)

        def s4_te(te):
            te.wait_ge(BC.get("r_v"), b4["r_v"] + 1)
            h, v = BC.inc("r_mm", 1)
            te.matmul(self.psF[:, 0:E], self.ltri[:, :], cbf[:, :],
                      start=True, stop=True).then_inc(h, 1)
            if t0 > 0:
                te.wait_ge(BC.get("r_v"), b4["r_v"] + 3)
                h, v = BC.inc("r_mm", 1)
                te.matmul(self.psF[:, 0:E], self.ones1[:, :], bndrow[0:1, :],
                          start=True, stop=True).then_inc(h, 1)
        self.hot_block(hc, dict(gp=s4_gp, ve=s4_ve, te=s4_te, sy=s4_sy))

        b5 = {n: BC.v(n) for n in ("r_l0", "r_l1", "r_mm", "r_lge", "r_v", "r_st")}

        def s5_sy(sy):
            for ti in range(NLO):
                bslot = ti % 2
                u = ti * 128
                nb_, col = u // S, u % S
                if ti >= 2:
                    sy.wait_ge(BC.get("r_mm"), b5["r_mm"] + ti - 1)
                h, v = BC.inc("r_l%d" % bslot, 16)
                sy.dma_start(losl[:, bslot, :, :],
                             bass.AP(dd["xnfm"].tensor, nb_ * D * S + col,
                                     [[S, 128], [128 * S, DC], [1, 128]])).then_inc(h, 16)

        def s5_te(te):
            for ti in range(NLO):
                bslot = ti % 2
                te.wait_ge(BC.get("r_l%d" % bslot), b5["r_l%d" % bslot] + 16 * (ti // 2 + 1))
                if ti >= 1:
                    te.wait_ge(BC.get("r_lge"), b5["r_lge"] + ti)
                h, v = BC.inc("r_mm", 1)
                for dc in range(DC):
                    mmi = te.matmul(self.psF[:, 0:E], losl[:, bslot, dc, :], gfm[:, 2, dc, :],
                                    start=(dc == 0), stop=(dc == DC - 1))
                mmi.then_inc(h, 1)

        def s5_ve(ve):
            for ti in range(NLO):
                ve.wait_ge(BC.get("r_mm"), b5["r_mm"] + ti + 1)
                h, v = BC.inc("r_lge", 1)
                ve.tensor_copy(lown_sb[:, ti, :], self.psF[:, 0:E]).then_inc(h, 1)

        def s5_sc(sc):
            sc.wait_ge(BC.get("r_lge"), b5["r_lge"] + NLO)
            h, v = BC.inc("r_st", 16)
            sc.dma_start(bass.AP(dd["lown"].tensor, 0, [[E, 128], [128 * E, NLO], [1, E]]),
                         lown_sb[:, 0:NLO, :]).then_inc(h, 16)
            sc.wait_ge(h, v)
        self.hot_block(hc, dict(sy=s5_sy, te=s5_te, ve=s5_ve, sc=s5_sc))

        b5r = {n: BC.v(n) for n in ("r_d", "r_v", "r_st")}

        def s5r_ve(ve):
            ve.memset(lg2[:, 0:JT, :], 0.0)
            h, v = BC.inc("r_v", 1)
            ve.drain().then_inc(h, 1)

        def s5r_sy(sy):
            sy.wait_ge(BC.get("r_v"), b5r["r_v"] + 1)
            sy.wait_ge(BC.get("r_st"), b5r["r_st"])
            if t0 == 0:
                h, v = BC.inc("r_d", 16)
                sy.dma_start(lg2[:, 0:JT, :],
                             bass.AP(dd["lown"].tensor, 0,
                                     [[JT * E, 128], [E, JT], [1, E]])).then_inc(h, 16)
                sy.wait_ge(h, v)
            else:
                h, v = BC.inc("r_d", 16)
                sy.dma_start(lg2[p_b:p_b + 1, j_b:JT, :],
                             bass.AP(dd["lown"].tensor, 0, [[0, 1], [E, JT - j_b], [1, E]])).then_inc(h, 16)
                h, v = BC.inc("r_d", 16)
                sy.dma_start(lg2[p_b + 1:128, 0:JT, :],
                             bass.AP(dd["lown"].tensor, (JT - j_b) * E,
                                     [[JT * E, 127 - p_b], [E, JT], [1, E]])).then_inc(h, 16)
                sy.wait_ge(h, v)
        self.hot_block(hc, dict(ve=s5r_ve, sy=s5r_sy))

        b6 = {n: BC.v(n) for n in ("r_d", "r_v", "r_a")}

        def s6_ve(ve):
            ve.wait_ge(BC.get("r_d"), b6["r_d"])
            ve.tensor_reduce(mrow[:, 0:JT], lg2[:, 0:JT, :], axis=AX.X, op=OP.max)
            ve.drain()
            ve.tensor_tensor(lg2[:, 0:JT, :], lg2[:, 0:JT, :],
                             mrow[:, 0:JT].unsqueeze(2).broadcast_to([128, JT, E]),
                             op=OP.subtract)
            h, v = BC.inc("r_v", 1)
            ve.drain().then_inc(h, 1)
            ve.wait_ge(BC.get("r_a"), b6["r_a"] + 1)
            for k in range(K):
                ve.tensor_tensor(wtmp[:, 0:JE].rearrange("p (j e) -> p j e", e=E),
                                 eqk[:, k, 0:JT, :],
                                 lg2[:, 0:JT, :], op=OP.mult)
                ve.drain()
                ve.tensor_reduce(wk4[:, k, 0:JT],
                                 wtmp[:, 0:JE].rearrange("p (j e) -> p j e", e=E),
                                 axis=AX.X, op=OP.add)
                ve.drain()
            ve.tensor_tensor(wden[:, 0:JT], wk4[:, 0, 0:JT], wk4[:, 1, 0:JT], op=OP.add)
            ve.drain()
            for k in range(2, K):
                ve.tensor_tensor(wden[:, 0:JT], wden[:, 0:JT], wk4[:, k, 0:JT], op=OP.add)
                ve.drain()
            ve.reciprocal(wden[:, 0:JT], wden[:, 0:JT])
            ve.drain()
            for k in range(K):
                ve.tensor_tensor(wk4[:, k, 0:JT], wk4[:, k, 0:JT], wden[:, 0:JT], op=OP.mult)
            h, v = BC.inc("r_v", 1)
            ve.drain().then_inc(h, 1)

        def s6_sc(sc):
            sc.wait_ge(BC.get("r_v"), b6["r_v"] + 1)
            h, v = BC.inc("r_a", 1)
            sc.activation(lg2[:, 0:JT, :], lg2[:, 0:JT, :], AF.Exp).then_inc(h, 1)
        self.hot_block(hc, dict(ve=s6_ve, sc=s6_sc))

        b7 = {n: BC.v(n) for n in ("r_v", "r_g", "r_sc")}

        def s7_ve(ve):
            ve.wait_ge(BC.get("r_v"), b7["r_v"])
            ve.wait_ge(BC.get("r_g"), b7["r_g"])
            ve.tensor_scalar(ownm[:, 0:JT], tokl[:, 0:JT], 0.0, None, op0=OP.is_ge)
            ve.drain()
            ve.tensor_scalar(mrow[:, 0:JT], tokl[:, 0:JT], float(TR - t0), None, op0=OP.is_lt)
            ve.drain()
            ve.tensor_tensor(ownm[:, 0:JT], ownm[:, 0:JT], mrow[:, 0:JT], op=OP.mult)
            ve.drain()
            ve.tensor_scalar(tmpa[:, 0:JA], rg[:, 0:JA], float(CAP), None, op0=OP.is_lt)
            ve.drain()
            ve.tensor_tensor(tmpa[:, 0:JA].rearrange("p (j k) -> p j k", k=K),
                             tmpa[:, 0:JA].rearrange("p (j k) -> p j k", k=K),
                             ownm[:, 0:JT].unsqueeze(2).broadcast_to([128, JT, K]),
                             op=OP.mult)
            ve.drain()
            ve.tensor_scalar(tmpb[:, 0:JA], rl[:, 0:JA], 0.0, float(cpad), op0=OP.max, op1=OP.min)
            ve.drain()
            ve.scalar_tensor_tensor(tmpb[:, 0:JA], tmpb[:, 0:JA], -float(cpad),
                                    tmpa[:, 0:JA], op0=OP.add, op1=OP.mult)
            ve.drain()
            ve.tensor_scalar(tmpb[:, 0:JA], tmpb[:, 0:JA], float(cpad), None, op0=OP.add)
            ve.drain()
            ve.scalar_tensor_tensor(gtg[:, 0:JA],
                                    idxa[:, 0:JT, :].rearrange("p j k -> p (j k)"),
                                    float(EST), tmpb[:, 0:JA], op0=OP.mult, op1=OP.add)
            ve.drain()
            ve.tensor_copy(gtgi[:, 0:JA], gtg[:, 0:JA])
            ve.drain()
            for k in range(K):
                ve.tensor_copy(pairs[:, 0:JA, 0].rearrange("p (j k) -> p j k", k=K)[:, :, k],
                               wk4[:, k, 0:JT])
            ve.tensor_copy(pairs[:, 0:JA, 1].rearrange("p (j k) -> p j k", k=K),
                           toklf[:, 0:JT].unsqueeze(2).broadcast_to([128, JT, K]))
            h, v = BC.inc("r_v", 1)
            ve.drain().then_inc(h, 1)

        def s7_gp(g):
            g.wait_ge(BC.get("r_v"), b7["r_v"] + 1)
            for col in range(JA):
                h, v = BC.inc("r_sc", 16)
                g.indirect_dma_start(dd["etab"][:, :],
                                     IndirectOffsetOnAxis(gtgi[:, col:col + 1], 0),
                                     pairs[:, col, :], None).then_inc(h, 16)
                g.wait_ge(h, v)
        self.hot_block(hc, dict(ve=s7_ve, gp=s7_gp))
    st.close()


Builder.ph_route = _ph_route


def _ph_moe(self):
    import contextlib
    nc, c, dd = self.nc, self.cfg, self.dram
    D, E, I, NT, S = c["D"], c["E"], c["I"], c["NT"], c["S"]
    DC, IC = self.DC, self.IC
    EST = self.ESTMAX
    ESTC = EST // 128
    R128 = self.ETROWS // 128
    S_ = self.sems
    T = self.TOWN
    CPM = self.CPADMAX

    st = contextlib.ExitStack()
    sb = lambda n, sh, dt=F32: st.enter_context(nc.sbuf_tensor(n, sh, dt))
    etw = sb("etw", [128, R128, 2], F32)
    toki = sb("toki", [128, R128], I32)
    tsci = sb("tsci", [128, R128], I32)
    sef = sb("sef", [128, R128], F32)
    mmask = sb("mmask", [128, R128], F32)
    ttmp = sb("ttmp", [128, R128], F32)
    wun = sb("wun", [128, 2, DC * I], BF16)
    xgr = sb("xgr", [128, 2, D], BF16)
    xgfm = sb("xgfm", [128, DC, CPM], BF16)
    gsil = sb("gsil", [128, IC, CPM], BF16)
    msb = sb("msb", [128, IC, CPM], BF16)
    eout = sb("eout", [128, 2, D], F32)

    for hc, hcfg in sorted(c["HOT"].items()):
        cpad = hcfg["cpad"]
        NST = cpad // 128
        NTC = ceil_div(cpad, NT)
        BC = BranchCnt(S_)
        b0 = {n: BC.v(n) for n in ("m_et", "m_v", "m_g")}

        def et_sy(sy):
            h, v = BC.inc("m_et", 16)
            sy.dma_start(etw[:, :, :],
                         dd["etab"].rearrange("(a p) b -> p a b", p=128)).then_inc(h, 16)
            sy.wait_ge(h, v)

        def et_gp(g):
            h, v = BC.inc("m_g", 1)
            g.iota(ttmp[:, 0:R128].bitcast(I32), [[0, E], [128, ESTC]], base=0,
                   channel_multiplier=1)
            g.drain().then_inc(h, 1)

        def et_ve(ve):
            ve.wait_ge(BC.get("m_et"), b0["m_et"] + 16)
            ve.wait_ge(BC.get("m_g"), b0["m_g"] + 1)
            ve.tensor_copy(sef[:, 0:R128], ttmp[:, 0:R128].bitcast(I32))
            ve.drain()
            ve.tensor_copy(toki[:, :], etw[:, :, 1])
            ve.drain()
            ve.tensor_scalar(mmask[:, :], etw[:, :, 0], 0.0, None, op0=OP.is_gt)
            ve.drain()
            ve.tensor_scalar(sef[:, :], sef[:, :], float(T), None, op0=OP.add)
            ve.drain()
            ve.tensor_tensor(ttmp[:, :], etw[:, :, 1], sef[:, :], op=OP.subtract)
            ve.drain()
            ve.tensor_tensor(ttmp[:, :], ttmp[:, :], mmask[:, :], op=OP.mult)
            ve.drain()
            ve.tensor_tensor(ttmp[:, :], ttmp[:, :], sef[:, :], op=OP.add)
            ve.drain()
            ve.tensor_copy(tsci[:, :], ttmp[:, :])
            h, v = BC.inc("m_v", 1)
            ve.drain().then_inc(h, 1)
        self.hot_block(hc, dict(sy=et_sy, gp=et_gp, ve=et_ve))

        bigb = {n: BC.v(n) for n in
                ("m_wl0", "m_wl1", "m_gd0", "m_gd1", "m_tr", "m_tf", "m_mm",
                 "m_ef", "m_sl", "m_sd")}
        NJC = D // NT
        mm_pe = IC * NTC * 2 + NST * NJC
        tr_pe = NST * DC
        ef_pe = IC * NTC + NST * NJC
        sl_pe = IC * NTC
        sd_pe = NST

        def big_gp(g):
            for e in range(E):
                mm0 = bigb["m_mm"] + e * mm_pe
                ge, ue = e % 2, 1 - e % 2
                if e > 0:
                    g.wait_ge(BC.get("m_mm"), mm0 - NST * NJC)
                for rr in range(DC):
                    h, v = BC.inc("m_wl%d" % ge, 16)
                    g.dma_start(wun[:, ge, :].rearrange("p (a b) -> p a b", a=DC)[:, rr, :],
                                dd["ew_g"][e, rr * 128:(rr + 1) * 128, :]).then_inc(h, 16)
                if e == 0:
                    g.wait_ge(BC.get("m_v"), b0["m_v"] + 1)
                for stile in range(NST):
                    kgl = e * NST + stile
                    slot = kgl % 2
                    if kgl >= 2:
                        g.wait_ge(BC.get("m_tr"), bigb["m_tr"] + (kgl - 1) * DC)
                    h, v = BC.inc("m_gd%d" % slot, 16)
                    g.indirect_dma_start(xgr[:, slot, :], None, dd["xn"][:, :],
                                         IndirectOffsetOnAxis(toki[:, e * ESTC + stile:e * ESTC + stile + 1], 0)
                                         ).then_inc(h, 16)
                if e > 0:
                    g.wait_ge(BC.get("m_mm"), mm0)
                for rr in range(DC):
                    h, v = BC.inc("m_wl%d" % ue, 16)
                    g.dma_start(wun[:, ue, :].rearrange("p (a b) -> p a b", a=DC)[:, rr, :],
                                dd["ew_u"][e, rr * 128:(rr + 1) * 128, :]).then_inc(h, 16)
                g.wait_ge(BC.get("m_mm"), mm0 + IC * NTC)
                for rr in range(IC):
                    h, v = BC.inc("m_wl%d" % ge, 16)
                    g.dma_start(wun[:, ge, :].rearrange("p (a b) -> p a b", a=IC)[:, rr, :],
                                dd["ew_d"][e, rr * 128:(rr + 1) * 128, :]).then_inc(h, 16)
                ef0 = bigb["m_ef"] + e * ef_pe
                for stile in range(NST):
                    slot = (e * NST + stile) % 2
                    if stile == 0 and e > 0:
                        g.wait_ge(BC.get("m_sd"), bigb["m_sd"] + 16 * NST * e)
                    g.wait_ge(BC.get("m_ef"), ef0 + IC * NTC + (stile + 1) * NJC)
                    h, v = BC.inc("m_sd", 16)
                    g.indirect_dma_start(dd["moe"][:, :],
                                         IndirectOffsetOnAxis(tsci[:, e * ESTC + stile:e * ESTC + stile + 1], 0),
                                         eout[:, slot, :], None,
                                         compute_op=OP.add).then_inc(h, 16)
            g.wait_ge(BC.get("m_sd"), BC.v("m_sd"))

        def big_te(te):
            for e in range(E):
                mm0 = bigb["m_mm"] + e * mm_pe
                tr0 = bigb["m_tr"] + e * tr_pe
                ef0 = bigb["m_ef"] + e * ef_pe
                sl0 = bigb["m_sl"] + e * sl_pe
                T6 = [self.psT[0][:, :], self.psT[1][:, :],
                      self.psC.bitcast(BF16)[:, 0:128], self.psE.bitcast(BF16)[:, 0:128],
                      self.psD.bitcast(BF16)[:, 0:128], self.psF.bitcast(BF16)[:, 0:128]]
                for stile in range(NST):
                    kgl = e * NST + stile
                    slot = kgl % 2
                    te.wait_ge(BC.get("m_gd%d" % slot), bigb["m_gd%d" % slot] + 16 * (kgl // 2 + 1))
                    for dc in range(DC):
                        kk = kgl * DC + dc
                        if kk >= 6:
                            te.wait_ge(BC.get("m_tf"), bigb["m_tf"] + kk - 5)
                        h, v = BC.inc("m_tr", 1)
                        te.transpose(T6[kk % 6], xgr[:, slot, dc * 128:(dc + 1) * 128],
                                     self.ident[:]).then_inc(h, 1)
                ge, ue = e % 2, 1 - e % 2
                wl_gate = [bigb["m_wl0"], bigb["m_wl1"]]
                for e2 in range(e + 1):
                    g2, u2 = e2 % 2, 1 - e2 % 2
                    wl_gate[g2] += 16 * DC
                    if e2 < e:
                        wl_gate[u2] += 16 * DC
                        wl_gate[g2] += 16 * IC
                wl_after_gate = wl_gate[ge]
                wl_after_up = wl_gate[ue] + 16 * DC
                wl_after_down = wl_after_gate + 16 * IC
                te.wait_ge(BC.get("m_wl%d" % ge), wl_after_gate)
                te.wait_ge(BC.get("m_tf"), bigb["m_tf"] + (e * NST + NST) * DC)
                k = 0
                for ic in range(IC):
                    for tc in range(NTC):
                        n = min(NT, cpad - tc * NT)
                        ps = [self.psA, self.psB][k % 2]
                        if k >= 2:
                            te.wait_ge(BC.get("m_sl"), sl0 + k - 1)
                        elif e > 0:
                            te.wait_ge(BC.get("m_ef"), ef0)
                        h, v = BC.inc("m_mm", 1)
                        for dc in range(DC):
                            mmi = te.matmul(ps[:, 0:n],
                                            wun[:, ge, :].rearrange("p (a b) -> p a b", a=DC)[:, dc, ic * 128:(ic + 1) * 128],
                                            xgfm[:, dc, tc * NT:tc * NT + n],
                                            start=(dc == 0), stop=(dc == DC - 1))
                        mmi.then_inc(h, 1)
                        k += 1
                te.wait_ge(BC.get("m_wl%d" % ue), wl_after_up)
                te.wait_ge(BC.get("m_sl"), sl0 + IC * NTC)
                k2 = 0
                for ic in range(IC):
                    for tc in range(NTC):
                        n = min(NT, cpad - tc * NT)
                        ps = [self.psA, self.psB][(k + k2) % 2]
                        if k2 >= 2:
                            te.wait_ge(BC.get("m_ef"), ef0 + k2 - 1)
                        h, v = BC.inc("m_mm", 1)
                        for dc in range(DC):
                            mmi = te.matmul(ps[:, 0:n],
                                            wun[:, ue, :].rearrange("p (a b) -> p a b", a=DC)[:, dc, ic * 128:(ic + 1) * 128],
                                            xgfm[:, dc, tc * NT:tc * NT + n],
                                            start=(dc == 0), stop=(dc == DC - 1))
                        mmi.then_inc(h, 1)
                        k2 += 1
                te.wait_ge(BC.get("m_wl%d" % ge), wl_after_down)
                te.wait_ge(BC.get("m_ef"), ef0 + IC * NTC)
                k3 = 0
                for stile in range(NST):
                    for jc in range(NJC):
                        ps = [self.psA, self.psB][(k + k2 + k3) % 2]
                        if k3 >= 2:
                            te.wait_ge(BC.get("m_ef"), ef0 + IC * NTC + k3 - 1)
                        h, v = BC.inc("m_mm", 1)
                        for ic in range(IC):
                            mmi = te.matmul(ps[:, 0:NT],
                                            msb[:, ic, stile * 128:(stile + 1) * 128],
                                            wun[:, ge, :].rearrange("p (a b) -> p a b", a=IC)[:, ic, jc * NT:(jc + 1) * NT],
                                            start=(ic == 0), stop=(ic == IC - 1))
                        mmi.then_inc(h, 1)
                        k3 += 1

        def big_sc(sc):
            for e in range(E):
                mm0 = bigb["m_mm"] + e * mm_pe
                k = 0
                for ic in range(IC):
                    for tc in range(NTC):
                        n = min(NT, cpad - tc * NT)
                        ps = [self.psA, self.psB][k % 2]
                        sc.wait_ge(BC.get("m_mm"), mm0 + k + 1)
                        h, v = BC.inc("m_sl", 1)
                        sc.activation(gsil[:, ic, tc * NT:tc * NT + n], ps[:, 0:n],
                                      AF.Silu).then_inc(h, 1)
                        k += 1

        def big_ve(ve):
            for e in range(E):
                mm0 = bigb["m_mm"] + e * mm_pe
                tr0 = bigb["m_tr"] + e * tr_pe
                sl0 = bigb["m_sl"] + e * sl_pe
                sd0 = bigb["m_sd"] + e * sd_pe * 16
                T6 = [self.psT[0][:, :], self.psT[1][:, :],
                      self.psC.bitcast(BF16)[:, 0:128], self.psE.bitcast(BF16)[:, 0:128],
                      self.psD.bitcast(BF16)[:, 0:128], self.psF.bitcast(BF16)[:, 0:128]]
                for stile in range(NST):
                    for dc in range(DC):
                        kgl = e * NST + stile
                        kk = kgl * DC + dc
                        ve.wait_ge(BC.get("m_tr"), bigb["m_tr"] + kk + 1)
                        if e > 0 and stile == 0 and dc == 0:
                            ve.wait_ge(BC.get("m_mm"), mm0 - NST * NJC)
                        h, v = BC.inc("m_tf", 1)
                        ve.tensor_copy(xgfm[:, dc, stile * 128:(stile + 1) * 128],
                                       T6[kk % 6]).then_inc(h, 1)
                j = 0
                for ic in range(IC):
                    for tc in range(NTC):
                        n = min(NT, cpad - tc * NT)
                        ps = [self.psA, self.psB][(IC * NTC + j) % 2]
                        ve.wait_ge(BC.get("m_mm"), mm0 + IC * NTC + j + 1)
                        ve.wait_ge(BC.get("m_sl"), sl0 + j + 1)
                        h, v = BC.inc("m_ef", 1)
                        ve.tensor_tensor(msb[:, ic, tc * NT:tc * NT + n], ps[:, 0:n],
                                         gsil[:, ic, tc * NT:tc * NT + n], op=OP.mult).then_inc(h, 1)
                        j += 1
                j3 = 0
                for stile in range(NST):
                    slot = (e * NST + stile) % 2
                    if e * NST + stile >= 2:
                        ve.wait_ge(BC.get("m_sd"), bigb["m_sd"] + 16 * (e * NST + stile - 1))
                    for jc in range(NJC):
                        ps = [self.psA, self.psB][(2 * IC * NTC + j3) % 2]
                        ve.wait_ge(BC.get("m_mm"), mm0 + 2 * IC * NTC + j3 + 1)
                        h, v = BC.inc("m_ef", 1)
                        ve.tensor_scalar(eout[:, slot, jc * NT:(jc + 1) * NT], ps[:, 0:NT],
                                         etw[:, e * ESTC + stile, 0:1].opt(), None,
                                         op0=OP.mult).then_inc(h, 1)
                        j3 += 1
        self.hot_block(hc, dict(gp=big_gp, te=big_te, sc=big_sc, ve=big_ve))
    st.close()


Builder.ph_moe = _ph_moe



_PROGRAM_CACHE = {}


def kernel(L0_attn_x, L0_attn_wq, L0_attn_bq, L0_attn_wk, L0_attn_bk,
           L0_attn_wv, L0_attn_bv, L0_attn_wo, L0_attn_bo,
           L0_ffn_gate_w, L0_ffn_w_gate, L0_ffn_w_up, L0_ffn_w_down,
           L0_ffn_sw_gate, L0_ffn_sw_up, L0_ffn_sw_down):
    cfg = full_cfg()
    NC = cfg["NCORES"]
    NB = cfg["NB"]
    B, S, D = cfg["B"], cfg["S"], cfg["D"]
    key = "full"
    if key not in _PROGRAM_CACHE:
        _PROGRAM_CACHE[key] = build_program(cfg)
    nc, b = _PROGRAM_CACHE[key]
    T = b.TOWN

    asf = lambda a: np.ascontiguousarray(np.asarray(a, dtype=np.float32))
    x = asf(L0_attn_x).reshape(B * S, D)
    w_qkvo = np.stack([asf(L0_attn_wq), asf(L0_attn_wk), asf(L0_attn_wv), asf(L0_attn_wo)])
    b_qkvo = np.stack([asf(L0_attn_bq), asf(L0_attn_bk), asf(L0_attn_bv), asf(L0_attn_bo)])
    consts = host_constants_inmap(cfg)
    xpre = np.ascontiguousarray(x[0:b.TPRE])
    common = dict(
        w_qkvo=w_qkvo, b_qkvo=b_qkvo, gate_w=asf(L0_ffn_gate_w),
        sw_g=asf(L0_ffn_sw_gate), sw_u=asf(L0_ffn_sw_up), sw_d=asf(L0_ffn_sw_down),
        ew_g=asf(L0_ffn_w_gate), ew_u=asf(L0_ffn_w_up), ew_d=asf(L0_ffn_w_down),
        x_pre=xpre, **consts,
    )
    xb = x.reshape(B, S, D)
    in_maps = []
    for c_ in range(NC):
        im = dict(common)
        im["cid"] = np.array([[c_]], np.int32)
        im["x_own"] = np.ascontiguousarray(
            np.concatenate([xb[c_], xb[c_ + NC]], 0))
        in_maps.append(im)
    res = run_bass_kernel_spmd(nc, in_maps, core_ids=list(range(NC)))
    out = np.empty((B, S, D), np.float32)
    for c_ in range(NC):
        o = res.results[c_]["out"]
        out[c_] = o[0:S]
        out[c_ + NC] = o[S:2 * S]
    return out
